# version 21
# baseline (speedup 1.0000x reference)
import numpy as np
import ml_dtypes
import concourse.bass as bass
import concourse.mybir as mybir
from concourse.bass_utils import run_bass_kernel_spmd
from contextlib import ExitStack

BFD = ml_dtypes.bfloat16
F32 = mybir.dt.float32
BF16 = mybir.dt.bfloat16
F16 = mybir.dt.float16
I32 = mybir.dt.int32
Alu = mybir.AluOpType
Act = mybir.ActivationFunctionType
AX = mybir.AxisListType.X

S, T, D, NE, NI, P = 4096, 2048, 1024, 3072, 4096, 128
KDBG = False
Q2ENG = "vector"
LN_EPS = 1e-5
INV32 = 1.0 / 32.0

ENGINES = ("sync", "tensor", "vector", "scalar", "gpsimd")
SERIAL = ("vector", "scalar", "gpsimd")

SEMS = (["tM", "vV", "sC", "gP", "sP0", "gC",
         "qM", "qMB", "qP1X", "qLN1T", "qGX1", "qGQ1", "qGX2", "qTR0", "qTR1"] +
        [f"{p}{i}" for p in ("qFCT", "qWN", "qWT", "qQT", "qXT", "qX1", "qQ1T", "qWG",
                             "qWU", "qQ2T", "qWD", "qA", "qAS", "qGLN", "qGQs",
                             "qGN", "qGO", "qTS") for i in (0, 1)])


def mk(meth, *args, **kw):
    def f(E):
        return getattr(E, meth)(*args, **kw)
    return f


class Builder:
    def __init__(self, sem_names):
        self.ops = {e: [] for e in ENGINES}
        self.cnt = {s: 0 for s in sem_names}
        self.sems = {}
        self.serial = {e: None for e in SERIAL}

    def op(self, eng, fn, inc=None, waits=(), n=1, is_dma=False):
        if inc is not None:
            self.cnt[inc] += n
        val = self.cnt[inc] if inc is not None else None
        waits = list(waits)
        if eng in SERIAL and not is_dma:
            m = self.serial.get(eng)
            if m:
                waits.append(m)
            if inc is not None:
                self.serial[eng] = (inc, val)
        waits = [(s, v) for (s, v) in waits if v is not None and v > 0]
        sems = self.sems
        inc_pair = (inc, n) if inc is not None else None

        def thunk(E):
            for s, v in waits:
                E.wait_ge(sems[s], v)
            ins = fn(E)
            if inc_pair is not None:
                ins.then_inc(sems[inc_pair[0]], inc_pair[1])
        self.ops[eng].append(thunk)
        return val

    def splice(self, eng, thunk, sem, val, pre_cell):
        if eng in SERIAL:
            pre_cell[0] = self.serial.get(eng)
            self.serial[eng] = (sem, val)
        self.ops[eng].append(thunk)

    def dma(self, eng, q, fn, waits=()):
        return self.op(eng, fn, inc=q, waits=waits, n=16, is_dma=True)

    def wait(self, eng, sem, val):
        if val is None or val <= 0:
            return
        sems = self.sems

        def thunk(E):
            E.wait_ge(sems[sem], val)
        self.ops[eng].append(thunk)


def build_program(nc):
    b = Builder(SEMS)

    x_hbm = nc.dram_tensor("x_full", [S, D], F32, kind="ExternalInput").ap()
    wqkv_hbm = nc.dram_tensor("w_qkv", [NE, D], F32, kind="ExternalInput").ap()
    wfc_hbm = nc.dram_tensor("w_fc", [D, D], F32, kind="ExternalInput").ap()
    bfc_hbm = nc.dram_tensor("b_fc", [1, D], F32, kind="ExternalInput").ap()
    gam_hbm = nc.dram_tensor("gamma", [1, D], F32, kind="ExternalInput").ap()
    bet_hbm = nc.dram_tensor("beta", [1, D], F32, kind="ExternalInput").ap()
    w3_hbm = [nc.dram_tensor("w_gate", [NI, D], F32, kind="ExternalInput").ap(),
              nc.dram_tensor("w_up", [NI, D], F32, kind="ExternalInput").ap(),
              nc.dram_tensor("w_down", [D, NI], F32, kind="ExternalInput").ap()]
    out_hbm = nc.dram_tensor("out", [T, D], F32, kind="ExternalOutput").ap()

    s_ln1 = nc.dram_tensor("s_ln1", [S, D], F16, kind="Internal").ap()
    s_q = nc.dram_tensor("s_q", [D, T], F16, kind="Internal").ap()
    s_wqkv = nc.dram_tensor("s_wqkv", [NE, D], F16, kind="Internal").ap()
    s_wv_lo = nc.dram_tensor("s_wv_lo", [D, D], F16, kind="Internal").ap()
    s_wfc = nc.dram_tensor("s_wfc", [D, D], F16, kind="Internal").ap()
    s_wfc_lo = nc.dram_tensor("s_wfc_lo", [D, D], F16, kind="Internal").ap()
    s_w3 = [nc.dram_tensor("s_wg2", [NI, D], BF16, kind="Internal").ap(),
            nc.dram_tensor("s_wu2", [NI, D], BF16, kind="Internal").ap(),
            nc.dram_tensor("s_wd2", [D, NI], F16, kind="Internal").ap()]
    s_wg2t = nc.dram_tensor("s_wg2t", [D, NI], BF16, kind="Internal").ap()
    s_wu2t = nc.dram_tensor("s_wu2t", [D, NI], BF16, kind="Internal").ap()
    s_wd2t = nc.dram_tensor("s_wd2t", [NI, D], F16, kind="Internal").ap()
    s_x1 = nc.dram_tensor("s_x1", [T, D], F32, kind="Internal").ap()
    s_xq1 = nc.dram_tensor("s_xq1", [T, D], BF16, kind="Internal").ap()
    s_xq2 = nc.dram_tensor("s_xq2", [T, NI], F16, kind="Internal").ap()
    s_dbg = nc.dram_tensor("s_dbg", [P, 5120], F32, kind="Internal").ap() if KDBG else None
    s_dby = nc.dram_tensor("s_dby", [T, NI], F16, kind="Internal").ap() if KDBG else None

    stack = ExitStack()
    sb = lambda name, shape, dt: stack.enter_context(nc.sbuf_tensor(name, shape, dt))
    psa = lambda name: stack.enter_context(nc.psum_tensor(name, [P, 512], F32))

    AR1 = sb("AR1", [P, 32768], BF16)
    AR2 = sb("AR2", [P, 32768], BF16)
    AR3 = sb("AR3", [P, 9216], BF16)
    AR4 = sb("AR4", [P, 4096], F32)

    ident = sb("ident", [P, P], F16)
    iota_a = sb("iota_a", [P, P], F32)
    iota_b = sb("iota_b", [P, P], F32)
    ones_rb = sb("ones_rb", [1, P], F16)
    ones_rf = sb("ones_rf", [1, P], F32)
    ones_cf = sb("ones_cf", [P, 1], F32)
    bfc_hi = sb("bfc_hi", [1, D], F16)
    bfc_lo = sb("bfc_lo", [1, D], F16)
    colfc_b = sb("colfc_b", [1, D], F16)
    colfc_lo = sb("colfc_lo", [1, D], F16)
    colsum_c = sb("colsum_c", [P, 8], F32)
    lnsum_c = sb("lnsum_c", [P, 8], F32)
    lncol_hi = sb("lncol_hi", [P, 8], F16)
    lncol_lo = sb("lncol_lo", [P, 8], F16)
    colsum_cb = sb("colsum_cb", [P, 8], F16)
    colsum_lo = sb("colsum_lo", [P, 8], F16)
    mu1 = sb("mu1", [P, 16], F32)
    r1c = sb("r1c", [P, 16], F32)
    mur2 = sb("mur2", [P, 32], F32)
    musq = sb("musq", [P, 16], F32)
    sqs = sb("sqs", [P, 16], F32)
    a1g = sb("a1g", [P, 16], F32)
    a1u = sb("a1u", [P, 16], F32)
    a2d = sb("a2d", [P, 16], F32)
    s1c = sb("s1c", [P, 16], F32)
    s2c = sb("s2c", [P, 16], F32)
    am2 = sb("am2", [P, 8], F32)
    svec = sb("svec", [P, 32], F32)
    wacc = sb("wacc", [P, 4], F32)
    tcols = sb("tcols", [P, 12], F32)
    rowt = sb("rowt", [1, 16], F32)
    rs_row = sb("rs_row", [1, P], F16)
    rs_col = sb("rs_col", [P, 1], F16)
    rs_row2 = sb("rs_row2", [1, P], F16)
    rs_col2 = sb("rs_col2", [P, 1], F16)
    rs_f = sb("rs_f", [P, 1], F32)
    xstage = sb("xstage", [P, 2048], F32)
    wstg = sb("wstg", [P, 2048], F32)
    p0scr = sb("p0scr", [P, 4096], BF16)
    p0f = sb("p0f", [P, 512], F32)
    qi = sb("qi", [P, 1024], I32)
    fcB = sb("fcB", [P, 1024], F32)

    pb = [psa(f"pb{i}") for i in range(8)]
    bank_gate = [None] * 8

    def bw(i):
        return [bank_gate[i]] if bank_gate[i] else []

    gam_b = AR1[:, 30720:31744].bitcast(F16)
    bet_b = AR1[:, 31744:32768].bitcast(F16)
    gam_lo = AR1[:, 28672:29696].bitcast(F16)
    bet_lo = AR1[:, 29696:30720].bitcast(F16)
    brow = xstage[0:1, 0:1024]

    b.op("gpsimd", mk("iota", iota_a[:, :], [[1, P]], channel_multiplier=0,
                      allow_small_or_imprecise_dtypes=True), inc="gP")
    e = b.op("gpsimd", mk("iota", iota_b[:, :], [[1, P]], channel_multiplier=1,
                          allow_small_or_imprecise_dtypes=True), inc="gP")
    b.op("vector", mk("tensor_sub", iota_b[:, :], iota_b[:, :], iota_a[:, :]),
         inc="vV", waits=[("gP", e)])
    b.op("vector", mk("tensor_tensor", ident[:, :], iota_b[:, :], iota_a[:, :], Alu.is_equal), inc="vV")
    b.op("vector", mk("memset", ones_rb[:, :], 1.0), inc="vV")
    b.op("vector", mk("memset", ones_rf[:, :], 1.0), inc="vV")
    b.op("vector", mk("memset", ones_cf[:, :], 1.0), inc="vV")
    b.op("vector", mk("memset", colsum_c[:, :], 0.0), inc="vV")
    b.op("vector", mk("memset", wacc[:, :], 0.0), inc="vV")
    e_init_v = b.cnt["vV"]

    e_bv = b.dma("sync", "qM", mk("dma_start", out=brow[:, :], in_=bfc_hbm[:, :]))
    b.op("vector", mk("tensor_copy", bfc_hi[:, :], brow[:, :]), inc="vV", waits=[("qM", e_bv)])
    e_bfc = b.op("vector", mk("scalar_tensor_tensor", bfc_lo[:, :], brow[:, :], 1.0, bfc_hi[:, :],
                              Alu.mult, Alu.subtract), inc="vV")

    p0_scrap = p0f[:, :]
    sgA = p0scr[:, 0:1024]
    sgB = p0scr[:, 1024:2048]
    qout = [p0scr[:, 2048:3072], p0scr[:, 3072:4096]]

    def w3_tile(m, j):
        if m < 2:
            return w3_hbm[m][j * 128:(j + 1) * 128, :]
        r, qc = divmod(j, 4)
        return w3_hbm[2][r * 128:(r + 1) * 128, qc * 1024:(qc + 1) * 1024]

    def s3_tile(m, j):
        if m < 2:
            return s_w3[m][j * 128:(j + 1) * 128, :]
        r, qc = divmod(j, 4)
        return s_w3[2][r * 128:(r + 1) * 128, qc * 1024:(qc + 1) * 1024]

    p0b_abs = []
    p0b_tiles_done = [0]

    def p0b_tiles_chunk(n):
        while p0b_tiles_done[0] < 96 and n > 0:
            idx = p0b_tiles_done[0]
            m, j = divmod(idx, 32)
            slot = idx % 2
            stg = wstg[:, slot * 1024:(slot + 1) * 1024]
            waits = []
            if idx >= 2:
                waits.append(("vV", p0b_abs[idx - 2]))
            src = w3_tile(m, j)
            ld = None
            for q4 in range(4):
                ld = b.dma("scalar", f"qA{slot}",
                           mk("dma_start", out=stg[:, q4 * 256:(q4 + 1) * 256],
                              in_=src[:, q4 * 256:(q4 + 1) * 256]),
                           waits=waits if q4 == 0 else ())
            aw = [(f"qA{slot}", ld)]
            if idx >= 2:
                aw.append(("gC", 2 * (idx - 2) + 2))
            b.op("vector", mk("scalar_tensor_tensor", p0_scrap, stg[:, 0:512], -1.0,
                              stg[:, 0:512], Alu.mult, Alu.max,
                              accum_out=svec[:, 20 + slot:21 + slot]),
                 inc="vV", waits=aw)
            ab = b.op("vector", mk("scalar_tensor_tensor", p0_scrap, stg[:, 512:1024], -1.0,
                                   stg[:, 512:1024], Alu.mult, Alu.max,
                                   accum_out=svec[:, 22 + slot:23 + slot]), inc="vV")
            p0b_abs.append(ab)
            p0b_tiles_done[0] += 1
            n -= 1

    p0b_emitted = [0]

    def p0b_add_chunk(n):
        while p0b_emitted[0] < len(p0b_abs) and n > 0:
            idx = p0b_emitted[0]
            m = idx // 32
            slot = idx % 2
            aw0 = [("vV", p0b_abs[idx])]
            if idx == 0:
                aw0.append(("vV", e_init_v))
            b.op("gpsimd", mk("tensor_add", wacc[:, m:m + 1], wacc[:, m:m + 1],
                              svec[:, 20 + slot:21 + slot]),
                 inc="gC", waits=aw0)
            b.op("gpsimd", mk("tensor_add", wacc[:, m:m + 1], wacc[:, m:m + 1],
                              svec[:, 22 + slot:23 + slot]), inc="gC")
            p0b_emitted[0] += 1
            n -= 1

    scale_ready = [None, None, None]

    def p0_scale(m):
        em = b.op("tensor", mk("matmul", pb[7][0:1, 0:1], wacc[:, m:m + 1], ones_cf[:, :],
                               start=True, stop=True),
                  inc="tM", waits=[("gC", (m + 1) * 64)] + bw(7))
        ev = b.op("vector", mk("tensor_copy", rowt[0:1, m:m + 1], pb[7][0:1, 0:1]),
                  inc="vV", waits=[("tM", em)])
        bank_gate[7] = ("vV", ev)
        b.op("vector", mk("tensor_scalar", rowt[0:1, 4 + m:5 + m], rowt[0:1, m:m + 1],
                          1.0 / (NI * D), 1e-5, Alu.mult, Alu.max), inc="vV")
        vv_now = b.cnt["vV"]
        em2 = b.op("tensor", mk("matmul", pb[7][:, 0:1], ones_rf[:, :], rowt[0:1, 4 + m:5 + m],
                                start=True, stop=True),
                   inc="tM", waits=[("vV", vv_now)] + bw(7))
        ev2 = b.op("vector", mk("tensor_copy", tcols[:, m:m + 1], pb[7][:, 0:1]),
                   inc="vV", waits=[("tM", em2)])
        bank_gate[7] = ("vV", ev2)
        b.op("vector", mk("tensor_single_scalar", tcols[:, 3 + m:4 + m], tcols[:, m:m + 1],
                          0.5, Alu.mult), inc="vV")
        ev3 = b.op("vector", mk("tensor_single_scalar", tcols[:, 6 + m:7 + m], tcols[:, 3 + m:4 + m],
                                -1.0, Alu.mult), inc="vV")
        scale_ready[m] = ev3

    p0c_state = {"i": 0, "signB": {}, "comb": {}, "st": {0: [], 1: []}}

    def p0c_tiles(n):
        while p0c_state["i"] < 96 and n > 0:
            idx = p0c_state["i"]
            m, j = divmod(idx, 32)
            slot = idx % 2
            stg = wstg[:, slot * 1024:(slot + 1) * 1024]
            lw = [("vV", p0b_abs[95])]
            if idx >= 2:
                lw.append(("vV", p0c_state["signB"][idx - 2]))
            src = w3_tile(m, j)
            ld = None
            for q4 in range(4):
                ld = b.dma("scalar", f"qA{slot}",
                           mk("dma_start", out=stg[:, q4 * 256:(q4 + 1) * 256],
                              in_=src[:, q4 * 256:(q4 + 1) * 256]),
                           waits=lw if q4 == 0 else ())
            wA = [(f"qA{slot}", ld), ("vV", scale_ready[m])]
            qv = qout[slot] if m < 2 else qout[slot].bitcast(F16)
            b.op("vector", mk("tensor_scalar", sgA[:, :], stg, tcols[:, 3 + m:4 + m], 2.0,
                              Alu.is_gt, Alu.mult), inc="vV", waits=wA)
            vB = b.op("vector", mk("tensor_scalar", sgB[:, :], stg, tcols[:, 6 + m:7 + m], 2.0,
                                   Alu.is_ge, Alu.mult), inc="vV")
            p0c_state["signB"][idx] = vB
            wC = []
            if len(p0c_state["st"][slot]) >= 1:
                wC.append((f"qAS{slot}", p0c_state["st"][slot][-1]))
            vC = b.op("vector", mk("scalar_tensor_tensor", qv, sgA[:, :], -2.0, sgB[:, :],
                                   Alu.add, Alu.add), inc="vV", waits=wC)
            p0c_state["comb"][idx] = vC
            st = b.dma("scalar", f"qAS{slot}", mk("dma_start", out=s3_tile(m, j), in_=qv),
                       waits=[("vV", vC)])
            p0c_state["st"][slot].append(st)
            p0c_state["i"] += 1
            n -= 1

    p0b_tiles_chunk(2)
    p0b_add_chunk(2)
    scr1 = AR3[:, 0:1024]
    e_norm_done = {}
    e_lnsum_mm = [None]
    e_lnsum_slot = {0: None, 1: None}
    for g in range(8):
        if g < 4:
            mu_dst = mu1[:, g * 4:(g + 1) * 4]
            r_dst = r1c[:, g * 4:(g + 1) * 4]
        else:
            mu_dst = mur2[:, (g - 4) * 4:(g - 3) * 4]
            r_dst = mur2[:, 16 + (g - 4) * 4:16 + (g - 3) * 4]
        for i in range(4):
            t_idx = g * 4 + i
            xt = AR4[:, i * 1024:(i + 1) * 1024]
            w = [("vV", e_norm_done[t_idx - 4])] if t_idx >= 4 else []
            for q2 in range(2):
                b.dma("gpsimd", "qP1X",
                      mk("dma_start", out=xt[:, q2 * 512:(q2 + 1) * 512],
                         in_=x_hbm[t_idx * 128:(t_idx + 1) * 128, q2 * 512:(q2 + 1) * 512]),
                      waits=w if q2 == 0 else ())
        grp_total = b.cnt["qP1X"]
        for i in range(4):
            xt = AR4[:, i * 1024:(i + 1) * 1024]
            b.op("vector", mk("reduce_sum", musq[:, i:i + 1], xt, AX),
                 inc="vV", waits=[("qP1X", grp_total)] if i == 0 else ())
            b.op("vector", mk("scalar_tensor_tensor", scr1, xt, 1.0, xt,
                              Alu.mult, Alu.mult, accum_out=sqs[:, i:i + 1]), inc="vV")
        b.op("vector", mk("tensor_single_scalar", mu_dst, musq[:, 0:4], 1.0 / D, Alu.mult), inc="vV")
        b.op("vector", mk("tensor_single_scalar", sqs[:, 0:4], sqs[:, 0:4], 1.0 / D, Alu.mult), inc="vV")
        b.op("vector", mk("scalar_tensor_tensor", svec[:, 0:4], mu_dst, 1.0,
                          mu_dst, Alu.mult, Alu.mult), inc="vV")
        b.op("vector", mk("tensor_sub", svec[:, 0:4], sqs[:, 0:4], svec[:, 0:4]), inc="vV")
        b.op("vector", mk("tensor_single_scalar", svec[:, 0:4], svec[:, 0:4], LN_EPS, Alu.add), inc="vV")
        ev_ve = b.op("vector", mk("reciprocal", musq[:, 0:4], svec[:, 0:4]), inc="vV")
        es = b.op("scalar", mk("activation", r_dst, musq[:, 0:4], Act.Sqrt),
                  inc="sC", waits=[("vV", ev_ve)])
        first = True
        for _ in range(2):
            b.op("vector", mk("scalar_tensor_tensor", sqs[:, 0:4], r_dst, 1.0,
                              r_dst, Alu.mult, Alu.mult), inc="vV",
                 waits=[("sC", es)] if first else ())
            first = False
            b.op("vector", mk("tensor_mul", sqs[:, 0:4], sqs[:, 0:4], svec[:, 0:4]), inc="vV")
            b.op("vector", mk("tensor_scalar", sqs[:, 0:4], sqs[:, 0:4], -0.5, 1.5,
                              Alu.mult, Alu.add), inc="vV")
            b.op("vector", mk("tensor_mul", r_dst, r_dst, sqs[:, 0:4]), inc="vV")
        for i in range(4):
            t_idx = g * 4 + i
            xt = AR4[:, i * 1024:(i + 1) * 1024]
            slot = t_idx % 2
            lnf = xstage[:, slot * 1024:(slot + 1) * 1024]
            waits = []
            if e_lnsum_slot[slot] is not None:
                waits.append(("tM", e_lnsum_slot[slot]))
            ev_f = b.op("vector", mk("tensor_scalar", lnf, xt,
                                     mu_dst[:, i:i + 1], r_dst[:, i:i + 1],
                                     Alu.subtract, Alu.mult), inc="vV", waits=waits)
            e_norm_done[t_idx] = ev_f
            em5 = None
            for ds in range(8):
                em5 = b.op("tensor", mk("matmul", pb[5][:, ds:ds + 1],
                                        lnf[:, ds * 128:(ds + 1) * 128], ones_cf[:, :],
                                        start=(t_idx == 0 and ds == 0), stop=False,
                                        skip_group_check=True),
                           inc="tM" if ds == 7 else None,
                           waits=[("vV", ev_f)] + (bw(5) if t_idx == 0 and ds == 0 else []))
            e_lnsum_slot[slot] = em5
            e_lnsum_mm[0] = em5
            p0b_tiles_chunk(3)
            p0b_add_chunk(3)
    p0b_tiles_chunk(96)
    p0b_add_chunk(96)
    ev5 = b.op("vector", mk("tensor_copy", lnsum_c[:, :], pb[5][:, 0:8]),
               inc="vV", waits=[("tM", e_lnsum_mm[0])])
    bank_gate[5] = ("vV", ev5)
    b.op("vector", mk("tensor_copy", lncol_hi[:, :], lnsum_c[:, :]), inc="vV")
    b.op("vector", mk("scalar_tensor_tensor", lncol_lo[:, :], lnsum_c[:, :], 1.0,
                      lncol_hi[:, :], Alu.mult, Alu.subtract), inc="vV")
    e_p1_v = b.cnt["vV"]

    cast_a = AR3[:, 6144:7168].bitcast(F16)
    cast_b = AR3[:, 7168:8192].bitcast(F16)
    e_nat_store = {0: [], 1: []}
    e_cast = []
    tiles = [("qkv", j, wqkv_hbm, s_wqkv) for j in range(16, 24)] + \
            [("fc", j, wfc_hbm, s_wfc) for j in range(8)]
    for idx, (which, j, src, dst) in enumerate(tiles):
        slot = idx % 2
        stg = xstage[:, slot * 1024:(slot + 1) * 1024]
        waits = []
        if idx < 2:
            waits.append(("vV", e_bfc))
            waits.append(("tM", e_lnsum_mm[0]))
        if idx >= 2:
            waits.append(("vV", e_cast[idx - 2]))
        ld = None
        for q4 in range(4):
            ld = b.dma("sync", f"qWN{slot}",
                       mk("dma_start", out=stg[:, q4 * 256:(q4 + 1) * 256],
                          in_=src[j * 128:(j + 1) * 128, q4 * 256:(q4 + 1) * 256]),
                       waits=waits if q4 == 0 else ())
        cst = cast_a if slot == 0 else cast_b
        cw = [(f"qWN{slot}", ld)]
        if idx < 2:
            cw.append(("vV", e_p1_v))
        if idx >= 2:
            cw.append((f"qGN{slot}", e_nat_store[slot][-1]))
        ev = b.op("vector", mk("tensor_copy", cst, stg), inc="vV", waits=cw)
        st = b.dma("gpsimd", f"qGN{slot}", mk("dma_start", out=dst[j * 128:(j + 1) * 128, :], in_=cst),
                   waits=[("vV", ev)])
        if which == "fc" or (which == "qkv" and j >= 16):
            lodst = s_wfc_lo if which == "fc" else s_wv_lo
            jj = j if which == "fc" else j - 16
            ev = b.op("vector", mk("scalar_tensor_tensor", cst, stg, 1.0, cst,
                                   Alu.mult, Alu.subtract), inc="vV",
                      waits=[(f"qGN{slot}", st)])
            st = b.dma("gpsimd", f"qGN{slot}",
                       mk("dma_start", out=lodst[jj * 128:(jj + 1) * 128, :], in_=cst),
                       waits=[("vV", ev)])
        e_cast.append(ev)
        e_nat_store[slot].append(st)
    e_cast_last = e_cast[-1]
    e_qgn_all = [b.cnt["qGN0"], b.cnt["qGN1"]]

    p0_scale(0)
    p0c_tiles(16)
    p0_scale(1)
    p0c_tiles(28)
    p0_scale(2)
    p0c_tiles(28)
    p0c_tiles(24)

    e_qas_all = [b.cnt["qAS0"], b.cnt["qAS1"]]
    e_grps = b.cnt["tM"]
    e_vmm = {}
    emv = None
    for ds in range(8):
        lw = []
        if ds == 0:
            lw += [("qGN0", e_qgn_all[0]), ("qGN1", e_qgn_all[1]), ("vV", e_cast_last),
                   ("vV", e_p1_v)]
        if ds >= 1:
            lw.append(("tM", e_vmm[ds - 1]))
        ldh = b.dma("sync", "qFCT0",
                    mk("dma_start", out=cast_a,
                       in_=s_wqkv[2 * D:3 * D, ds * 128:(ds + 1) * 128], transpose=True),
                    waits=lw)
        ldl = b.dma("sync", "qFCT1",
                    mk("dma_start", out=cast_b,
                       in_=s_wv_lo[:, ds * 128:(ds + 1) * 128], transpose=True),
                    waits=lw if ds == 0 else ())
        mw0 = [("qFCT0", ldh), ("qFCT1", ldl)]
        if ds == 0:
            mw0 += bw(6)
        for sub in range(8):
            first = (ds == 0 and sub == 0)
            b.op("tensor", mk("matmul", pb[6][:, sub:sub + 1],
                              cast_a[:, sub * 128:(sub + 1) * 128], lncol_hi[:, ds:ds + 1],
                              start=first, stop=False, skip_group_check=True),
                 waits=mw0 if sub == 0 else ())
            b.op("tensor", mk("matmul", pb[6][:, sub:sub + 1],
                              cast_a[:, sub * 128:(sub + 1) * 128], lncol_lo[:, ds:ds + 1],
                              start=False, stop=False, skip_group_check=True))
            emv = b.op("tensor", mk("matmul", pb[6][:, sub:sub + 1],
                                    cast_b[:, sub * 128:(sub + 1) * 128], lncol_hi[:, ds:ds + 1],
                                    start=False, stop=False, skip_group_check=True),
                       inc="tM" if sub == 7 else None)
        e_vmm[ds] = emv
    ev6 = b.op("vector", mk("tensor_copy", colsum_c[:, :], pb[6][:, 0:8]),
               inc="vV", waits=[("tM", emv)])
    bank_gate[6] = ("vV", ev6)
    b.op("vector", mk("tensor_copy", colsum_cb[:, :], colsum_c[:, :]), inc="vV")
    b.op("vector", mk("scalar_tensor_tensor", colsum_lo[:, :], colsum_c[:, :], 1.0,
                      colsum_cb[:, :], Alu.mult, Alu.subtract), inc="vV")
    e_cc = b.cnt["vV"]
    e_fcmm = {}
    em3 = None
    for dc in range(8):
        lw = []
        if dc == 0:
            lw += [("qGN0", e_qgn_all[0]), ("qGN1", e_qgn_all[1]), ("vV", e_cast_last),
                   ("tM", e_vmm[7])]
        if dc >= 1:
            lw.append(("tM", e_fcmm[dc - 1]))
        ldf = b.dma("sync", "qFCT0",
                    mk("dma_start", out=cast_a,
                       in_=s_wfc[:, dc * 128:(dc + 1) * 128], transpose=True), waits=lw)
        ldf2 = b.dma("sync", "qFCT1",
                     mk("dma_start", out=cast_b,
                        in_=s_wfc_lo[:, dc * 128:(dc + 1) * 128], transpose=True),
                     waits=lw if dc == 0 else ())
        mw = [("qFCT0", ldf), ("qFCT1", ldf2)]
        if dc == 0:
            mw += [("vV", e_cc)] + bw(2) + bw(3)
        b.op("tensor", mk("matmul", pb[2][0:1, :], colsum_cb[:, dc:dc + 1],
                          cast_a[:, 0:512], start=(dc == 0), stop=False),
             waits=mw)
        b.op("tensor", mk("matmul", pb[2][0:1, :], colsum_lo[:, dc:dc + 1],
                          cast_a[:, 0:512], start=False, stop=False))
        b.op("tensor", mk("matmul", pb[2][0:1, :], colsum_cb[:, dc:dc + 1],
                          cast_b[:, 0:512], start=False, stop=(dc == 7)))
        b.op("tensor", mk("matmul", pb[3][0:1, :], colsum_cb[:, dc:dc + 1],
                          cast_a[:, 512:1024], start=(dc == 0), stop=False))
        b.op("tensor", mk("matmul", pb[3][0:1, :], colsum_lo[:, dc:dc + 1],
                          cast_a[:, 512:1024], start=False, stop=False))
        em3 = b.op("tensor", mk("matmul", pb[3][0:1, :], colsum_cb[:, dc:dc + 1],
                                cast_b[:, 512:1024], start=False, stop=(dc == 7)),
                   inc="tM")
        e_fcmm[dc] = em3
    e_colfc = None
    for eh in range(2):
        b.op("vector", mk("tensor_copy", colfc_b[0:1, eh * 512:(eh + 1) * 512],
                          pb[2 + eh][0:1, :]), inc="vV", waits=[("tM", em3)])
        e_colfc = b.op("vector", mk("scalar_tensor_tensor", colfc_lo[0:1, eh * 512:(eh + 1) * 512],
                                    pb[2 + eh][0:1, :], 1.0,
                                    colfc_b[0:1, eh * 512:(eh + 1) * 512],
                                    Alu.mult, Alu.subtract), inc="vV")
        bank_gate[2 + eh] = ("vV", e_colfc)
    e_p2_tensor = b.cnt["tM"]

    rt_slots = [AR3[:, 5120:7168], AR3[:, 7168:9216]]
    rt_items = ([("gu", mtx, ds, half) for mtx in (0, 1) for ds in range(8)
                 for half in range(2)] +
                [("wd", ic, 0, 0) for ic in range(32)])
    rt_state = {"i": 0, "st": {}}

    def rt_emit(n):
        while rt_state["i"] < len(rt_items) and n > 0:
            i = rt_state["i"]
            kind, a1, a2, a3 = rt_items[i]
            sl = i % 2
            slot = rt_slots[sl]
            lw = []
            if i == 0:
                lw += [("qAS0", e_qas_all[0]), ("qAS1", e_qas_all[1]),
                       ("tM", e_p2_tensor)]
            if i >= 2:
                lw.append((f"qTS{sl}", rt_state["st"][i - 2]))
            if kind == "gu":
                src = s_w3[a1]
                dst = s_wg2t if a1 == 0 else s_wu2t
                ldr = None
                for q in range(2):
                    ldr = b.dma("scalar", f"qTR{sl}",
                                mk("dma_start", out=slot[:, q * 1024:(q + 1) * 1024],
                                   in_=src[a3 * 2048 + q * 1024:a3 * 2048 + (q + 1) * 1024,
                                           a2 * 128:(a2 + 1) * 128],
                                   transpose=True), waits=lw if q == 0 else ())
                st = b.dma("scalar", f"qTS{sl}",
                           mk("dma_start",
                              out=dst[a2 * 128:(a2 + 1) * 128, a3 * 2048:(a3 + 1) * 2048],
                              in_=slot), waits=[(f"qTR{sl}", ldr)])
            else:
                sl16 = slot[:, 0:1024].bitcast(F16)
                ldr = b.dma("scalar", f"qTR{sl}",
                            mk("dma_start", out=sl16,
                               in_=s_w3[2][:, a1 * 128:(a1 + 1) * 128],
                               transpose=True), waits=lw)
                st = b.dma("scalar", f"qTS{sl}",
                           mk("dma_start", out=s_wd2t[a1 * 128:(a1 + 1) * 128, :],
                              in_=sl16), waits=[(f"qTR{sl}", ldr)])
            rt_state["st"][i] = st
            rt_state["i"] += 1
            n -= 1

    gbrow = fcB[0:1, 0:1024]
    e_gamld = b.dma("gpsimd", "qMB", mk("dma_start", out=gbrow, in_=gam_hbm[:, :]),
                    waits=[("vV", e_bfc), ("tM", e_grps), ("vV", e_p1_v)])
    ev = None
    em = None
    for eh in range(2):
        em = b.op("tensor", mk("matmul", pb[7][:, :], ones_rf[:, :],
                               gbrow[0:1, eh * 512:(eh + 1) * 512], start=True, stop=True),
                  inc="tM", waits=[("qMB", e_gamld)] + bw(7))
        b.op("vector", mk("tensor_copy", gam_b[:, eh * 512:(eh + 1) * 512], pb[7][:, :]),
             inc="vV", waits=[("tM", em), ("tM", e_grps)])
        ev = b.op("vector", mk("scalar_tensor_tensor", gam_lo[:, eh * 512:(eh + 1) * 512],
                               pb[7][:, :], 1.0, gam_b[:, eh * 512:(eh + 1) * 512],
                               Alu.mult, Alu.subtract), inc="vV")
        bank_gate[7] = ("vV", ev)
    e_betld = b.dma("gpsimd", "qMB", mk("dma_start", out=gbrow, in_=bet_hbm[:, :]),
                    waits=[("tM", em)])
    for eh in range(2):
        em = b.op("tensor", mk("matmul", pb[7][:, :], ones_rf[:, :],
                               gbrow[0:1, eh * 512:(eh + 1) * 512], start=True, stop=True),
                  inc="tM", waits=[("qMB", e_betld)] + bw(7))
        b.op("vector", mk("tensor_copy", bet_b[:, eh * 512:(eh + 1) * 512], pb[7][:, :]),
             inc="vV", waits=[("tM", em)])
        ev = b.op("vector", mk("scalar_tensor_tensor", bet_lo[:, eh * 512:(eh + 1) * 512],
                               pb[7][:, :], 1.0, bet_b[:, eh * 512:(eh + 1) * 512],
                               Alu.mult, Alu.subtract), inc="vV")
        bank_gate[7] = ("vV", ev)
    e_gambet_v = ev

    fc_bc = fcB[:, :]
    ev_fb = None
    for eh in range(2):
        w0 = [("vV", e_colfc), ("vV", e_bfc), ("vV", e_gambet_v)] + bw(4)
        b.op("tensor", mk("matmul", pb[4][:, :], ones_rb[:, :],
                          colfc_b[0:1, eh * 512:(eh + 1) * 512], start=True, stop=False),
             waits=w0)
        b.op("tensor", mk("matmul", pb[4][:, :], ones_rb[:, :],
                          colfc_lo[0:1, eh * 512:(eh + 1) * 512], start=False, stop=False))
        b.op("tensor", mk("matmul", pb[4][:, :], ones_rb[:, :],
                          bfc_hi[0:1, eh * 512:(eh + 1) * 512], start=False, stop=False))
        em = b.op("tensor", mk("matmul", pb[4][:, :], ones_rb[:, :],
                               bfc_lo[0:1, eh * 512:(eh + 1) * 512], start=False, stop=True),
                  inc="tM")
        ev_fb = b.op("vector", mk("tensor_copy", fc_bc[:, eh * 512:(eh + 1) * 512],
                                  pb[4][:, :]), inc="vV", waits=[("tM", em)])
        bank_gate[4] = ("vV", ev_fb)
    e_fcbc = ev_fb
    if KDBG:
        edf = b.dma("gpsimd", "qMB", mk("dma_start", out=s_dbg[:, 0:1024],
                                        in_=fc_bc), waits=[("vV", e_fcbc)])
        b.wait("vector", "qMB", edf)

    t2scr = AR4[:, 2048:3072]
    sqscr = t2scr
    ln1t = AR4[:, 0:1024]
    hbuf = AR4[:, 1024:2048]
    prev_ep = {}

    for qt in range(16):
        qslot = qt % 2
        xt = xstage[:, qslot * 1024:(qslot + 1) * 1024]
        xw = [("vV", e_cast_last)]
        if qt >= 2:
            xw.append(("vV", prev_ep[qt - 2]["x_done"]))
        ld_x = None
        for q2 in range(2):
            ld_x = b.dma("sync", f"qXT{qslot}",
                         mk("dma_start", out=xt[:, q2 * 512:(q2 + 1) * 512],
                            in_=x_hbm[qt * 128:(qt + 1) * 128, q2 * 512:(q2 + 1) * 512]),
                         waits=xw if q2 == 0 else ())
        lw = [(f"qXT{qslot}", ld_x), ("vV", e_fcbc)]
        ev_ln1t = b.op("vector", mk("tensor_scalar", ln1t, xt, mu1[:, qt:qt + 1], r1c[:, qt:qt + 1],
                                    Alu.subtract, Alu.mult), inc="vV", waits=lw)
        b.op("gpsimd", mk("tensor_tensor", xt, xt, bet_b[:, :], Alu.add), inc="gP",
             waits=[("vV", ev_ln1t)])
        eg_x4 = b.op("gpsimd", mk("tensor_tensor", xt, xt, bet_lo[:, :], Alu.add), inc="gP")
        hw = []
        if qt >= 1:
            hw.append(("qGX1", 16 * qt))
        b.op("vector", mk("tensor_add", hbuf, ln1t, fc_bc), inc="vV", waits=hw)
        b.op("vector", mk("reduce_sum", svec[:, 8:9], hbuf, AX), inc="vV")
        b.op("vector", mk("scalar_tensor_tensor", sqscr, hbuf, 1.0, hbuf,
                          Alu.mult, Alu.mult, accum_out=svec[:, 9:10]), inc="vV")
        b.op("vector", mk("tensor_single_scalar", svec[:, 10:11], svec[:, 8:9], 1.0 / D, Alu.mult), inc="vV")
        b.op("vector", mk("tensor_single_scalar", svec[:, 11:12], svec[:, 9:10], 1.0 / D, Alu.mult), inc="vV")
        b.op("vector", mk("scalar_tensor_tensor", svec[:, 12:13], svec[:, 10:11], 1.0,
                          svec[:, 10:11], Alu.mult, Alu.mult), inc="vV")
        b.op("vector", mk("tensor_sub", svec[:, 12:13], svec[:, 11:12], svec[:, 12:13]), inc="vV")
        b.op("vector", mk("tensor_single_scalar", svec[:, 12:13], svec[:, 12:13], LN_EPS, Alu.add), inc="vV")
        ev_r = b.op("vector", mk("reciprocal", svec[:, 13:14], svec[:, 12:13]), inc="vV")
        es_s = b.op("scalar", mk("activation", svec[:, 14:15], svec[:, 13:14], Act.Sqrt),
                    inc="sC", waits=[("vV", ev_r)])
        first = True
        for _ in range(2):
            b.op("vector", mk("scalar_tensor_tensor", svec[:, 15:16], svec[:, 14:15], 1.0,
                              svec[:, 14:15], Alu.mult, Alu.mult), inc="vV",
                 waits=[("sC", es_s)] if first else ())
            first = False
            b.op("vector", mk("tensor_mul", svec[:, 15:16], svec[:, 15:16], svec[:, 12:13]), inc="vV")
            b.op("vector", mk("tensor_scalar", svec[:, 15:16], svec[:, 15:16], -0.5, 1.5,
                              Alu.mult, Alu.add), inc="vV")
            b.op("vector", mk("tensor_mul", svec[:, 14:15], svec[:, 14:15], svec[:, 15:16]), inc="vV")
        ev_t2 = b.op("vector", mk("tensor_scalar", t2scr, hbuf, svec[:, 10:11], svec[:, 14:15],
                                  Alu.subtract, Alu.mult), inc="vV")
        t2b = AR4[:, 3072:4096]
        b.op("gpsimd", mk("tensor_mul", t2b, t2scr, gam_lo[:, :]), inc="gP",
             waits=[("vV", ev_t2)])
        eg_t3 = b.op("gpsimd", mk("tensor_mul", t2scr, t2scr, gam_b[:, :]), inc="gP")
        b.op("vector", mk("tensor_add", hbuf, t2scr, t2b), inc="vV",
             waits=[("gP", eg_t3), ("gP", eg_x4)])
        ev_x1 = b.op("vector", mk("tensor_add", hbuf, hbuf, xt), inc="vV")
        b.dma("gpsimd", "qGX1", mk("dma_start", out=s_x1[qt * 128:(qt + 1) * 128, :], in_=hbuf),
              waits=[("vV", ev_x1)])
        prev_ep[qt] = {"x_done": ev_x1}
        rt_emit(4)

    rt_emit(64)
    e_rt_wgwu = [("qTS0", rt_state["st"][30]), ("qTS1", rt_state["st"][31])]
    e_rt_all = [("qTS0", rt_state["st"][62]), ("qTS1", rt_state["st"][63])]
    e_p3_tensor = b.cnt["tM"]
    e_p3_v = b.cnt["vV"]
    e_p3_g = b.cnt["gP"]
    e_p3_x1st = b.cnt["qGX1"]

    ln2s = wstg[:, 0:1024]
    ybuf1 = wstg[:, 1024:2048]
    ybuf = AR3[:, 0:1024].bitcast(F16)
    r0b = qi[:, :]
    g1b = AR3[:, 1024:2048].bitcast(F16)
    g2b = AR3[:, 2048:3072].bitcast(F16)
    g3b = AR3[:, 3072:4096].bitcast(F16)
    q2sp = AR3[:, 4096:5120].bitcast(F16)
    q2sn = AR3[:, 5120:6144].bitcast(F16)
    xqo = AR3[:, 8192:9216]
    xq1T = [AR2[:, 0:4096], AR2[:, 4096:8192]]
    wgb = [AR2[:, 8192:12288], AR2[:, 12288:16384]]
    wub = [AR2[:, 16384:20480], AR2[:, 20480:24576]]
    wdb = [AR2[:, 24576:25600].bitcast(F16), AR2[:, 25600:26624].bitcast(F16)]
    sgs = AR2[:, 26624:27136].bitcast(F16)
    uss = AR2[:, 27136:27648].bitcast(F16)
    x1t = [AR4[:, 0:1024], AR4[:, 1024:2048]]
    x2t = [AR4[:, 2048:3072], AR4[:, 3072:4096]]

    e_q1 = {}
    for tt in range(16):
        slot = tt % 2
        xi = x1t[slot]
        w0 = [("qGX1", 16 * (tt + 1))]
        if tt < 2:
            w0.append(("qGX1", e_p3_x1st))
        if tt == 0:
            w0 += [("vV", e_p3_v), ("gP", e_p3_g), ("tM", e_p2_tensor),
                   ("vV", p0c_state["signB"][95])]
        if tt >= 2:
            w0.append(("vV", e_q1[tt - 2]))
        ld = None
        for q2 in range(2):
            ld = b.dma("sync", f"qX1{slot}",
                       mk("dma_start", out=xi[:, q2 * 512:(q2 + 1) * 512],
                          in_=s_x1[tt * 128:(tt + 1) * 128, q2 * 512:(q2 + 1) * 512]),
                       waits=w0 if q2 == 0 else ())
        sw = [(f"qX1{slot}", ld)]
        b.op("vector", mk("reduce_sum", svec[:, 0:1], xi, AX), inc="vV", waits=sw)
        b.op("vector", mk("scalar_tensor_tensor", ln2s, xi, 1.0, xi, Alu.mult, Alu.mult,
                          accum_out=svec[:, 1:2]), inc="vV")
        b.op("vector", mk("tensor_single_scalar", svec[:, 2:3], svec[:, 0:1], 1.0 / D, Alu.mult), inc="vV")
        b.op("vector", mk("tensor_single_scalar", svec[:, 3:4], svec[:, 1:2], 1.0 / D, Alu.mult), inc="vV")
        b.op("vector", mk("scalar_tensor_tensor", svec[:, 4:5], svec[:, 2:3], 1.0, svec[:, 2:3],
                          Alu.mult, Alu.mult), inc="vV")
        b.op("vector", mk("tensor_sub", svec[:, 4:5], svec[:, 3:4], svec[:, 4:5]), inc="vV")
        b.op("vector", mk("tensor_single_scalar", svec[:, 4:5], svec[:, 4:5], LN_EPS, Alu.add), inc="vV")
        ev_r = b.op("vector", mk("reciprocal", svec[:, 5:6], svec[:, 4:5]), inc="vV")
        es_s = b.op("scalar", mk("activation", svec[:, 6:7], svec[:, 5:6], Act.Sqrt),
                    inc="sC", waits=[("vV", ev_r)])
        first = True
        for _ in range(2):
            b.op("vector", mk("scalar_tensor_tensor", svec[:, 7:8], svec[:, 6:7], 1.0,
                              svec[:, 6:7], Alu.mult, Alu.mult), inc="vV",
                 waits=[("sC", es_s)] if first else ())
            first = False
            b.op("vector", mk("tensor_mul", svec[:, 7:8], svec[:, 7:8], svec[:, 4:5]), inc="vV")
            b.op("vector", mk("tensor_scalar", svec[:, 7:8], svec[:, 7:8], -0.5, 1.5,
                              Alu.mult, Alu.add), inc="vV")
            b.op("vector", mk("tensor_mul", svec[:, 6:7], svec[:, 6:7], svec[:, 7:8]), inc="vV")
        b.op("vector", mk("tensor_scalar", ln2s, xi, svec[:, 2:3], svec[:, 6:7],
                          Alu.subtract, Alu.mult), inc="vV")
        b.op("vector", mk("reduce_max", svec[:, 8:9], ln2s, AX, True), inc="vV")
        b.op("vector", mk("tensor_scalar_max", svec[:, 8:9], svec[:, 8:9], 1e-5), inc="vV")
        b.op("vector", mk("reciprocal", svec[:, 9:10], svec[:, 8:9]), inc="vV")
        b.op("vector", mk("tensor_single_scalar", s1c[:, tt:tt + 1], svec[:, 9:10], 127.0, Alu.mult), inc="vV")
        b.op("vector", mk("tensor_single_scalar", svec[:, 10:11], svec[:, 8:9], 1.0 / 127.0, Alu.mult), inc="vV")
        b.op("vector", mk("tensor_mul", a1g[:, tt:tt + 1], svec[:, 10:11], tcols[:, 3:4]), inc="vV",
             waits=[("vV", scale_ready[0]), ("vV", scale_ready[1])])
        ev_sc = b.op("vector", mk("tensor_mul", a1u[:, tt:tt + 1], svec[:, 10:11], tcols[:, 4:5]), inc="vV")
        es_y = b.op("scalar", mk("activation", ybuf1, ln2s, Act.Copy, scale=s1c[:, tt:tt + 1]),
                    inc="sC", waits=[("vV", ev_sc)])
        if KDBG and tt == 0:
            ed1 = b.dma("gpsimd", "qMB", mk("dma_start", out=s_dbg[:, 0:1024], in_=ybuf1),
                        waits=[("sC", es_y)])
            ed2 = b.dma("gpsimd", "qMB", mk("dma_start", out=s_dbg[:, 1024:2048], in_=ln2s))
            b.wait("vector", "qMB", ed2)
        e_r0 = b.op("vector", mk("tensor_copy", r0b, ybuf1), inc="vV", waits=[("sC", es_y)])
        e_g1a = b.op("vector", mk("scalar_tensor_tensor", g1b, ybuf1, -0.5, r0b, Alu.add, Alu.is_gt), inc="vV")
        e_g2a = b.op("vector", mk("scalar_tensor_tensor", g2b, ybuf1, 0.5, r0b, Alu.add, Alu.is_ge), inc="vV")
        if KDBG and tt == 0:
            ed3 = b.dma("gpsimd", "qMB", mk("dma_start", out=s_dbg[:, 2048:3072].bitcast(I32), in_=r0b),
                        waits=[("vV", e_g2a)])
            ed4 = b.dma("gpsimd", "qMB", mk("dma_start", out=s_dbg[:, 3072:4096], in_=g1b))
            ed5 = b.dma("gpsimd", "qMB", mk("dma_start", out=s_dbg[:, 4096:5120], in_=g2b))
            b.wait("vector", "qMB", ed5)
        b.op("vector", mk("scalar_tensor_tensor", g1b, g1b, -1.0, g2b, Alu.add, Alu.add), inc="vV")
        qw2 = [("qGQ1", 16 * tt)] if tt >= 1 else list(e_rt_all)
        ev_q = b.op("vector", mk("scalar_tensor_tensor", xqo, g1b, 1.0, r0b, Alu.mult, Alu.add),
                    inc="vV", waits=qw2)
        e_q1[tt] = ev_q
        b.dma("gpsimd", "qGQ1", mk("dma_start", out=s_xq1[tt * 128:(tt + 1) * 128, :], in_=xqo),
              waits=[("vV", ev_q)])
    e_q1_last = e_q1[15]

    AR1h = [AR1[:, 0:16384], AR1[:, 16384:32768]]

    def prod_ap(k, tl, dc):
        return AR1h[k % 2][:, tl * 4096 + dc * 512: tl * 4096 + (dc + 1) * 512].bitcast(F16)

    def prod_tt(k, tl):
        return AR1h[k % 2][:, tl * 4096:(tl + 1) * 4096].bitcast(F16)

    def xq2T_ap(k, ic):
        return AR1h[k % 2][:, ic * 512:(ic + 1) * 512].bitcast(F16)

    e_xq1T = {}
    e_gu_done = {}
    e_q2_done = {}
    e_xq2T = {}
    e_down_mm = {}
    e_out_store = {}
    e_prev_wchunk = {}
    e_prev_prod = [None]
    e_prev_x2 = {}
    e_wd_prev = {}

    def emit_xq1T(k):
        sl = k % 2
        w = [("qGQ1", 16 * (4 * k + 4))]
        if k >= 2:
            w.append(("tM", e_gu_done[k - 2]))
        if k == 0:
            w.append(("tM", e_p3_tensor))
            w += e_rt_all
        last = None
        for ds in range(8):
            last = b.dma("sync", f"qQ1T{sl}",
                         mk("dma_start", out=xq1T[sl][:, ds * 512:(ds + 1) * 512],
                            in_=s_xq1[k * 512:(k + 1) * 512, ds * 128:(ds + 1) * 128],
                            transpose=True), waits=w if ds == 0 else ())
        e_xq1T[k] = last

    def emit_gate_up(k):
        sl = k % 2
        last_mm = None
        e_am_init = {}
        for tl in range(4):
            aw = []
            if k >= 2:
                aw.append(("vV", e_q2_done[k - 2]))
            e_am_init[tl] = b.op("vector",
                                 mk("memset", am2[:, 4 * (k % 2) + tl: 4 * (k % 2) + tl + 1], 0.0),
                                 inc="vV", waits=aw)
        for dc in range(8):
            wsl = dc % 2
            wload_w = []
            if k == 0 and dc <= 1:
                wload_w += e_rt_wgwu
            if k == 0 and dc == 0:
                wload_w.append(("tM", e_p3_tensor))
            prev_key = (k, dc - 2) if dc >= 2 else ((k - 1, 6 + dc) if k > 0 else None)
            if prev_key is not None:
                wload_w.append(("tM", e_prev_wchunk.get(prev_key)))
            lg = None
            for ds in range(8):
                lg = b.dma("scalar", f"qWG{wsl}",
                           mk("dma_start", out=wgb[wsl][:, ds * 512:(ds + 1) * 512],
                              in_=s_wg2t[ds * 128:(ds + 1) * 128, dc * 512:(dc + 1) * 512]),
                           waits=wload_w if ds == 0 else ())
            lu = None
            for ds in range(8):
                lu = b.dma("scalar", f"qWU{wsl}",
                           mk("dma_start", out=wub[wsl][:, ds * 512:(ds + 1) * 512],
                              in_=s_wu2t[ds * 128:(ds + 1) * 128, dc * 512:(dc + 1) * 512]),
                           waits=wload_w if ds == 0 else ())
            for tl in range(4):
                tt = 4 * k + tl
                bg = (2 * tl) % 4
                bu = (2 * tl + 1) % 4
                w0 = [(f"qWG{wsl}", lg), (f"qQ1T{sl}", e_xq1T[k])] + bw(bg)
                emg = None
                for ds in range(8):
                    lhs = xq1T[sl][:, ds * 512 + tl * 128: ds * 512 + (tl + 1) * 128]
                    rhs = wgb[wsl][:, ds * 512:(ds + 1) * 512]
                    emg = b.op("tensor", mk("matmul", pb[bg][:, :], lhs, rhs,
                                            start=(ds == 0), stop=(ds == 7)),
                               inc="tM" if ds == 7 else None, waits=w0 if ds == 0 else ())
                sgw = [("tM", emg)]
                if e_prev_prod[0] is not None:
                    sgw.append(("vV", e_prev_prod[0]))
                esg = b.op("scalar", mk("activation", sgs, pb[bg][:, :], Act.Silu,
                                        scale=a1g[:, tt:tt + 1]), inc="sC", waits=sgw)
                bank_gate[bg] = ("sC", esg)
                w1 = [(f"qWU{wsl}", lu)] + bw(bu)
                emu = None
                for ds in range(8):
                    lhs = xq1T[sl][:, ds * 512 + tl * 128: ds * 512 + (tl + 1) * 128]
                    rhs = wub[wsl][:, ds * 512:(ds + 1) * 512]
                    emu = b.op("tensor", mk("matmul", pb[bu][:, :], lhs, rhs,
                                            start=(ds == 0), stop=(ds == 7)),
                               inc="tM" if ds == 7 else None, waits=w1 if ds == 0 else ())
                last_mm = emu
                esu = b.op("scalar", mk("activation", uss, pb[bu][:, :], Act.Copy,
                                        scale=a1u[:, tt:tt + 1]), inc="sC", waits=[("tM", emu)])
                bank_gate[bu] = ("sC", esu)
                pw = [("sC", esu)]
                if k >= 2 and dc == 0 and tl == 0:
                    pw.append(("qGX2", 16 * 4 * (k - 1)))
                ev_p = b.op("vector", mk("tensor_mul", prod_ap(k, tl, dc), sgs, uss),
                            inc="vV", waits=pw)
                e_prev_prod[0] = ev_p
                b.op("vector", mk("reduce_max", svec[:, 26:27], prod_ap(k, tl, dc), AX, True),
                     inc="vV", waits=[("vV", e_am_init[tl])])
                b.op("vector", mk("tensor_max", am2[:, 4 * (k % 2) + tl: 4 * (k % 2) + tl + 1],
                                  am2[:, 4 * (k % 2) + tl: 4 * (k % 2) + tl + 1], svec[:, 26:27]),
                     inc="vV")
            e_prev_wchunk[(k, dc)] = last_mm
        e_gu_done[k] = last_mm

    q2_st_prev = [None]

    def emit_quant2(k):
        last_v = None
        for tl in range(4):
            tt = 4 * k + tl
            amc = am2[:, 4 * (k % 2) + tl: 4 * (k % 2) + tl + 1]
            b.op("vector", mk("tensor_scalar_max", svec[:, 16:17], amc, 1e-5), inc="vV")
            b.op("vector", mk("reciprocal", svec[:, 17:18], svec[:, 16:17]), inc="vV")
            b.op("vector", mk("tensor_single_scalar", s2c[:, tt:tt + 1], svec[:, 17:18], 127.0,
                              Alu.mult), inc="vV")
            b.op("vector", mk("tensor_single_scalar", svec[:, 18:19], svec[:, 16:17], 1.0 / 127.0,
                              Alu.mult), inc="vV")
            ev_a2 = b.op("vector", mk("tensor_mul", a2d[:, tt:tt + 1], svec[:, 18:19], tcols[:, 5:6]),
                         inc="vV", waits=[("vV", scale_ready[2])])
            prev_g2 = None
            for sub in range(4):
                srcp = prod_tt(k, tl)[:, sub * 1024:(sub + 1) * 1024]
                yw = [("vV", ev_a2)]
                if prev_g2 is not None:
                    yw.append(("vV", prev_g2))
                es_y = b.op("scalar", mk("activation", ybuf, srcp, Act.Copy,
                                         scale=s2c[:, tt:tt + 1]), inc="sC", waits=yw)
                if KDBG:
                    edy = b.dma("gpsimd", "qMB",
                                mk("dma_start",
                                   out=s_dby[(4 * k + tl) * 128:(4 * k + tl + 1) * 128,
                                             sub * 1024:(sub + 1) * 1024],
                                   in_=ybuf), waits=[("sC", es_y)])
                    b.wait("scalar", "qMB", edy)
                    b.wait("vector", "qMB", edy)
                b.op("vector", mk("tensor_copy", r0b, ybuf), inc="vV", waits=[("sC", es_y)])
                b.op("vector", mk("tensor_copy", g3b, r0b), inc="vV")
                prev_g2 = b.op("vector", mk("tensor_sub", g1b, ybuf, g3b), inc="vV")
                b.op("vector", mk("tensor_single_scalar", g2b, g1b, 0.5, Alu.is_gt), inc="vV")
                b.op("vector", mk("tensor_single_scalar", q2sp, g1b, 1.5, Alu.is_gt), inc="vV")
                b.op("vector", mk("tensor_add", g2b, g2b, q2sp), inc="vV")
                b.op("vector", mk("tensor_single_scalar", q2sp, g1b, -0.5, Alu.is_ge), inc="vV")
                b.op("vector", mk("tensor_single_scalar", q2sn, g1b, -1.5, Alu.is_ge), inc="vV")
                b.op("vector", mk("tensor_add", q2sp, q2sp, q2sn), inc="vV")
                b.op("vector", mk("tensor_add", g2b, g2b, q2sp), inc="vV")
                b.op("vector", mk("tensor_single_scalar", g2b, g2b, -2.0, Alu.add), inc="vV")
                last_v = b.op("vector", mk("tensor_add", srcp, g2b, g3b), inc="vV")
            sw_ = [("vV", last_v)]
            if q2_st_prev[0] is not None:
                sw_.append(("qGX2", q2_st_prev[0]))
            q2_st_prev[0] = b.dma("gpsimd", "qGX2",
                                  mk("dma_start", out=s_xq2[tt * 128:(tt + 1) * 128, :],
                                     in_=prod_tt(k, tl)),
                                  waits=sw_)
        e_q2_done[k] = last_v

    xqTb = [AR3[:, 6144 + s * 512:6656 + s * 512].bitcast(F16) for s in range(4)]
    xqT_sem = ["qQ2T0", "qQ2T1", "qQT0", "qQT1"]

    def emit_down(k):
        e_wd = {}
        for ic in range(32):
            wsl = ic % 2
            ts_ = ic % 4
            tw = []
            if ic == 0:
                tw.append(("qGX2", 16 * 4 * (k + 1)))
            if ic >= 4:
                tw.append(("tM", e_wd[ic - 4]))
            elif k > 0:
                tw.append(("tM", e_wd_prev.get(28 + ic)))
            ld_t = b.dma("sync", xqT_sem[ts_],
                         mk("dma_start", out=xqTb[ts_],
                            in_=s_xq2[k * 512:(k + 1) * 512, ic * 128:(ic + 1) * 128],
                            transpose=True), waits=tw)
            w = []
            if k == 0 and ic <= 1:
                w += e_rt_all
            if ic >= 2:
                w.append(("tM", e_wd[ic - 2]))
            elif k > 0:
                w.append(("tM", e_wd_prev.get(30 + ic)))
            ld = b.dma("scalar", f"qWD{wsl}",
                       mk("dma_start", out=wdb[wsl],
                          in_=s_wd2t[ic * 128:(ic + 1) * 128, :]),
                       waits=w)
            em = None
            for tl in range(4):
                for eh in range(2):
                    bank = tl * 2 + eh
                    w0 = [(f"qWD{wsl}", ld), (xqT_sem[ts_], ld_t)] + (bw(bank) if ic == 0 else [])
                    em = b.op("tensor",
                              mk("matmul", pb[bank][:, :],
                                 xqTb[ts_][:, tl * 128:(tl + 1) * 128],
                                 wdb[wsl][:, eh * 512:(eh + 1) * 512],
                                 start=(ic == 0), stop=(ic == 31)),
                              inc="tM" if (tl == 3 and eh == 1) else None,
                              waits=w0)
            e_wd[ic] = em
        e_down_mm[k] = e_wd[31]
        e_wd_prev.clear()
        e_wd_prev.update(e_wd)
        for tl in range(4):
            tt = 4 * k + tl
            slot = tt % 2
            xi = x1t[slot]
            lw = [("vV", e_q1_last)]
            if tt >= 2:
                lw.append(("vV", e_prev_x2.get(tt - 2)))
            ld = None
            for q2 in range(2):
                ld = b.dma("sync", f"qX1{slot}",
                           mk("dma_start", out=xi[:, q2 * 512:(q2 + 1) * 512],
                              in_=s_x1[tt * 128:(tt + 1) * 128, q2 * 512:(q2 + 1) * 512]),
                           waits=lw if q2 == 0 else ())
            ev = None
            for eh in range(2):
                bank = tl * 2 + eh
                xw = [("tM", e_down_mm[k]), (f"qX1{slot}", ld)]
                if tt < 2:
                    xw += [("gP", e_p3_g), ("vV", e_p3_v)]
                if tt >= 2:
                    xw.append((f"qGO{slot}", e_out_store.get(tt - 2)))
                ev = b.op("vector", mk("scalar_tensor_tensor", x2t[slot][:, eh * 512:(eh + 1) * 512],
                                       pb[bank][:, :], a2d[:, tt:tt + 1],
                                       xi[:, eh * 512:(eh + 1) * 512], Alu.mult, Alu.add),
                          inc="vV", waits=xw)
                bank_gate[bank] = ("vV", ev)
            e_prev_x2[tt] = ev
            st = b.dma("gpsimd", f"qGO{slot}",
                       mk("dma_start", out=out_hbm[tt * 128:(tt + 1) * 128, :], in_=x2t[slot]),
                       waits=[("vV", ev)])
            e_out_store[tt] = st

    emit_xq1T(0)
    emit_gate_up(0)
    emit_quant2(0)
    emit_xq1T(1)
    emit_gate_up(1)
    emit_quant2(1)
    emit_down(0)
    emit_xq1T(2)
    emit_gate_up(2)
    emit_quant2(2)
    emit_down(1)
    emit_xq1T(3)
    emit_gate_up(3)
    emit_quant2(3)
    emit_down(2)
    emit_down(3)

    b.wait("scalar", "qGO0", b.cnt["qGO0"])
    b.wait("scalar", "qGO1", b.cnt["qGO1"])
    b.wait("sync", "qGO0", b.cnt["qGO0"])
    b.wait("sync", "qGO1", b.cnt["qGO1"])

    for name in SEMS:
        b.sems[name] = stack.enter_context(nc.semaphore(name))
    with nc.Block() as block:
        @block.sync
        def _(E):
            for th in b.ops["sync"]:
                th(E)

        @block.tensor
        def _(E):
            for th in b.ops["tensor"]:
                th(E)

        @block.vector
        def _(E):
            for th in b.ops["vector"]:
                th(E)

        @block.scalar
        def _(E):
            for th in b.ops["scalar"]:
                th(E)

        @block.gpsimd
        def _(E):
            for th in b.ops["gpsimd"]:
                th(E)

    stack.close()
    return nc


_NC_CACHE = None


def kernel(L0_attn_x, L0_attn_w_qkv, L0_attn_w_fc, L0_attn_b_fc, L0_attn_gamma,
           L0_attn_beta, L0_ffn_w_gate, L0_ffn_w_up, L0_ffn_w_down):
    global _NC_CACHE
    x = np.ascontiguousarray(np.asarray(L0_attn_x, dtype=np.float32))
    B_, S_, D_ = x.shape
    weights = {
        "w_qkv": np.ascontiguousarray(np.asarray(L0_attn_w_qkv, dtype=np.float32)),
        "w_fc": np.ascontiguousarray(np.asarray(L0_attn_w_fc, dtype=np.float32)),
        "b_fc": np.ascontiguousarray(np.asarray(L0_attn_b_fc, dtype=np.float32)).reshape(1, D),
        "gamma": np.ascontiguousarray(np.asarray(L0_attn_gamma, dtype=np.float32)).reshape(1, D),
        "beta": np.ascontiguousarray(np.asarray(L0_attn_beta, dtype=np.float32)).reshape(1, D),
        "w_gate": np.ascontiguousarray(np.asarray(L0_ffn_w_gate, dtype=np.float32)),
        "w_up": np.ascontiguousarray(np.asarray(L0_ffn_w_up, dtype=np.float32)),
        "w_down": np.ascontiguousarray(np.asarray(L0_ffn_w_down, dtype=np.float32)),
    }
    in_maps = []
    for c in range(8):
        bi, h = divmod(c, 2)
        own = x[bi, h * T:(h + 1) * T, :]
        oth = x[bi, (1 - h) * T:(2 - h) * T, :]
        xc = np.ascontiguousarray(np.concatenate([own, oth], axis=0))
        m = {"x_full": xc}
        m.update(weights)
        in_maps.append(m)

    if _NC_CACHE is None:
        nc_ = bass.Bass("TRN2", target_bir_lowering=False)
        build_program(nc_)
        _NC_CACHE = nc_
    res = run_bass_kernel_spmd(_NC_CACHE, in_maps, core_ids=list(range(8)))
    out = np.empty_like(x)
    for c in range(8):
        bi, h = divmod(c, 2)
        out[bi, h * T:(h + 1) * T, :] = res.results[c]["out"]
    return out


# version 22
# speedup vs baseline: 1.1757x; 1.1757x over previous
import numpy as np
import ml_dtypes
import concourse.bass as bass
import concourse.mybir as mybir
from concourse.bass_utils import run_bass_kernel_spmd
from contextlib import ExitStack

BFD = ml_dtypes.bfloat16
F32 = mybir.dt.float32
BF16 = mybir.dt.bfloat16
F16 = mybir.dt.float16
I32 = mybir.dt.int32
Alu = mybir.AluOpType
Act = mybir.ActivationFunctionType
AX = mybir.AxisListType.X

S, T, D, NE, NI, P = 4096, 2048, 1024, 3072, 4096, 128
KDBG = False
Q2ENG = "vector"
LN_EPS = 1e-5
INV32 = 1.0 / 32.0

ENGINES = ("sync", "tensor", "vector", "scalar", "gpsimd")
SERIAL = ("vector", "scalar", "gpsimd")

SEMS = (["tM", "vV", "sC", "gP", "sP0", "gC",
         "qM", "qMB", "qP1X", "qLN1T", "qGX1", "qGQ1", "qGX2", "qTR0", "qTR1"] +
        [f"{p}{i}" for p in ("qFCT", "qWN", "qWT", "qQT", "qXT", "qX1", "qQ1T", "qWG",
                             "qWU", "qQ2T", "qWD", "qA", "qAS", "qGLN", "qGQs",
                             "qGN", "qGO", "qTS") for i in (0, 1)])


def mk(meth, *args, **kw):
    def f(E):
        return getattr(E, meth)(*args, **kw)
    return f


class Builder:
    def __init__(self, sem_names):
        self.ops = {e: [] for e in ENGINES}
        self.cnt = {s: 0 for s in sem_names}
        self.sems = {}
        self.serial = {e: None for e in SERIAL}

    def op(self, eng, fn, inc=None, waits=(), n=1, is_dma=False):
        if inc is not None:
            self.cnt[inc] += n
        val = self.cnt[inc] if inc is not None else None
        waits = list(waits)
        if eng in SERIAL and not is_dma:
            m = self.serial.get(eng)
            if m:
                waits.append(m)
            if inc is not None:
                self.serial[eng] = (inc, val)
        waits = [(s, v) for (s, v) in waits if v is not None and v > 0]
        sems = self.sems
        inc_pair = (inc, n) if inc is not None else None

        def thunk(E):
            for s, v in waits:
                E.wait_ge(sems[s], v)
            ins = fn(E)
            if inc_pair is not None:
                ins.then_inc(sems[inc_pair[0]], inc_pair[1])
        self.ops[eng].append(thunk)
        return val

    def splice(self, eng, thunk, sem, val, pre_cell):
        if eng in SERIAL:
            pre_cell[0] = self.serial.get(eng)
            self.serial[eng] = (sem, val)
        self.ops[eng].append(thunk)

    def dma(self, eng, q, fn, waits=()):
        return self.op(eng, fn, inc=q, waits=waits, n=16, is_dma=True)

    def wait(self, eng, sem, val):
        if val is None or val <= 0:
            return
        sems = self.sems

        def thunk(E):
            E.wait_ge(sems[sem], val)
        self.ops[eng].append(thunk)


def build_program(nc):
    b = Builder(SEMS)

    x_hbm = nc.dram_tensor("x_full", [S, D], F32, kind="ExternalInput").ap()
    wqkv_hbm = nc.dram_tensor("w_qkv", [NE, D], F32, kind="ExternalInput").ap()
    wfc_hbm = nc.dram_tensor("w_fc", [D, D], F32, kind="ExternalInput").ap()
    bfc_hbm = nc.dram_tensor("b_fc", [1, D], F32, kind="ExternalInput").ap()
    gam_hbm = nc.dram_tensor("gamma", [1, D], F32, kind="ExternalInput").ap()
    bet_hbm = nc.dram_tensor("beta", [1, D], F32, kind="ExternalInput").ap()
    w3_hbm = [nc.dram_tensor("w_gate", [NI, D], F32, kind="ExternalInput").ap(),
              nc.dram_tensor("w_up", [NI, D], F32, kind="ExternalInput").ap(),
              nc.dram_tensor("w_down", [D, NI], F32, kind="ExternalInput").ap()]
    out_hbm = nc.dram_tensor("out", [T, D], F32, kind="ExternalOutput").ap()

    s_ln1 = nc.dram_tensor("s_ln1", [S, D], F16, kind="Internal").ap()
    s_q = nc.dram_tensor("s_q", [D, T], F16, kind="Internal").ap()
    s_wqkv = nc.dram_tensor("s_wqkv", [NE, D], F16, kind="Internal").ap()
    s_wv_lo = nc.dram_tensor("s_wv_lo", [D, D], F16, kind="Internal").ap()
    s_wfc = nc.dram_tensor("s_wfc", [D, D], F16, kind="Internal").ap()
    s_wfc_lo = nc.dram_tensor("s_wfc_lo", [D, D], F16, kind="Internal").ap()
    s_w3 = [nc.dram_tensor("s_wg2", [NI, D], BF16, kind="Internal").ap(),
            nc.dram_tensor("s_wu2", [NI, D], BF16, kind="Internal").ap(),
            nc.dram_tensor("s_wd2", [D, NI], F16, kind="Internal").ap()]
    s_wg2t = nc.dram_tensor("s_wg2t", [D, NI], BF16, kind="Internal").ap()
    s_wu2t = nc.dram_tensor("s_wu2t", [D, NI], BF16, kind="Internal").ap()
    s_wd2t = nc.dram_tensor("s_wd2t", [NI, D], F16, kind="Internal").ap()
    s_x1 = nc.dram_tensor("s_x1", [T, D], F32, kind="Internal").ap()
    s_xq1 = nc.dram_tensor("s_xq1", [T, D], BF16, kind="Internal").ap()
    s_xq2 = nc.dram_tensor("s_xq2", [T, NI], F16, kind="Internal").ap()
    s_dbg = nc.dram_tensor("s_dbg", [P, 5120], F32, kind="Internal").ap() if KDBG else None
    s_dby = nc.dram_tensor("s_dby", [T, NI], F16, kind="Internal").ap() if KDBG else None

    stack = ExitStack()
    sb = lambda name, shape, dt: stack.enter_context(nc.sbuf_tensor(name, shape, dt))
    psa = lambda name: stack.enter_context(nc.psum_tensor(name, [P, 512], F32))

    AR1 = sb("AR1", [P, 32768], BF16)
    AR2 = sb("AR2", [P, 32768], BF16)
    AR3 = sb("AR3", [P, 9216], BF16)
    AR4 = sb("AR4", [P, 4096], F32)

    ident = sb("ident", [P, P], F16)
    iota_a = sb("iota_a", [P, P], F32)
    iota_b = sb("iota_b", [P, P], F32)
    ones_rb = sb("ones_rb", [1, P], F16)
    ones_rf = sb("ones_rf", [1, P], F32)
    ones_cf = sb("ones_cf", [P, 1], F32)
    bfc_hi = sb("bfc_hi", [1, D], F16)
    bfc_lo = sb("bfc_lo", [1, D], F16)
    colfc_b = sb("colfc_b", [1, D], F16)
    colfc_lo = sb("colfc_lo", [1, D], F16)
    colsum_c = sb("colsum_c", [P, 8], F32)
    lnsum_c = sb("lnsum_c", [P, 8], F32)
    lncol_hi = sb("lncol_hi", [P, 8], F16)
    lncol_lo = sb("lncol_lo", [P, 8], F16)
    colsum_cb = sb("colsum_cb", [P, 8], F16)
    colsum_lo = sb("colsum_lo", [P, 8], F16)
    mu1 = sb("mu1", [P, 16], F32)
    r1c = sb("r1c", [P, 16], F32)
    mur2 = sb("mur2", [P, 32], F32)
    musq = sb("musq", [P, 16], F32)
    sqs = sb("sqs", [P, 16], F32)
    a1g = sb("a1g", [P, 16], F32)
    a1u = sb("a1u", [P, 16], F32)
    a2d = sb("a2d", [P, 16], F32)
    s1c = sb("s1c", [P, 16], F32)
    s2c = sb("s2c", [P, 16], F32)
    am2 = sb("am2", [P, 8], F32)
    svec = sb("svec", [P, 32], F32)
    wacc = sb("wacc", [P, 4], F32)
    tcols = sb("tcols", [P, 12], F32)
    rowt = sb("rowt", [1, 16], F32)
    rs_row = sb("rs_row", [1, P], F16)
    rs_col = sb("rs_col", [P, 1], F16)
    rs_row2 = sb("rs_row2", [1, P], F16)
    rs_col2 = sb("rs_col2", [P, 1], F16)
    rs_f = sb("rs_f", [P, 1], F32)
    xstage = sb("xstage", [P, 2048], F32)
    wstg = sb("wstg", [P, 2048], F32)
    p0scr = sb("p0scr", [P, 4096], BF16)
    p0f = sb("p0f", [P, 512], F32)
    qi = sb("qi", [P, 1024], I32)
    fcB = sb("fcB", [P, 1024], F32)

    pb = [psa(f"pb{i}") for i in range(8)]
    bank_gate = [None] * 8

    def bw(i):
        return [bank_gate[i]] if bank_gate[i] else []

    gam_b = AR1[:, 30720:31744].bitcast(F16)
    bet_b = AR1[:, 31744:32768].bitcast(F16)
    gam_lo = AR1[:, 28672:29696].bitcast(F16)
    bet_lo = AR1[:, 29696:30720].bitcast(F16)
    brow = xstage[0:1, 0:1024]

    b.op("gpsimd", mk("iota", iota_a[:, :], [[1, P]], channel_multiplier=0,
                      allow_small_or_imprecise_dtypes=True), inc="gP")
    e = b.op("gpsimd", mk("iota", iota_b[:, :], [[1, P]], channel_multiplier=1,
                          allow_small_or_imprecise_dtypes=True), inc="gP")
    b.op("vector", mk("tensor_sub", iota_b[:, :], iota_b[:, :], iota_a[:, :]),
         inc="vV", waits=[("gP", e)])
    b.op("vector", mk("tensor_tensor", ident[:, :], iota_b[:, :], iota_a[:, :], Alu.is_equal), inc="vV")
    b.op("vector", mk("memset", ones_rb[:, :], 1.0), inc="vV")
    b.op("vector", mk("memset", ones_rf[:, :], 1.0), inc="vV")
    b.op("vector", mk("memset", ones_cf[:, :], 1.0), inc="vV")
    b.op("vector", mk("memset", colsum_c[:, :], 0.0), inc="vV")
    b.op("vector", mk("memset", wacc[:, :], 0.0), inc="vV")
    e_init_v = b.cnt["vV"]

    e_bv = b.dma("sync", "qM", mk("dma_start", out=brow[:, :], in_=bfc_hbm[:, :]))
    b.op("vector", mk("tensor_copy", bfc_hi[:, :], brow[:, :]), inc="vV", waits=[("qM", e_bv)])
    e_bfc = b.op("vector", mk("scalar_tensor_tensor", bfc_lo[:, :], brow[:, :], 1.0, bfc_hi[:, :],
                              Alu.mult, Alu.subtract), inc="vV")

    p0_scrap = p0f[:, :]
    sgA = p0scr[:, 0:1024]
    sgB = p0scr[:, 1024:2048]
    qout = [p0scr[:, 2048:3072], p0scr[:, 3072:4096]]

    def w3_tile(m, j):
        if m < 2:
            return w3_hbm[m][j * 128:(j + 1) * 128, :]
        r, qc = divmod(j, 4)
        return w3_hbm[2][r * 128:(r + 1) * 128, qc * 1024:(qc + 1) * 1024]

    def s3_tile(m, j):
        if m < 2:
            return s_w3[m][j * 128:(j + 1) * 128, :]
        r, qc = divmod(j, 4)
        return s_w3[2][r * 128:(r + 1) * 128, qc * 1024:(qc + 1) * 1024]

    p0b_abs = []
    p0b_tiles_done = [0]

    def p0b_tiles_chunk(n):
        while p0b_tiles_done[0] < 96 and n > 0:
            idx = p0b_tiles_done[0]
            m, j = divmod(idx, 32)
            slot = idx % 2
            stg = wstg[:, slot * 1024:(slot + 1) * 1024]
            waits = []
            if idx >= 2:
                waits.append(("vV", p0b_abs[idx - 2]))
            src = w3_tile(m, j)
            ld = None
            for q4 in range(4):
                ld = b.dma("scalar", f"qA{slot}",
                           mk("dma_start", out=stg[:, q4 * 256:(q4 + 1) * 256],
                              in_=src[:, q4 * 256:(q4 + 1) * 256]),
                           waits=waits if q4 == 0 else ())
            aw = [(f"qA{slot}", ld)]
            if idx >= 2:
                aw.append(("gC", 2 * (idx - 2) + 2))
            b.op("vector", mk("scalar_tensor_tensor", p0_scrap, stg[:, 0:512], -1.0,
                              stg[:, 0:512], Alu.mult, Alu.max,
                              accum_out=svec[:, 20 + slot:21 + slot]),
                 inc="vV", waits=aw)
            ab = b.op("vector", mk("scalar_tensor_tensor", p0_scrap, stg[:, 512:1024], -1.0,
                                   stg[:, 512:1024], Alu.mult, Alu.max,
                                   accum_out=svec[:, 22 + slot:23 + slot]), inc="vV")
            p0b_abs.append(ab)
            p0b_tiles_done[0] += 1
            n -= 1

    p0b_emitted = [0]

    def p0b_add_chunk(n):
        while p0b_emitted[0] < len(p0b_abs) and n > 0:
            idx = p0b_emitted[0]
            m = idx // 32
            slot = idx % 2
            aw0 = [("vV", p0b_abs[idx])]
            if idx == 0:
                aw0.append(("vV", e_init_v))
            b.op("gpsimd", mk("tensor_add", wacc[:, m:m + 1], wacc[:, m:m + 1],
                              svec[:, 20 + slot:21 + slot]),
                 inc="gC", waits=aw0)
            b.op("gpsimd", mk("tensor_add", wacc[:, m:m + 1], wacc[:, m:m + 1],
                              svec[:, 22 + slot:23 + slot]), inc="gC")
            p0b_emitted[0] += 1
            n -= 1

    scale_ready = [None, None, None]

    def p0_scale(m):
        em = b.op("tensor", mk("matmul", pb[7][0:1, 0:1], wacc[:, m:m + 1], ones_cf[:, :],
                               start=True, stop=True),
                  inc="tM", waits=[("gC", (m + 1) * 64)] + bw(7))
        ev = b.op("vector", mk("tensor_copy", rowt[0:1, m:m + 1], pb[7][0:1, 0:1]),
                  inc="vV", waits=[("tM", em)])
        bank_gate[7] = ("vV", ev)
        b.op("vector", mk("tensor_scalar", rowt[0:1, 4 + m:5 + m], rowt[0:1, m:m + 1],
                          1.0 / (NI * D), 1e-5, Alu.mult, Alu.max), inc="vV")
        vv_now = b.cnt["vV"]
        em2 = b.op("tensor", mk("matmul", pb[7][:, 0:1], ones_rf[:, :], rowt[0:1, 4 + m:5 + m],
                                start=True, stop=True),
                   inc="tM", waits=[("vV", vv_now)] + bw(7))
        ev2 = b.op("vector", mk("tensor_copy", tcols[:, m:m + 1], pb[7][:, 0:1]),
                   inc="vV", waits=[("tM", em2)])
        bank_gate[7] = ("vV", ev2)
        b.op("vector", mk("tensor_single_scalar", tcols[:, 3 + m:4 + m], tcols[:, m:m + 1],
                          0.5, Alu.mult), inc="vV")
        ev3 = b.op("vector", mk("tensor_single_scalar", tcols[:, 6 + m:7 + m], tcols[:, 3 + m:4 + m],
                                -1.0, Alu.mult), inc="vV")
        scale_ready[m] = ev3

    p0c_state = {"i": 0, "signB": {}, "comb": {}, "st": {0: [], 1: []}}

    def p0c_tiles(n):
        while p0c_state["i"] < 96 and n > 0:
            idx = p0c_state["i"]
            m, j = divmod(idx, 32)
            slot = idx % 2
            stg = wstg[:, slot * 1024:(slot + 1) * 1024]
            lw = [("vV", p0b_abs[95])]
            if idx >= 2:
                lw.append(("vV", p0c_state["signB"][idx - 2]))
            src = w3_tile(m, j)
            ld = None
            for q4 in range(4):
                ld = b.dma("scalar", f"qA{slot}",
                           mk("dma_start", out=stg[:, q4 * 256:(q4 + 1) * 256],
                              in_=src[:, q4 * 256:(q4 + 1) * 256]),
                           waits=lw if q4 == 0 else ())
            wA = [(f"qA{slot}", ld), ("vV", scale_ready[m])]
            qv = qout[slot] if m < 2 else qout[slot].bitcast(F16)
            b.op("vector", mk("tensor_scalar", sgA[:, :], stg, tcols[:, 3 + m:4 + m], 2.0,
                              Alu.is_gt, Alu.mult), inc="vV", waits=wA)
            vB = b.op("vector", mk("tensor_scalar", sgB[:, :], stg, tcols[:, 6 + m:7 + m], 2.0,
                                   Alu.is_ge, Alu.mult), inc="vV")
            p0c_state["signB"][idx] = vB
            wC = []
            if len(p0c_state["st"][slot]) >= 1:
                wC.append((f"qAS{slot}", p0c_state["st"][slot][-1]))
            vC = b.op("vector", mk("scalar_tensor_tensor", qv, sgA[:, :], -2.0, sgB[:, :],
                                   Alu.add, Alu.add), inc="vV", waits=wC)
            p0c_state["comb"][idx] = vC
            st = b.dma("scalar", f"qAS{slot}", mk("dma_start", out=s3_tile(m, j), in_=qv),
                       waits=[("vV", vC)])
            p0c_state["st"][slot].append(st)
            p0c_state["i"] += 1
            n -= 1

    p0b_tiles_chunk(2)
    p0b_add_chunk(2)
    scr1 = AR3[:, 0:1024]
    e_norm_done = {}
    e_lnsum_mm = [None]
    e_lnsum_slot = {0: None, 1: None}
    for g in range(8):
        if g < 4:
            mu_dst = mu1[:, g * 4:(g + 1) * 4]
            r_dst = r1c[:, g * 4:(g + 1) * 4]
        else:
            mu_dst = mur2[:, (g - 4) * 4:(g - 3) * 4]
            r_dst = mur2[:, 16 + (g - 4) * 4:16 + (g - 3) * 4]
        for i in range(4):
            t_idx = g * 4 + i
            xt = AR4[:, i * 1024:(i + 1) * 1024]
            w = [("vV", e_norm_done[t_idx - 4])] if t_idx >= 4 else []
            for q2 in range(2):
                b.dma("gpsimd", "qP1X",
                      mk("dma_start", out=xt[:, q2 * 512:(q2 + 1) * 512],
                         in_=x_hbm[t_idx * 128:(t_idx + 1) * 128, q2 * 512:(q2 + 1) * 512]),
                      waits=w if q2 == 0 else ())
        grp_total = b.cnt["qP1X"]
        for i in range(4):
            xt = AR4[:, i * 1024:(i + 1) * 1024]
            b.op("vector", mk("reduce_sum", musq[:, i:i + 1], xt, AX),
                 inc="vV", waits=[("qP1X", grp_total)] if i == 0 else ())
            b.op("vector", mk("scalar_tensor_tensor", scr1, xt, 1.0, xt,
                              Alu.mult, Alu.mult, accum_out=sqs[:, i:i + 1]), inc="vV")
        b.op("vector", mk("tensor_single_scalar", mu_dst, musq[:, 0:4], 1.0 / D, Alu.mult), inc="vV")
        b.op("vector", mk("tensor_single_scalar", sqs[:, 0:4], sqs[:, 0:4], 1.0 / D, Alu.mult), inc="vV")
        b.op("vector", mk("scalar_tensor_tensor", svec[:, 0:4], mu_dst, 1.0,
                          mu_dst, Alu.mult, Alu.mult), inc="vV")
        b.op("vector", mk("tensor_sub", svec[:, 0:4], sqs[:, 0:4], svec[:, 0:4]), inc="vV")
        b.op("vector", mk("tensor_single_scalar", svec[:, 0:4], svec[:, 0:4], LN_EPS, Alu.add), inc="vV")
        ev_ve = b.op("vector", mk("reciprocal", musq[:, 0:4], svec[:, 0:4]), inc="vV")
        es = b.op("scalar", mk("activation", r_dst, musq[:, 0:4], Act.Sqrt),
                  inc="sC", waits=[("vV", ev_ve)])
        first = True
        for _ in range(2):
            b.op("vector", mk("scalar_tensor_tensor", sqs[:, 0:4], r_dst, 1.0,
                              r_dst, Alu.mult, Alu.mult), inc="vV",
                 waits=[("sC", es)] if first else ())
            first = False
            b.op("vector", mk("tensor_mul", sqs[:, 0:4], sqs[:, 0:4], svec[:, 0:4]), inc="vV")
            b.op("vector", mk("tensor_scalar", sqs[:, 0:4], sqs[:, 0:4], -0.5, 1.5,
                              Alu.mult, Alu.add), inc="vV")
            b.op("vector", mk("tensor_mul", r_dst, r_dst, sqs[:, 0:4]), inc="vV")
        for i in range(4):
            t_idx = g * 4 + i
            xt = AR4[:, i * 1024:(i + 1) * 1024]
            slot = t_idx % 2
            lnf = xstage[:, slot * 1024:(slot + 1) * 1024]
            waits = []
            if e_lnsum_slot[slot] is not None:
                waits.append(("tM", e_lnsum_slot[slot]))
            ev_f = b.op("vector", mk("tensor_scalar", lnf, xt,
                                     mu_dst[:, i:i + 1], r_dst[:, i:i + 1],
                                     Alu.subtract, Alu.mult), inc="vV", waits=waits)
            e_norm_done[t_idx] = ev_f
            em5 = None
            for ds in range(8):
                em5 = b.op("tensor", mk("matmul", pb[5][:, ds:ds + 1],
                                        lnf[:, ds * 128:(ds + 1) * 128], ones_cf[:, :],
                                        start=(t_idx == 0 and ds == 0), stop=False,
                                        skip_group_check=True),
                           inc="tM" if ds == 7 else None,
                           waits=[("vV", ev_f)] + (bw(5) if t_idx == 0 and ds == 0 else []))
            e_lnsum_slot[slot] = em5
            e_lnsum_mm[0] = em5
            p0b_tiles_chunk(3)
            p0b_add_chunk(3)
    p0b_tiles_chunk(96)
    p0b_add_chunk(96)
    ev5 = b.op("vector", mk("tensor_copy", lnsum_c[:, :], pb[5][:, 0:8]),
               inc="vV", waits=[("tM", e_lnsum_mm[0])])
    bank_gate[5] = ("vV", ev5)
    b.op("vector", mk("tensor_copy", lncol_hi[:, :], lnsum_c[:, :]), inc="vV")
    b.op("vector", mk("scalar_tensor_tensor", lncol_lo[:, :], lnsum_c[:, :], 1.0,
                      lncol_hi[:, :], Alu.mult, Alu.subtract), inc="vV")
    e_p1_v = b.cnt["vV"]

    cast_a = AR3[:, 6144:7168].bitcast(F16)
    cast_b = AR3[:, 7168:8192].bitcast(F16)
    e_nat_store = {0: [], 1: []}
    e_cast = []
    tiles = [("qkv", j, wqkv_hbm, s_wqkv) for j in range(16, 24)] + \
            [("fc", j, wfc_hbm, s_wfc) for j in range(8)]
    for idx, (which, j, src, dst) in enumerate(tiles):
        slot = idx % 2
        stg = xstage[:, slot * 1024:(slot + 1) * 1024]
        waits = []
        if idx < 2:
            waits.append(("vV", e_bfc))
            waits.append(("tM", e_lnsum_mm[0]))
        if idx >= 2:
            waits.append(("vV", e_cast[idx - 2]))
        ld = None
        for q4 in range(4):
            ld = b.dma("sync", f"qWN{slot}",
                       mk("dma_start", out=stg[:, q4 * 256:(q4 + 1) * 256],
                          in_=src[j * 128:(j + 1) * 128, q4 * 256:(q4 + 1) * 256]),
                       waits=waits if q4 == 0 else ())
        cst = cast_a if slot == 0 else cast_b
        cw = [(f"qWN{slot}", ld)]
        if idx < 2:
            cw.append(("vV", e_p1_v))
        if idx >= 2:
            cw.append((f"qGN{slot}", e_nat_store[slot][-1]))
        ev = b.op("vector", mk("tensor_copy", cst, stg), inc="vV", waits=cw)
        st = b.dma("gpsimd", f"qGN{slot}", mk("dma_start", out=dst[j * 128:(j + 1) * 128, :], in_=cst),
                   waits=[("vV", ev)])
        if which == "fc" or (which == "qkv" and j >= 16):
            lodst = s_wfc_lo if which == "fc" else s_wv_lo
            jj = j if which == "fc" else j - 16
            ev = b.op("vector", mk("scalar_tensor_tensor", cst, stg, 1.0, cst,
                                   Alu.mult, Alu.subtract), inc="vV",
                      waits=[(f"qGN{slot}", st)])
            st = b.dma("gpsimd", f"qGN{slot}",
                       mk("dma_start", out=lodst[jj * 128:(jj + 1) * 128, :], in_=cst),
                       waits=[("vV", ev)])
        e_cast.append(ev)
        e_nat_store[slot].append(st)
    e_cast_last = e_cast[-1]
    e_qgn_all = [b.cnt["qGN0"], b.cnt["qGN1"]]

    p0_scale(0)
    p0c_tiles(16)
    p0_scale(1)
    p0c_tiles(28)
    p0_scale(2)
    p0c_tiles(28)
    p0c_tiles(24)

    e_qas_all = [b.cnt["qAS0"], b.cnt["qAS1"]]
    e_grps = b.cnt["tM"]
    e_vmm = {}
    emv = None
    for ds in range(8):
        lw = []
        if ds == 0:
            lw += [("qGN0", e_qgn_all[0]), ("qGN1", e_qgn_all[1]), ("vV", e_cast_last),
                   ("vV", e_p1_v)]
        if ds >= 1:
            lw.append(("tM", e_vmm[ds - 1]))
        ldh = b.dma("sync", "qFCT0",
                    mk("dma_start", out=cast_a,
                       in_=s_wqkv[2 * D:3 * D, ds * 128:(ds + 1) * 128], transpose=True),
                    waits=lw)
        ldl = b.dma("sync", "qFCT1",
                    mk("dma_start", out=cast_b,
                       in_=s_wv_lo[:, ds * 128:(ds + 1) * 128], transpose=True),
                    waits=lw if ds == 0 else ())
        mw0 = [("qFCT0", ldh), ("qFCT1", ldl)]
        if ds == 0:
            mw0 += bw(6)
        for sub in range(8):
            first = (ds == 0 and sub == 0)
            b.op("tensor", mk("matmul", pb[6][:, sub:sub + 1],
                              cast_a[:, sub * 128:(sub + 1) * 128], lncol_hi[:, ds:ds + 1],
                              start=first, stop=False, skip_group_check=True),
                 waits=mw0 if sub == 0 else ())
            b.op("tensor", mk("matmul", pb[6][:, sub:sub + 1],
                              cast_a[:, sub * 128:(sub + 1) * 128], lncol_lo[:, ds:ds + 1],
                              start=False, stop=False, skip_group_check=True))
            emv = b.op("tensor", mk("matmul", pb[6][:, sub:sub + 1],
                                    cast_b[:, sub * 128:(sub + 1) * 128], lncol_hi[:, ds:ds + 1],
                                    start=False, stop=False, skip_group_check=True),
                       inc="tM" if sub == 7 else None)
        e_vmm[ds] = emv
    ev6 = b.op("vector", mk("tensor_copy", colsum_c[:, :], pb[6][:, 0:8]),
               inc="vV", waits=[("tM", emv)])
    bank_gate[6] = ("vV", ev6)
    b.op("vector", mk("tensor_copy", colsum_cb[:, :], colsum_c[:, :]), inc="vV")
    b.op("vector", mk("scalar_tensor_tensor", colsum_lo[:, :], colsum_c[:, :], 1.0,
                      colsum_cb[:, :], Alu.mult, Alu.subtract), inc="vV")
    e_cc = b.cnt["vV"]
    e_fcmm = {}
    em3 = None
    for dc in range(8):
        lw = []
        if dc == 0:
            lw += [("qGN0", e_qgn_all[0]), ("qGN1", e_qgn_all[1]), ("vV", e_cast_last),
                   ("tM", e_vmm[7])]
        if dc >= 1:
            lw.append(("tM", e_fcmm[dc - 1]))
        ldf = b.dma("sync", "qFCT0",
                    mk("dma_start", out=cast_a,
                       in_=s_wfc[:, dc * 128:(dc + 1) * 128], transpose=True), waits=lw)
        ldf2 = b.dma("sync", "qFCT1",
                     mk("dma_start", out=cast_b,
                        in_=s_wfc_lo[:, dc * 128:(dc + 1) * 128], transpose=True),
                     waits=lw if dc == 0 else ())
        mw = [("qFCT0", ldf), ("qFCT1", ldf2)]
        if dc == 0:
            mw += [("vV", e_cc)] + bw(2) + bw(3)
        b.op("tensor", mk("matmul", pb[2][0:1, :], colsum_cb[:, dc:dc + 1],
                          cast_a[:, 0:512], start=(dc == 0), stop=False),
             waits=mw)
        b.op("tensor", mk("matmul", pb[2][0:1, :], colsum_lo[:, dc:dc + 1],
                          cast_a[:, 0:512], start=False, stop=False))
        b.op("tensor", mk("matmul", pb[2][0:1, :], colsum_cb[:, dc:dc + 1],
                          cast_b[:, 0:512], start=False, stop=(dc == 7)))
        b.op("tensor", mk("matmul", pb[3][0:1, :], colsum_cb[:, dc:dc + 1],
                          cast_a[:, 512:1024], start=(dc == 0), stop=False))
        b.op("tensor", mk("matmul", pb[3][0:1, :], colsum_lo[:, dc:dc + 1],
                          cast_a[:, 512:1024], start=False, stop=False))
        em3 = b.op("tensor", mk("matmul", pb[3][0:1, :], colsum_cb[:, dc:dc + 1],
                                cast_b[:, 512:1024], start=False, stop=(dc == 7)),
                   inc="tM")
        e_fcmm[dc] = em3
    e_colfc = None
    for eh in range(2):
        b.op("vector", mk("tensor_copy", colfc_b[0:1, eh * 512:(eh + 1) * 512],
                          pb[2 + eh][0:1, :]), inc="vV", waits=[("tM", em3)])
        e_colfc = b.op("vector", mk("scalar_tensor_tensor", colfc_lo[0:1, eh * 512:(eh + 1) * 512],
                                    pb[2 + eh][0:1, :], 1.0,
                                    colfc_b[0:1, eh * 512:(eh + 1) * 512],
                                    Alu.mult, Alu.subtract), inc="vV")
        bank_gate[2 + eh] = ("vV", e_colfc)
    e_p2_tensor = b.cnt["tM"]

    rt_slots = [AR3[:, 5120:7168], AR3[:, 7168:9216]]
    rt_items = ([("gu", mtx, ds, half) for mtx in (0, 1) for ds in range(8)
                 for half in range(2)] +
                [("wd", ic, 0, 0) for ic in range(32)])
    rt_state = {"i": 0, "st": {}}

    def rt_emit(n):
        while rt_state["i"] < len(rt_items) and n > 0:
            i = rt_state["i"]
            kind, a1, a2, a3 = rt_items[i]
            sl = i % 2
            slot = rt_slots[sl]
            lw = []
            if i == 0:
                lw += [("qAS0", e_qas_all[0]), ("qAS1", e_qas_all[1]),
                       ("tM", e_p2_tensor)]
            if i >= 2:
                lw.append((f"qTS{sl}", rt_state["st"][i - 2]))
            if kind == "gu":
                src = s_w3[a1]
                dst = s_wg2t if a1 == 0 else s_wu2t
                ldr = None
                for q in range(2):
                    ldr = b.dma("scalar", f"qTR{sl}",
                                mk("dma_start", out=slot[:, q * 1024:(q + 1) * 1024],
                                   in_=src[a3 * 2048 + q * 1024:a3 * 2048 + (q + 1) * 1024,
                                           a2 * 128:(a2 + 1) * 128],
                                   transpose=True), waits=lw if q == 0 else ())
                st = b.dma("scalar", f"qTS{sl}",
                           mk("dma_start",
                              out=dst[a2 * 128:(a2 + 1) * 128, a3 * 2048:(a3 + 1) * 2048],
                              in_=slot), waits=[(f"qTR{sl}", ldr)])
            else:
                sl16 = slot[:, 0:1024].bitcast(F16)
                ldr = b.dma("scalar", f"qTR{sl}",
                            mk("dma_start", out=sl16,
                               in_=s_w3[2][:, a1 * 128:(a1 + 1) * 128],
                               transpose=True), waits=lw)
                st = b.dma("scalar", f"qTS{sl}",
                           mk("dma_start", out=s_wd2t[a1 * 128:(a1 + 1) * 128, :],
                              in_=sl16), waits=[(f"qTR{sl}", ldr)])
            rt_state["st"][i] = st
            rt_state["i"] += 1
            n -= 1

    gbrow = fcB[0:1, 0:1024]
    e_gamld = b.dma("gpsimd", "qMB", mk("dma_start", out=gbrow, in_=gam_hbm[:, :]),
                    waits=[("vV", e_bfc), ("tM", e_grps), ("vV", e_p1_v)])
    ev = None
    em = None
    for eh in range(2):
        em = b.op("tensor", mk("matmul", pb[7][:, :], ones_rf[:, :],
                               gbrow[0:1, eh * 512:(eh + 1) * 512], start=True, stop=True),
                  inc="tM", waits=[("qMB", e_gamld)] + bw(7))
        b.op("vector", mk("tensor_copy", gam_b[:, eh * 512:(eh + 1) * 512], pb[7][:, :]),
             inc="vV", waits=[("tM", em), ("tM", e_grps)])
        ev = b.op("vector", mk("scalar_tensor_tensor", gam_lo[:, eh * 512:(eh + 1) * 512],
                               pb[7][:, :], 1.0, gam_b[:, eh * 512:(eh + 1) * 512],
                               Alu.mult, Alu.subtract), inc="vV")
        bank_gate[7] = ("vV", ev)
    e_betld = b.dma("gpsimd", "qMB", mk("dma_start", out=gbrow, in_=bet_hbm[:, :]),
                    waits=[("tM", em)])
    for eh in range(2):
        em = b.op("tensor", mk("matmul", pb[7][:, :], ones_rf[:, :],
                               gbrow[0:1, eh * 512:(eh + 1) * 512], start=True, stop=True),
                  inc="tM", waits=[("qMB", e_betld)] + bw(7))
        b.op("vector", mk("tensor_copy", bet_b[:, eh * 512:(eh + 1) * 512], pb[7][:, :]),
             inc="vV", waits=[("tM", em)])
        ev = b.op("vector", mk("scalar_tensor_tensor", bet_lo[:, eh * 512:(eh + 1) * 512],
                               pb[7][:, :], 1.0, bet_b[:, eh * 512:(eh + 1) * 512],
                               Alu.mult, Alu.subtract), inc="vV")
        bank_gate[7] = ("vV", ev)
    e_gambet_v = ev

    fc_bc = fcB[:, :]
    ev_fb = None
    for eh in range(2):
        w0 = [("vV", e_colfc), ("vV", e_bfc), ("vV", e_gambet_v)] + bw(4)
        b.op("tensor", mk("matmul", pb[4][:, :], ones_rb[:, :],
                          colfc_b[0:1, eh * 512:(eh + 1) * 512], start=True, stop=False),
             waits=w0)
        b.op("tensor", mk("matmul", pb[4][:, :], ones_rb[:, :],
                          colfc_lo[0:1, eh * 512:(eh + 1) * 512], start=False, stop=False))
        b.op("tensor", mk("matmul", pb[4][:, :], ones_rb[:, :],
                          bfc_hi[0:1, eh * 512:(eh + 1) * 512], start=False, stop=False))
        em = b.op("tensor", mk("matmul", pb[4][:, :], ones_rb[:, :],
                               bfc_lo[0:1, eh * 512:(eh + 1) * 512], start=False, stop=True),
                  inc="tM")
        ev_fb = b.op("vector", mk("tensor_copy", fc_bc[:, eh * 512:(eh + 1) * 512],
                                  pb[4][:, :]), inc="vV", waits=[("tM", em)])
        bank_gate[4] = ("vV", ev_fb)
    e_fcbc = ev_fb
    if KDBG:
        edf = b.dma("gpsimd", "qMB", mk("dma_start", out=s_dbg[:, 0:1024],
                                        in_=fc_bc), waits=[("vV", e_fcbc)])
        b.wait("vector", "qMB", edf)

    t2scr = AR4[:, 2048:3072]
    sqscr = t2scr
    ln1t = AR4[:, 0:1024]
    hbuf = AR4[:, 1024:2048]
    prev_ep = {}

    for qt in range(16):
        qslot = qt % 2
        xt = xstage[:, qslot * 1024:(qslot + 1) * 1024]
        xw = [("vV", e_cast_last)]
        if qt >= 2:
            xw.append(("vV", prev_ep[qt - 2]["x_done"]))
        ld_x = None
        for q2 in range(2):
            ld_x = b.dma("sync", f"qXT{qslot}",
                         mk("dma_start", out=xt[:, q2 * 512:(q2 + 1) * 512],
                            in_=x_hbm[qt * 128:(qt + 1) * 128, q2 * 512:(q2 + 1) * 512]),
                         waits=xw if q2 == 0 else ())
        lw = [(f"qXT{qslot}", ld_x), ("vV", e_fcbc)]
        ev_ln1t = b.op("vector", mk("tensor_scalar", ln1t, xt, mu1[:, qt:qt + 1], r1c[:, qt:qt + 1],
                                    Alu.subtract, Alu.mult), inc="vV", waits=lw)
        b.op("gpsimd", mk("tensor_tensor", xt, xt, bet_b[:, :], Alu.add), inc="gP",
             waits=[("vV", ev_ln1t)])
        eg_x4 = b.op("gpsimd", mk("tensor_tensor", xt, xt, bet_lo[:, :], Alu.add), inc="gP")
        hw = []
        if qt >= 1:
            hw.append(("qGX1", 16 * qt))
        b.op("vector", mk("tensor_add", hbuf, ln1t, fc_bc), inc="vV", waits=hw)
        b.op("vector", mk("reduce_sum", svec[:, 8:9], hbuf, AX), inc="vV")
        b.op("vector", mk("scalar_tensor_tensor", sqscr, hbuf, 1.0, hbuf,
                          Alu.mult, Alu.mult, accum_out=svec[:, 9:10]), inc="vV")
        b.op("vector", mk("tensor_single_scalar", svec[:, 10:11], svec[:, 8:9], 1.0 / D, Alu.mult), inc="vV")
        b.op("vector", mk("tensor_single_scalar", svec[:, 11:12], svec[:, 9:10], 1.0 / D, Alu.mult), inc="vV")
        b.op("vector", mk("scalar_tensor_tensor", svec[:, 12:13], svec[:, 10:11], 1.0,
                          svec[:, 10:11], Alu.mult, Alu.mult), inc="vV")
        b.op("vector", mk("tensor_sub", svec[:, 12:13], svec[:, 11:12], svec[:, 12:13]), inc="vV")
        b.op("vector", mk("tensor_single_scalar", svec[:, 12:13], svec[:, 12:13], LN_EPS, Alu.add), inc="vV")
        ev_r = b.op("vector", mk("reciprocal", svec[:, 13:14], svec[:, 12:13]), inc="vV")
        es_s = b.op("scalar", mk("activation", svec[:, 14:15], svec[:, 13:14], Act.Sqrt),
                    inc="sC", waits=[("vV", ev_r)])
        first = True
        for _ in range(2):
            b.op("vector", mk("scalar_tensor_tensor", svec[:, 15:16], svec[:, 14:15], 1.0,
                              svec[:, 14:15], Alu.mult, Alu.mult), inc="vV",
                 waits=[("sC", es_s)] if first else ())
            first = False
            b.op("vector", mk("tensor_mul", svec[:, 15:16], svec[:, 15:16], svec[:, 12:13]), inc="vV")
            b.op("vector", mk("tensor_scalar", svec[:, 15:16], svec[:, 15:16], -0.5, 1.5,
                              Alu.mult, Alu.add), inc="vV")
            b.op("vector", mk("tensor_mul", svec[:, 14:15], svec[:, 14:15], svec[:, 15:16]), inc="vV")
        ev_t2 = b.op("vector", mk("tensor_scalar", t2scr, hbuf, svec[:, 10:11], svec[:, 14:15],
                                  Alu.subtract, Alu.mult), inc="vV")
        t2b = AR4[:, 3072:4096]
        b.op("gpsimd", mk("tensor_mul", t2b, t2scr, gam_lo[:, :]), inc="gP",
             waits=[("vV", ev_t2)])
        eg_t3 = b.op("gpsimd", mk("tensor_mul", t2scr, t2scr, gam_b[:, :]), inc="gP")
        b.op("vector", mk("tensor_add", hbuf, t2scr, t2b), inc="vV",
             waits=[("gP", eg_t3), ("gP", eg_x4)])
        ev_x1 = b.op("vector", mk("tensor_add", hbuf, hbuf, xt), inc="vV")
        b.dma("gpsimd", "qGX1", mk("dma_start", out=s_x1[qt * 128:(qt + 1) * 128, :], in_=hbuf),
              waits=[("vV", ev_x1)])
        prev_ep[qt] = {"x_done": ev_x1}
        rt_emit(4)

    rt_emit(64)
    e_rt_wgwu = [("qTS0", rt_state["st"][30]), ("qTS1", rt_state["st"][31])]
    e_rt_all = [("qTS0", rt_state["st"][62]), ("qTS1", rt_state["st"][63])]
    e_p3_tensor = b.cnt["tM"]
    e_p3_v = b.cnt["vV"]
    e_p3_g = b.cnt["gP"]
    e_p3_x1st = b.cnt["qGX1"]

    ln2s = wstg[:, 0:1024]
    ybuf1 = wstg[:, 1024:2048]
    ybuf = AR3[:, 0:1024].bitcast(F16)
    r0b = qi[:, :]
    g1b = AR3[:, 1024:2048].bitcast(F16)
    g2b = AR3[:, 2048:3072].bitcast(F16)
    g3b = AR3[:, 3072:4096].bitcast(F16)
    q2sp = AR3[:, 4096:5120].bitcast(F16)
    q2sn = AR3[:, 5120:6144].bitcast(F16)
    xqo = AR3[:, 8192:9216]
    xq1T = [AR2[:, 0:4096], AR2[:, 4096:8192]]
    wgb = [AR2[:, 8192:12288], AR2[:, 12288:16384]]
    wub = [AR2[:, 16384:20480], AR2[:, 20480:24576]]
    wdb = [AR2[:, 24576:25600].bitcast(F16), AR2[:, 25600:26624].bitcast(F16)]
    sgs = AR2[:, 26624:27136].bitcast(F16)
    uss = AR2[:, 27136:27648].bitcast(F16)
    x1t = [AR4[:, 0:1024], AR4[:, 1024:2048]]
    x2t = [AR4[:, 2048:3072], AR4[:, 3072:4096]]

    e_q1 = {}
    for tt in range(16):
        slot = tt % 2
        xi = x1t[slot]
        w0 = [("qGX1", 16 * (tt + 1))]
        if tt < 2:
            w0.append(("qGX1", e_p3_x1st))
        if tt == 0:
            w0 += [("vV", e_p3_v), ("gP", e_p3_g), ("tM", e_p2_tensor),
                   ("vV", p0c_state["signB"][95])]
        if tt >= 2:
            w0.append(("vV", e_q1[tt - 2]))
        ld = None
        for q2 in range(2):
            ld = b.dma("sync", f"qX1{slot}",
                       mk("dma_start", out=xi[:, q2 * 512:(q2 + 1) * 512],
                          in_=s_x1[tt * 128:(tt + 1) * 128, q2 * 512:(q2 + 1) * 512]),
                       waits=w0 if q2 == 0 else ())
        sw = [(f"qX1{slot}", ld)]
        b.op("vector", mk("reduce_sum", svec[:, 0:1], xi, AX), inc="vV", waits=sw)
        b.op("vector", mk("scalar_tensor_tensor", ln2s, xi, 1.0, xi, Alu.mult, Alu.mult,
                          accum_out=svec[:, 1:2]), inc="vV")
        b.op("vector", mk("tensor_single_scalar", svec[:, 2:3], svec[:, 0:1], 1.0 / D, Alu.mult), inc="vV")
        b.op("vector", mk("tensor_single_scalar", svec[:, 3:4], svec[:, 1:2], 1.0 / D, Alu.mult), inc="vV")
        b.op("vector", mk("scalar_tensor_tensor", svec[:, 4:5], svec[:, 2:3], 1.0, svec[:, 2:3],
                          Alu.mult, Alu.mult), inc="vV")
        b.op("vector", mk("tensor_sub", svec[:, 4:5], svec[:, 3:4], svec[:, 4:5]), inc="vV")
        b.op("vector", mk("tensor_single_scalar", svec[:, 4:5], svec[:, 4:5], LN_EPS, Alu.add), inc="vV")
        ev_r = b.op("vector", mk("reciprocal", svec[:, 5:6], svec[:, 4:5]), inc="vV")
        es_s = b.op("scalar", mk("activation", svec[:, 6:7], svec[:, 5:6], Act.Sqrt),
                    inc="sC", waits=[("vV", ev_r)])
        first = True
        for _ in range(2):
            b.op("vector", mk("scalar_tensor_tensor", svec[:, 7:8], svec[:, 6:7], 1.0,
                              svec[:, 6:7], Alu.mult, Alu.mult), inc="vV",
                 waits=[("sC", es_s)] if first else ())
            first = False
            b.op("vector", mk("tensor_mul", svec[:, 7:8], svec[:, 7:8], svec[:, 4:5]), inc="vV")
            b.op("vector", mk("tensor_scalar", svec[:, 7:8], svec[:, 7:8], -0.5, 1.5,
                              Alu.mult, Alu.add), inc="vV")
            b.op("vector", mk("tensor_mul", svec[:, 6:7], svec[:, 6:7], svec[:, 7:8]), inc="vV")
        b.op("vector", mk("tensor_scalar", ln2s, xi, svec[:, 2:3], svec[:, 6:7],
                          Alu.subtract, Alu.mult), inc="vV")
        b.op("vector", mk("reduce_max", svec[:, 8:9], ln2s, AX, True), inc="vV")
        b.op("vector", mk("tensor_scalar_max", svec[:, 8:9], svec[:, 8:9], 1e-5), inc="vV")
        b.op("vector", mk("reciprocal", svec[:, 9:10], svec[:, 8:9]), inc="vV")
        b.op("vector", mk("tensor_single_scalar", s1c[:, tt:tt + 1], svec[:, 9:10], 127.0, Alu.mult), inc="vV")
        b.op("vector", mk("tensor_single_scalar", svec[:, 10:11], svec[:, 8:9], 1.0 / 127.0, Alu.mult), inc="vV")
        b.op("vector", mk("tensor_mul", a1g[:, tt:tt + 1], svec[:, 10:11], tcols[:, 3:4]), inc="vV",
             waits=[("vV", scale_ready[0]), ("vV", scale_ready[1])])
        ev_sc = b.op("vector", mk("tensor_mul", a1u[:, tt:tt + 1], svec[:, 10:11], tcols[:, 4:5]), inc="vV")
        es_y = b.op("scalar", mk("activation", ybuf1, ln2s, Act.Copy, scale=s1c[:, tt:tt + 1]),
                    inc="sC", waits=[("vV", ev_sc)])
        if KDBG and tt == 0:
            ed1 = b.dma("gpsimd", "qMB", mk("dma_start", out=s_dbg[:, 0:1024], in_=ybuf1),
                        waits=[("sC", es_y)])
            ed2 = b.dma("gpsimd", "qMB", mk("dma_start", out=s_dbg[:, 1024:2048], in_=ln2s))
            b.wait("vector", "qMB", ed2)
        e_r0 = b.op("vector", mk("tensor_copy", r0b, ybuf1), inc="vV", waits=[("sC", es_y)])
        e_g1a = b.op("vector", mk("scalar_tensor_tensor", g1b, ybuf1, -0.5, r0b, Alu.add, Alu.is_gt), inc="vV")
        e_g2a = b.op("vector", mk("scalar_tensor_tensor", g2b, ybuf1, 0.5, r0b, Alu.add, Alu.is_ge), inc="vV")
        if KDBG and tt == 0:
            ed3 = b.dma("gpsimd", "qMB", mk("dma_start", out=s_dbg[:, 2048:3072].bitcast(I32), in_=r0b),
                        waits=[("vV", e_g2a)])
            ed4 = b.dma("gpsimd", "qMB", mk("dma_start", out=s_dbg[:, 3072:4096], in_=g1b))
            ed5 = b.dma("gpsimd", "qMB", mk("dma_start", out=s_dbg[:, 4096:5120], in_=g2b))
            b.wait("vector", "qMB", ed5)
        b.op("vector", mk("scalar_tensor_tensor", g1b, g1b, -1.0, g2b, Alu.add, Alu.add), inc="vV")
        qw2 = [("qGQ1", 16 * tt)] if tt >= 1 else list(e_rt_all)
        ev_q = b.op("vector", mk("scalar_tensor_tensor", xqo, g1b, 1.0, r0b, Alu.mult, Alu.add),
                    inc="vV", waits=qw2)
        e_q1[tt] = ev_q
        b.dma("gpsimd", "qGQ1", mk("dma_start", out=s_xq1[tt * 128:(tt + 1) * 128, :], in_=xqo),
              waits=[("vV", ev_q)])
    e_q1_last = e_q1[15]

    AR1h = [AR1[:, 0:16384], AR1[:, 16384:32768]]

    def prod_ap(k, tl, dc):
        return AR1h[k % 2][:, tl * 4096 + dc * 512: tl * 4096 + (dc + 1) * 512].bitcast(F16)

    def prod_tt(k, tl):
        return AR1h[k % 2][:, tl * 4096:(tl + 1) * 4096].bitcast(F16)

    def xq2T_ap(k, ic):
        return AR1h[k % 2][:, ic * 512:(ic + 1) * 512].bitcast(F16)

    e_xq1T = {}
    e_gu_done = {}
    e_q2_done = {}
    e_xq2T = {}
    e_down_mm = {}
    e_out_store = {}
    e_prev_wchunk = {}
    e_prev_prod = [None]
    e_prev_x2 = {}
    e_wd_prev = {}

    def emit_xq1T(k):
        sl = k % 2
        w = [("qGQ1", 16 * (4 * k + 4))]
        if k >= 2:
            w.append(("tM", e_gu_done[k - 2]))
        if k == 0:
            w.append(("tM", e_p3_tensor))
            w += e_rt_all
        last = None
        for ds in range(8):
            last = b.dma("sync", f"qQ1T{sl}",
                         mk("dma_start", out=xq1T[sl][:, ds * 512:(ds + 1) * 512],
                            in_=s_xq1[k * 512:(k + 1) * 512, ds * 128:(ds + 1) * 128],
                            transpose=True), waits=w if ds == 0 else ())
        e_xq1T[k] = last

    def emit_gate_up(k):
        sl = k % 2
        last_mm = None
        e_am_init = {}
        for tl in range(4):
            aw = []
            if k >= 2:
                aw.append(("vV", e_q2_done[k - 2]))
            e_am_init[tl] = b.op("vector",
                                 mk("memset", am2[:, 4 * (k % 2) + tl: 4 * (k % 2) + tl + 1], 0.0),
                                 inc="vV", waits=aw)
        for dc in range(8):
            wsl = dc % 2
            wload_w = []
            if k == 0 and dc <= 1:
                wload_w += e_rt_wgwu
            if k == 0 and dc == 0:
                wload_w.append(("tM", e_p3_tensor))
            prev_key = (k, dc - 2) if dc >= 2 else ((k - 1, 6 + dc) if k > 0 else None)
            if prev_key is not None:
                wload_w.append(("tM", e_prev_wchunk.get(prev_key)))
            lg = None
            for ds in range(8):
                lg = b.dma("sync", f"qWG{wsl}",
                           mk("dma_start", out=wgb[wsl][:, ds * 512:(ds + 1) * 512],
                              in_=s_wg2t[ds * 128:(ds + 1) * 128, dc * 512:(dc + 1) * 512]),
                           waits=wload_w if ds == 0 else ())
            lu = None
            for ds in range(8):
                lu = b.dma("sync", f"qWU{wsl}",
                           mk("dma_start", out=wub[wsl][:, ds * 512:(ds + 1) * 512],
                              in_=s_wu2t[ds * 128:(ds + 1) * 128, dc * 512:(dc + 1) * 512]),
                           waits=wload_w if ds == 0 else ())
            for tl in range(4):
                tt = 4 * k + tl
                bg = (2 * tl) % 4
                bu = (2 * tl + 1) % 4
                w0 = [(f"qWG{wsl}", lg), (f"qQ1T{sl}", e_xq1T[k])] + bw(bg)
                emg = None
                for ds in range(8):
                    lhs = xq1T[sl][:, ds * 512 + tl * 128: ds * 512 + (tl + 1) * 128]
                    rhs = wgb[wsl][:, ds * 512:(ds + 1) * 512]
                    emg = b.op("tensor", mk("matmul", pb[bg][:, :], lhs, rhs,
                                            start=(ds == 0), stop=(ds == 7)),
                               inc="tM" if ds == 7 else None, waits=w0 if ds == 0 else ())
                sgw = [("tM", emg)]
                if e_prev_prod[0] is not None:
                    sgw.append(("vV", e_prev_prod[0]))
                esg = b.op("scalar", mk("activation", sgs, pb[bg][:, :], Act.Silu,
                                        scale=a1g[:, tt:tt + 1]), inc="sC", waits=sgw)
                bank_gate[bg] = ("sC", esg)
                w1 = [(f"qWU{wsl}", lu)] + bw(bu)
                emu = None
                for ds in range(8):
                    lhs = xq1T[sl][:, ds * 512 + tl * 128: ds * 512 + (tl + 1) * 128]
                    rhs = wub[wsl][:, ds * 512:(ds + 1) * 512]
                    emu = b.op("tensor", mk("matmul", pb[bu][:, :], lhs, rhs,
                                            start=(ds == 0), stop=(ds == 7)),
                               inc="tM" if ds == 7 else None, waits=w1 if ds == 0 else ())
                last_mm = emu
                esu = b.op("scalar", mk("activation", uss, pb[bu][:, :], Act.Copy,
                                        scale=a1u[:, tt:tt + 1]), inc="sC", waits=[("tM", emu)])
                bank_gate[bu] = ("sC", esu)
                pw = [("sC", esu)]
                if k >= 2 and dc == 0 and tl == 0:
                    pw.append(("qGX2", 16 * 4 * (k - 1)))
                ev_p = b.op("vector", mk("tensor_mul", prod_ap(k, tl, dc), sgs, uss),
                            inc="vV", waits=pw)
                e_prev_prod[0] = ev_p
                b.op("vector", mk("reduce_max", svec[:, 26:27], prod_ap(k, tl, dc), AX, True),
                     inc="vV", waits=[("vV", e_am_init[tl])])
                b.op("vector", mk("tensor_max", am2[:, 4 * (k % 2) + tl: 4 * (k % 2) + tl + 1],
                                  am2[:, 4 * (k % 2) + tl: 4 * (k % 2) + tl + 1], svec[:, 26:27]),
                     inc="vV")
            e_prev_wchunk[(k, dc)] = last_mm
        e_gu_done[k] = last_mm

    q2_st_prev = [None]

    def emit_quant2(k):
        last_v = None
        for tl in range(4):
            tt = 4 * k + tl
            amc = am2[:, 4 * (k % 2) + tl: 4 * (k % 2) + tl + 1]
            b.op("vector", mk("tensor_scalar_max", svec[:, 16:17], amc, 1e-5), inc="vV")
            b.op("vector", mk("reciprocal", svec[:, 17:18], svec[:, 16:17]), inc="vV")
            b.op("vector", mk("tensor_single_scalar", s2c[:, tt:tt + 1], svec[:, 17:18], 127.0,
                              Alu.mult), inc="vV")
            b.op("vector", mk("tensor_single_scalar", svec[:, 18:19], svec[:, 16:17], 1.0 / 127.0,
                              Alu.mult), inc="vV")
            ev_a2 = b.op("vector", mk("tensor_mul", a2d[:, tt:tt + 1], svec[:, 18:19], tcols[:, 5:6]),
                         inc="vV", waits=[("vV", scale_ready[2])])
            prev_g2 = None
            for sub in range(4):
                srcp = prod_tt(k, tl)[:, sub * 1024:(sub + 1) * 1024]
                yw = [("vV", ev_a2)]
                if prev_g2 is not None:
                    yw.append(("vV", prev_g2))
                es_y = b.op("scalar", mk("activation", ybuf, srcp, Act.Copy,
                                         scale=s2c[:, tt:tt + 1]), inc="sC", waits=yw)
                if KDBG:
                    edy = b.dma("gpsimd", "qMB",
                                mk("dma_start",
                                   out=s_dby[(4 * k + tl) * 128:(4 * k + tl + 1) * 128,
                                             sub * 1024:(sub + 1) * 1024],
                                   in_=ybuf), waits=[("sC", es_y)])
                    b.wait("scalar", "qMB", edy)
                    b.wait("vector", "qMB", edy)
                b.op("vector", mk("tensor_copy", r0b, ybuf), inc="vV", waits=[("sC", es_y)])
                b.op("vector", mk("tensor_copy", g3b, r0b), inc="vV")
                prev_g2 = b.op("vector", mk("tensor_sub", g1b, ybuf, g3b), inc="vV")
                b.op("vector", mk("tensor_single_scalar", g2b, g1b, 0.5, Alu.is_gt), inc="vV")
                b.op("vector", mk("tensor_single_scalar", q2sp, g1b, 1.5, Alu.is_gt), inc="vV")
                b.op("vector", mk("tensor_add", g2b, g2b, q2sp), inc="vV")
                b.op("vector", mk("tensor_single_scalar", q2sp, g1b, -0.5, Alu.is_ge), inc="vV")
                b.op("vector", mk("tensor_single_scalar", q2sn, g1b, -1.5, Alu.is_ge), inc="vV")
                b.op("vector", mk("tensor_add", q2sp, q2sp, q2sn), inc="vV")
                b.op("vector", mk("tensor_add", g2b, g2b, q2sp), inc="vV")
                b.op("vector", mk("tensor_single_scalar", g2b, g2b, -2.0, Alu.add), inc="vV")
                last_v = b.op("vector", mk("tensor_add", srcp, g2b, g3b), inc="vV")
            sw_ = [("vV", last_v)]
            if q2_st_prev[0] is not None:
                sw_.append(("qGX2", q2_st_prev[0]))
            q2_st_prev[0] = b.dma("gpsimd", "qGX2",
                                  mk("dma_start", out=s_xq2[tt * 128:(tt + 1) * 128, :],
                                     in_=prod_tt(k, tl)),
                                  waits=sw_)
        e_q2_done[k] = last_v

    xqTb = [AR3[:, 6144 + s * 512:6656 + s * 512].bitcast(F16) for s in range(4)]
    xqT_sem = ["qQ2T0", "qQ2T1", "qQT0", "qQT1"]

    def emit_down(k):
        e_wd = {}
        for ic in range(32):
            wsl = ic % 2
            ts_ = ic % 4
            tw = []
            if ic == 0:
                tw.append(("qGX2", 16 * 4 * (k + 1)))
            if ic >= 4:
                tw.append(("tM", e_wd[ic - 4]))
            elif k > 0:
                tw.append(("tM", e_wd_prev.get(28 + ic)))
            ld_t = b.dma("sync", xqT_sem[ts_],
                         mk("dma_start", out=xqTb[ts_],
                            in_=s_xq2[k * 512:(k + 1) * 512, ic * 128:(ic + 1) * 128],
                            transpose=True), waits=tw)
            w = []
            if k == 0 and ic <= 1:
                w += e_rt_all
            if ic >= 2:
                w.append(("tM", e_wd[ic - 2]))
            elif k > 0:
                w.append(("tM", e_wd_prev.get(30 + ic)))
            ld = b.dma("sync", f"qWD{wsl}",
                       mk("dma_start", out=wdb[wsl],
                          in_=s_wd2t[ic * 128:(ic + 1) * 128, :]),
                       waits=w)
            em = None
            for tl in range(4):
                for eh in range(2):
                    bank = tl * 2 + eh
                    w0 = [(f"qWD{wsl}", ld), (xqT_sem[ts_], ld_t)] + (bw(bank) if ic == 0 else [])
                    em = b.op("tensor",
                              mk("matmul", pb[bank][:, :],
                                 xqTb[ts_][:, tl * 128:(tl + 1) * 128],
                                 wdb[wsl][:, eh * 512:(eh + 1) * 512],
                                 start=(ic == 0), stop=(ic == 31)),
                              inc="tM" if (tl == 3 and eh == 1) else None,
                              waits=w0)
            e_wd[ic] = em
        e_down_mm[k] = e_wd[31]
        e_wd_prev.clear()
        e_wd_prev.update(e_wd)
        for tl in range(4):
            tt = 4 * k + tl
            slot = tt % 2
            xi = x1t[slot]
            lw = [("vV", e_q1_last)]
            if tt >= 2:
                lw.append(("vV", e_prev_x2.get(tt - 2)))
            ld = None
            for q2 in range(2):
                ld = b.dma("sync", f"qX1{slot}",
                           mk("dma_start", out=xi[:, q2 * 512:(q2 + 1) * 512],
                              in_=s_x1[tt * 128:(tt + 1) * 128, q2 * 512:(q2 + 1) * 512]),
                           waits=lw if q2 == 0 else ())
            ev = None
            for eh in range(2):
                bank = tl * 2 + eh
                xw = [("tM", e_down_mm[k]), (f"qX1{slot}", ld)]
                if tt < 2:
                    xw += [("gP", e_p3_g), ("vV", e_p3_v)]
                if tt >= 2:
                    xw.append((f"qGO{slot}", e_out_store.get(tt - 2)))
                ev = b.op("vector", mk("scalar_tensor_tensor", x2t[slot][:, eh * 512:(eh + 1) * 512],
                                       pb[bank][:, :], a2d[:, tt:tt + 1],
                                       xi[:, eh * 512:(eh + 1) * 512], Alu.mult, Alu.add),
                          inc="vV", waits=xw)
                bank_gate[bank] = ("vV", ev)
            e_prev_x2[tt] = ev
            st = b.dma("gpsimd", f"qGO{slot}",
                       mk("dma_start", out=out_hbm[tt * 128:(tt + 1) * 128, :], in_=x2t[slot]),
                       waits=[("vV", ev)])
            e_out_store[tt] = st

    emit_xq1T(0)
    emit_gate_up(0)
    emit_quant2(0)
    emit_xq1T(1)
    emit_gate_up(1)
    emit_quant2(1)
    emit_down(0)
    emit_xq1T(2)
    emit_gate_up(2)
    emit_quant2(2)
    emit_down(1)
    emit_xq1T(3)
    emit_gate_up(3)
    emit_quant2(3)
    emit_down(2)
    emit_down(3)

    b.wait("scalar", "qGO0", b.cnt["qGO0"])
    b.wait("scalar", "qGO1", b.cnt["qGO1"])
    b.wait("sync", "qGO0", b.cnt["qGO0"])
    b.wait("sync", "qGO1", b.cnt["qGO1"])

    for name in SEMS:
        b.sems[name] = stack.enter_context(nc.semaphore(name))
    with nc.Block() as block:
        @block.sync
        def _(E):
            for th in b.ops["sync"]:
                th(E)

        @block.tensor
        def _(E):
            for th in b.ops["tensor"]:
                th(E)

        @block.vector
        def _(E):
            for th in b.ops["vector"]:
                th(E)

        @block.scalar
        def _(E):
            for th in b.ops["scalar"]:
                th(E)

        @block.gpsimd
        def _(E):
            for th in b.ops["gpsimd"]:
                th(E)

    stack.close()
    return nc


_NC_CACHE = None


def kernel(L0_attn_x, L0_attn_w_qkv, L0_attn_w_fc, L0_attn_b_fc, L0_attn_gamma,
           L0_attn_beta, L0_ffn_w_gate, L0_ffn_w_up, L0_ffn_w_down):
    global _NC_CACHE
    x = np.ascontiguousarray(np.asarray(L0_attn_x, dtype=np.float32))
    B_, S_, D_ = x.shape
    weights = {
        "w_qkv": np.ascontiguousarray(np.asarray(L0_attn_w_qkv, dtype=np.float32)),
        "w_fc": np.ascontiguousarray(np.asarray(L0_attn_w_fc, dtype=np.float32)),
        "b_fc": np.ascontiguousarray(np.asarray(L0_attn_b_fc, dtype=np.float32)).reshape(1, D),
        "gamma": np.ascontiguousarray(np.asarray(L0_attn_gamma, dtype=np.float32)).reshape(1, D),
        "beta": np.ascontiguousarray(np.asarray(L0_attn_beta, dtype=np.float32)).reshape(1, D),
        "w_gate": np.ascontiguousarray(np.asarray(L0_ffn_w_gate, dtype=np.float32)),
        "w_up": np.ascontiguousarray(np.asarray(L0_ffn_w_up, dtype=np.float32)),
        "w_down": np.ascontiguousarray(np.asarray(L0_ffn_w_down, dtype=np.float32)),
    }
    in_maps = []
    for c in range(8):
        bi, h = divmod(c, 2)
        own = x[bi, h * T:(h + 1) * T, :]
        oth = x[bi, (1 - h) * T:(2 - h) * T, :]
        xc = np.ascontiguousarray(np.concatenate([own, oth], axis=0))
        m = {"x_full": xc}
        m.update(weights)
        in_maps.append(m)

    if _NC_CACHE is None:
        nc_ = bass.Bass("TRN2", target_bir_lowering=False)
        build_program(nc_)
        _NC_CACHE = nc_
    res = run_bass_kernel_spmd(_NC_CACHE, in_maps, core_ids=list(range(8)))
    out = np.empty_like(x)
    for c in range(8):
        bi, h = divmod(c, 2)
        out[bi, h * T:(h + 1) * T, :] = res.results[c]["out"]
    return out


# version 23
# speedup vs baseline: 1.1767x; 1.0009x over previous
import numpy as np
import ml_dtypes
import concourse.bass as bass
import concourse.mybir as mybir
from concourse.bass_utils import run_bass_kernel_spmd
from contextlib import ExitStack

BFD = ml_dtypes.bfloat16
F32 = mybir.dt.float32
BF16 = mybir.dt.bfloat16
F16 = mybir.dt.float16
I32 = mybir.dt.int32
Alu = mybir.AluOpType
Act = mybir.ActivationFunctionType
AX = mybir.AxisListType.X

S, T, D, NE, NI, P = 4096, 2048, 1024, 3072, 4096, 128
KDBG = False
Q2ENG = "vector"
LN_EPS = 1e-5
INV32 = 1.0 / 32.0

ENGINES = ("sync", "tensor", "vector", "scalar", "gpsimd")
SERIAL = ("vector", "scalar", "gpsimd")

SEMS = (["tM", "vV", "sC", "gP", "sP0", "gC",
         "qM", "qMB", "qP1X", "qLN1T", "qGX1", "qGQ1", "qGX2", "qTR0", "qTR1"] +
        [f"{p}{i}" for p in ("qFCT", "qWN", "qWT", "qQT", "qXT", "qX1", "qQ1T", "qWG",
                             "qWU", "qQ2T", "qWD", "qA", "qAS", "qGLN", "qGQs",
                             "qGN", "qGO", "qTS") for i in (0, 1)])


def mk(meth, *args, **kw):
    def f(E):
        return getattr(E, meth)(*args, **kw)
    return f


class Builder:
    def __init__(self, sem_names):
        self.ops = {e: [] for e in ENGINES}
        self.cnt = {s: 0 for s in sem_names}
        self.sems = {}
        self.serial = {e: None for e in SERIAL}

    def op(self, eng, fn, inc=None, waits=(), n=1, is_dma=False):
        if inc is not None:
            self.cnt[inc] += n
        val = self.cnt[inc] if inc is not None else None
        waits = list(waits)
        if eng in SERIAL and not is_dma:
            m = self.serial.get(eng)
            if m:
                waits.append(m)
            if inc is not None:
                self.serial[eng] = (inc, val)
        waits = [(s, v) for (s, v) in waits if v is not None and v > 0]
        sems = self.sems
        inc_pair = (inc, n) if inc is not None else None

        def thunk(E):
            for s, v in waits:
                E.wait_ge(sems[s], v)
            ins = fn(E)
            if inc_pair is not None:
                ins.then_inc(sems[inc_pair[0]], inc_pair[1])
        self.ops[eng].append(thunk)
        return val

    def splice(self, eng, thunk, sem, val, pre_cell):
        if eng in SERIAL:
            pre_cell[0] = self.serial.get(eng)
            self.serial[eng] = (sem, val)
        self.ops[eng].append(thunk)

    def dma(self, eng, q, fn, waits=()):
        return self.op(eng, fn, inc=q, waits=waits, n=16, is_dma=True)

    def wait(self, eng, sem, val):
        if val is None or val <= 0:
            return
        sems = self.sems

        def thunk(E):
            E.wait_ge(sems[sem], val)
        self.ops[eng].append(thunk)


def build_program(nc):
    b = Builder(SEMS)

    x_hbm = nc.dram_tensor("x_full", [S, D], F32, kind="ExternalInput").ap()
    wqkv_hbm = nc.dram_tensor("w_qkv", [NE, D], F32, kind="ExternalInput").ap()
    wfc_hbm = nc.dram_tensor("w_fc", [D, D], F32, kind="ExternalInput").ap()
    bfc_hbm = nc.dram_tensor("b_fc", [1, D], F32, kind="ExternalInput").ap()
    gam_hbm = nc.dram_tensor("gamma", [1, D], F32, kind="ExternalInput").ap()
    bet_hbm = nc.dram_tensor("beta", [1, D], F32, kind="ExternalInput").ap()
    w3_hbm = [nc.dram_tensor("w_gate", [NI, D], F32, kind="ExternalInput").ap(),
              nc.dram_tensor("w_up", [NI, D], F32, kind="ExternalInput").ap(),
              nc.dram_tensor("w_down", [D, NI], F32, kind="ExternalInput").ap()]
    out_hbm = nc.dram_tensor("out", [T, D], F32, kind="ExternalOutput").ap()

    s_ln1 = nc.dram_tensor("s_ln1", [S, D], F16, kind="Internal").ap()
    s_q = nc.dram_tensor("s_q", [D, T], F16, kind="Internal").ap()
    s_wqkv = nc.dram_tensor("s_wqkv", [NE, D], F16, kind="Internal").ap()
    s_wv_lo = nc.dram_tensor("s_wv_lo", [D, D], F16, kind="Internal").ap()
    s_wfc = nc.dram_tensor("s_wfc", [D, D], F16, kind="Internal").ap()
    s_wfc_lo = nc.dram_tensor("s_wfc_lo", [D, D], F16, kind="Internal").ap()
    s_w3 = [nc.dram_tensor("s_wg2", [NI, D], BF16, kind="Internal").ap(),
            nc.dram_tensor("s_wu2", [NI, D], BF16, kind="Internal").ap(),
            nc.dram_tensor("s_wd2", [D, NI], F16, kind="Internal").ap()]
    s_wg2t = nc.dram_tensor("s_wg2t", [D, NI], BF16, kind="Internal").ap()
    s_wu2t = nc.dram_tensor("s_wu2t", [D, NI], BF16, kind="Internal").ap()
    s_wd2t = nc.dram_tensor("s_wd2t", [NI, D], F16, kind="Internal").ap()
    s_x1 = nc.dram_tensor("s_x1", [T, D], F32, kind="Internal").ap()
    s_xq1 = nc.dram_tensor("s_xq1", [T, D], BF16, kind="Internal").ap()
    s_xq2 = nc.dram_tensor("s_xq2", [T, NI], F16, kind="Internal").ap()
    s_dbg = nc.dram_tensor("s_dbg", [P, 5120], F32, kind="Internal").ap() if KDBG else None
    s_dby = nc.dram_tensor("s_dby", [T, NI], F16, kind="Internal").ap() if KDBG else None

    stack = ExitStack()
    sb = lambda name, shape, dt: stack.enter_context(nc.sbuf_tensor(name, shape, dt))
    psa = lambda name: stack.enter_context(nc.psum_tensor(name, [P, 512], F32))

    AR1 = sb("AR1", [P, 32768], BF16)
    AR2 = sb("AR2", [P, 32768], BF16)
    AR3 = sb("AR3", [P, 9216], BF16)
    AR4 = sb("AR4", [P, 4096], F32)

    ident = sb("ident", [P, P], F16)
    iota_a = sb("iota_a", [P, P], F32)
    iota_b = sb("iota_b", [P, P], F32)
    ones_rb = sb("ones_rb", [1, P], F16)
    ones_rf = sb("ones_rf", [1, P], F32)
    ones_cf = sb("ones_cf", [P, 1], F32)
    bfc_hi = sb("bfc_hi", [1, D], F16)
    bfc_lo = sb("bfc_lo", [1, D], F16)
    colfc_b = sb("colfc_b", [1, D], F16)
    colfc_lo = sb("colfc_lo", [1, D], F16)
    colsum_c = sb("colsum_c", [P, 8], F32)
    lnsum_c = sb("lnsum_c", [P, 8], F32)
    lncol_hi = sb("lncol_hi", [P, 8], F16)
    lncol_lo = sb("lncol_lo", [P, 8], F16)
    colsum_cb = sb("colsum_cb", [P, 8], F16)
    colsum_lo = sb("colsum_lo", [P, 8], F16)
    mu1 = sb("mu1", [P, 16], F32)
    r1c = sb("r1c", [P, 16], F32)
    mur2 = sb("mur2", [P, 32], F32)
    musq = sb("musq", [P, 16], F32)
    sqs = sb("sqs", [P, 16], F32)
    a1g = sb("a1g", [P, 16], F32)
    a1u = sb("a1u", [P, 16], F32)
    a2d = sb("a2d", [P, 16], F32)
    s1c = sb("s1c", [P, 16], F32)
    s2c = sb("s2c", [P, 16], F32)
    am2 = sb("am2", [P, 8], F32)
    svec = sb("svec", [P, 32], F32)
    wacc = sb("wacc", [P, 4], F32)
    tcols = sb("tcols", [P, 12], F32)
    rowt = sb("rowt", [1, 16], F32)
    rs_row = sb("rs_row", [1, P], F16)
    rs_col = sb("rs_col", [P, 1], F16)
    rs_row2 = sb("rs_row2", [1, P], F16)
    rs_col2 = sb("rs_col2", [P, 1], F16)
    rs_f = sb("rs_f", [P, 1], F32)
    xstage = sb("xstage", [P, 2048], F32)
    wstg = sb("wstg", [P, 2048], F32)
    p0scr = sb("p0scr", [P, 4096], BF16)
    p0f = sb("p0f", [P, 512], F32)
    qi = sb("qi", [P, 1024], I32)
    fcB = sb("fcB", [P, 1024], F32)

    pb = [psa(f"pb{i}") for i in range(8)]
    bank_gate = [None] * 8

    def bw(i):
        return [bank_gate[i]] if bank_gate[i] else []

    gam_b = AR1[:, 30720:31744].bitcast(F16)
    bet_b = AR1[:, 31744:32768].bitcast(F16)
    gam_lo = AR1[:, 28672:29696].bitcast(F16)
    bet_lo = AR1[:, 29696:30720].bitcast(F16)
    brow = xstage[0:1, 0:1024]

    b.op("gpsimd", mk("iota", iota_a[:, :], [[1, P]], channel_multiplier=0,
                      allow_small_or_imprecise_dtypes=True), inc="gP")
    e = b.op("gpsimd", mk("iota", iota_b[:, :], [[1, P]], channel_multiplier=1,
                          allow_small_or_imprecise_dtypes=True), inc="gP")
    b.op("vector", mk("tensor_sub", iota_b[:, :], iota_b[:, :], iota_a[:, :]),
         inc="vV", waits=[("gP", e)])
    b.op("vector", mk("tensor_tensor", ident[:, :], iota_b[:, :], iota_a[:, :], Alu.is_equal), inc="vV")
    b.op("vector", mk("memset", ones_rb[:, :], 1.0), inc="vV")
    b.op("vector", mk("memset", ones_rf[:, :], 1.0), inc="vV")
    b.op("vector", mk("memset", ones_cf[:, :], 1.0), inc="vV")
    b.op("vector", mk("memset", colsum_c[:, :], 0.0), inc="vV")
    b.op("vector", mk("memset", wacc[:, :], 0.0), inc="vV")
    e_init_v = b.cnt["vV"]

    e_bv = b.dma("sync", "qM", mk("dma_start", out=brow[:, :], in_=bfc_hbm[:, :]))
    b.op("vector", mk("tensor_copy", bfc_hi[:, :], brow[:, :]), inc="vV", waits=[("qM", e_bv)])
    e_bfc = b.op("vector", mk("scalar_tensor_tensor", bfc_lo[:, :], brow[:, :], 1.0, bfc_hi[:, :],
                              Alu.mult, Alu.subtract), inc="vV")

    p0_scrap = p0f[:, :]
    sgA = p0scr[:, 0:1024]
    sgB = p0scr[:, 1024:2048]
    qout = [p0scr[:, 2048:3072], p0scr[:, 3072:4096]]

    def w3_tile(m, j):
        if m < 2:
            return w3_hbm[m][j * 128:(j + 1) * 128, :]
        r, qc = divmod(j, 4)
        return w3_hbm[2][r * 128:(r + 1) * 128, qc * 1024:(qc + 1) * 1024]

    def s3_tile(m, j):
        if m < 2:
            return s_w3[m][j * 128:(j + 1) * 128, :]
        r, qc = divmod(j, 4)
        return s_w3[2][r * 128:(r + 1) * 128, qc * 1024:(qc + 1) * 1024]

    p0b_abs = []
    p0b_tiles_done = [0]

    def p0b_tiles_chunk(n):
        while p0b_tiles_done[0] < 96 and n > 0:
            idx = p0b_tiles_done[0]
            m, j = divmod(idx, 32)
            slot = idx % 2
            stg = wstg[:, slot * 1024:(slot + 1) * 1024]
            waits = []
            if idx >= 2:
                waits.append(("vV", p0b_abs[idx - 2]))
            src = w3_tile(m, j)
            ld = None
            for q4 in range(4):
                ld = b.dma("scalar", f"qA{slot}",
                           mk("dma_start", out=stg[:, q4 * 256:(q4 + 1) * 256],
                              in_=src[:, q4 * 256:(q4 + 1) * 256]),
                           waits=waits if q4 == 0 else ())
            aw = [(f"qA{slot}", ld)]
            if idx >= 2:
                aw.append(("gC", 2 * (idx - 2) + 2))
            b.op("vector", mk("scalar_tensor_tensor", p0_scrap, stg[:, 0:512], -1.0,
                              stg[:, 0:512], Alu.mult, Alu.max,
                              accum_out=svec[:, 20 + slot:21 + slot]),
                 inc="vV", waits=aw)
            ab = b.op("vector", mk("scalar_tensor_tensor", p0_scrap, stg[:, 512:1024], -1.0,
                                   stg[:, 512:1024], Alu.mult, Alu.max,
                                   accum_out=svec[:, 22 + slot:23 + slot]), inc="vV")
            p0b_abs.append(ab)
            p0b_tiles_done[0] += 1
            n -= 1

    p0b_emitted = [0]

    def p0b_add_chunk(n):
        while p0b_emitted[0] < len(p0b_abs) and n > 0:
            idx = p0b_emitted[0]
            m = idx // 32
            slot = idx % 2
            aw0 = [("vV", p0b_abs[idx])]
            if idx == 0:
                aw0.append(("vV", e_init_v))
            b.op("gpsimd", mk("tensor_add", wacc[:, m:m + 1], wacc[:, m:m + 1],
                              svec[:, 20 + slot:21 + slot]),
                 inc="gC", waits=aw0)
            b.op("gpsimd", mk("tensor_add", wacc[:, m:m + 1], wacc[:, m:m + 1],
                              svec[:, 22 + slot:23 + slot]), inc="gC")
            p0b_emitted[0] += 1
            n -= 1

    scale_ready = [None, None, None]

    def p0_scale(m):
        em = b.op("tensor", mk("matmul", pb[7][0:1, 0:1], wacc[:, m:m + 1], ones_cf[:, :],
                               start=True, stop=True),
                  inc="tM", waits=[("gC", (m + 1) * 64)] + bw(7))
        ev = b.op("vector", mk("tensor_copy", rowt[0:1, m:m + 1], pb[7][0:1, 0:1]),
                  inc="vV", waits=[("tM", em)])
        bank_gate[7] = ("vV", ev)
        b.op("vector", mk("tensor_scalar", rowt[0:1, 4 + m:5 + m], rowt[0:1, m:m + 1],
                          1.0 / (NI * D), 1e-5, Alu.mult, Alu.max), inc="vV")
        vv_now = b.cnt["vV"]
        em2 = b.op("tensor", mk("matmul", pb[7][:, 0:1], ones_rf[:, :], rowt[0:1, 4 + m:5 + m],
                                start=True, stop=True),
                   inc="tM", waits=[("vV", vv_now)] + bw(7))
        ev2 = b.op("vector", mk("tensor_copy", tcols[:, m:m + 1], pb[7][:, 0:1]),
                   inc="vV", waits=[("tM", em2)])
        bank_gate[7] = ("vV", ev2)
        b.op("vector", mk("tensor_single_scalar", tcols[:, 3 + m:4 + m], tcols[:, m:m + 1],
                          0.5, Alu.mult), inc="vV")
        ev3 = b.op("vector", mk("tensor_single_scalar", tcols[:, 6 + m:7 + m], tcols[:, 3 + m:4 + m],
                                -1.0, Alu.mult), inc="vV")
        scale_ready[m] = ev3

    p0c_state = {"i": 0, "signB": {}, "comb": {}, "st": {0: [], 1: []}}

    def p0c_tiles(n):
        while p0c_state["i"] < 96 and n > 0:
            idx = p0c_state["i"]
            m, j = divmod(idx, 32)
            slot = idx % 2
            stg = wstg[:, slot * 1024:(slot + 1) * 1024]
            lw = [("vV", p0b_abs[95])]
            if idx >= 2:
                lw.append(("vV", p0c_state["signB"][idx - 2]))
            src = w3_tile(m, j)
            ld = None
            for q4 in range(4):
                ld = b.dma("scalar", f"qA{slot}",
                           mk("dma_start", out=stg[:, q4 * 256:(q4 + 1) * 256],
                              in_=src[:, q4 * 256:(q4 + 1) * 256]),
                           waits=lw if q4 == 0 else ())
            wA = [(f"qA{slot}", ld), ("vV", scale_ready[m])]
            qv = qout[slot] if m < 2 else qout[slot].bitcast(F16)
            b.op("vector", mk("tensor_scalar", sgA[:, :], stg, tcols[:, 3 + m:4 + m], 2.0,
                              Alu.is_gt, Alu.mult), inc="vV", waits=wA)
            vB = b.op("vector", mk("tensor_scalar", sgB[:, :], stg, tcols[:, 6 + m:7 + m], 2.0,
                                   Alu.is_ge, Alu.mult), inc="vV")
            p0c_state["signB"][idx] = vB
            wC = []
            if len(p0c_state["st"][slot]) >= 1:
                wC.append((f"qAS{slot}", p0c_state["st"][slot][-1]))
            vC = b.op("vector", mk("scalar_tensor_tensor", qv, sgA[:, :], -2.0, sgB[:, :],
                                   Alu.add, Alu.add), inc="vV", waits=wC)
            p0c_state["comb"][idx] = vC
            st = b.dma("scalar", f"qAS{slot}", mk("dma_start", out=s3_tile(m, j), in_=qv),
                       waits=[("vV", vC)])
            p0c_state["st"][slot].append(st)
            p0c_state["i"] += 1
            n -= 1

    p0b_tiles_chunk(2)
    p0b_add_chunk(2)
    scr1 = AR3[:, 0:1024]
    e_norm_done = {}
    e_lnsum_mm = [None]
    e_lnsum_slot = {0: None, 1: None}
    for g in range(8):
        if g < 4:
            mu_dst = mu1[:, g * 4:(g + 1) * 4]
            r_dst = r1c[:, g * 4:(g + 1) * 4]
        else:
            mu_dst = mur2[:, (g - 4) * 4:(g - 3) * 4]
            r_dst = mur2[:, 16 + (g - 4) * 4:16 + (g - 3) * 4]
        for i in range(4):
            t_idx = g * 4 + i
            xt = AR4[:, i * 1024:(i + 1) * 1024]
            w = [("vV", e_norm_done[t_idx - 4])] if t_idx >= 4 else []
            for q2 in range(2):
                b.dma("gpsimd", "qP1X",
                      mk("dma_start", out=xt[:, q2 * 512:(q2 + 1) * 512],
                         in_=x_hbm[t_idx * 128:(t_idx + 1) * 128, q2 * 512:(q2 + 1) * 512]),
                      waits=w if q2 == 0 else ())
        grp_total = b.cnt["qP1X"]
        for i in range(4):
            xt = AR4[:, i * 1024:(i + 1) * 1024]
            b.op("vector", mk("reduce_sum", musq[:, i:i + 1], xt, AX),
                 inc="vV", waits=[("qP1X", grp_total)] if i == 0 else ())
            b.op("vector", mk("scalar_tensor_tensor", scr1, xt, 1.0, xt,
                              Alu.mult, Alu.mult, accum_out=sqs[:, i:i + 1]), inc="vV")
        b.op("vector", mk("tensor_single_scalar", mu_dst, musq[:, 0:4], 1.0 / D, Alu.mult), inc="vV")
        b.op("vector", mk("tensor_single_scalar", sqs[:, 0:4], sqs[:, 0:4], 1.0 / D, Alu.mult), inc="vV")
        b.op("vector", mk("scalar_tensor_tensor", svec[:, 0:4], mu_dst, 1.0,
                          mu_dst, Alu.mult, Alu.mult), inc="vV")
        b.op("vector", mk("tensor_sub", svec[:, 0:4], sqs[:, 0:4], svec[:, 0:4]), inc="vV")
        b.op("vector", mk("tensor_single_scalar", svec[:, 0:4], svec[:, 0:4], LN_EPS, Alu.add), inc="vV")
        ev_ve = b.op("vector", mk("reciprocal", musq[:, 0:4], svec[:, 0:4]), inc="vV")
        es = b.op("scalar", mk("activation", r_dst, musq[:, 0:4], Act.Sqrt),
                  inc="sC", waits=[("vV", ev_ve)])
        first = True
        for _ in range(2):
            b.op("vector", mk("scalar_tensor_tensor", sqs[:, 0:4], r_dst, 1.0,
                              r_dst, Alu.mult, Alu.mult), inc="vV",
                 waits=[("sC", es)] if first else ())
            first = False
            b.op("vector", mk("tensor_mul", sqs[:, 0:4], sqs[:, 0:4], svec[:, 0:4]), inc="vV")
            b.op("vector", mk("tensor_scalar", sqs[:, 0:4], sqs[:, 0:4], -0.5, 1.5,
                              Alu.mult, Alu.add), inc="vV")
            b.op("vector", mk("tensor_mul", r_dst, r_dst, sqs[:, 0:4]), inc="vV")
        for i in range(4):
            t_idx = g * 4 + i
            xt = AR4[:, i * 1024:(i + 1) * 1024]
            slot = t_idx % 2
            lnf = xstage[:, slot * 1024:(slot + 1) * 1024]
            waits = []
            if e_lnsum_slot[slot] is not None:
                waits.append(("tM", e_lnsum_slot[slot]))
            ev_f = b.op("vector", mk("tensor_scalar", lnf, xt,
                                     mu_dst[:, i:i + 1], r_dst[:, i:i + 1],
                                     Alu.subtract, Alu.mult), inc="vV", waits=waits)
            e_norm_done[t_idx] = ev_f
            em5 = None
            for ds in range(8):
                em5 = b.op("tensor", mk("matmul", pb[5][:, ds:ds + 1],
                                        lnf[:, ds * 128:(ds + 1) * 128], ones_cf[:, :],
                                        start=(t_idx == 0 and ds == 0), stop=False,
                                        skip_group_check=True),
                           inc="tM" if ds == 7 else None,
                           waits=[("vV", ev_f)] + (bw(5) if t_idx == 0 and ds == 0 else []))
            e_lnsum_slot[slot] = em5
            e_lnsum_mm[0] = em5
            p0b_tiles_chunk(3)
            p0b_add_chunk(3)
    p0b_tiles_chunk(96)
    p0b_add_chunk(96)
    ev5 = b.op("vector", mk("tensor_copy", lnsum_c[:, :], pb[5][:, 0:8]),
               inc="vV", waits=[("tM", e_lnsum_mm[0])])
    bank_gate[5] = ("vV", ev5)
    b.op("vector", mk("tensor_copy", lncol_hi[:, :], lnsum_c[:, :]), inc="vV")
    b.op("vector", mk("scalar_tensor_tensor", lncol_lo[:, :], lnsum_c[:, :], 1.0,
                      lncol_hi[:, :], Alu.mult, Alu.subtract), inc="vV")
    e_p1_v = b.cnt["vV"]

    cast_a = AR3[:, 6144:7168].bitcast(F16)
    cast_b = AR3[:, 7168:8192].bitcast(F16)
    e_nat_store = {0: [], 1: []}
    e_cast = []
    tiles = [("qkv", j, wqkv_hbm, s_wqkv) for j in range(16, 24)] + \
            [("fc", j, wfc_hbm, s_wfc) for j in range(8)]
    for idx, (which, j, src, dst) in enumerate(tiles):
        slot = idx % 2
        stg = xstage[:, slot * 1024:(slot + 1) * 1024]
        waits = []
        if idx < 2:
            waits.append(("vV", e_bfc))
            waits.append(("tM", e_lnsum_mm[0]))
        if idx >= 2:
            waits.append(("vV", e_cast[idx - 2]))
        ld = None
        for q4 in range(4):
            ld = b.dma("sync", f"qWN{slot}",
                       mk("dma_start", out=stg[:, q4 * 256:(q4 + 1) * 256],
                          in_=src[j * 128:(j + 1) * 128, q4 * 256:(q4 + 1) * 256]),
                       waits=waits if q4 == 0 else ())
        cst = cast_a if slot == 0 else cast_b
        cw = [(f"qWN{slot}", ld)]
        if idx < 2:
            cw.append(("vV", e_p1_v))
        if idx >= 2:
            cw.append((f"qGN{slot}", e_nat_store[slot][-1]))
        ev = b.op("vector", mk("tensor_copy", cst, stg), inc="vV", waits=cw)
        st = b.dma("gpsimd", f"qGN{slot}", mk("dma_start", out=dst[j * 128:(j + 1) * 128, :], in_=cst),
                   waits=[("vV", ev)])
        if which == "fc" or (which == "qkv" and j >= 16):
            lodst = s_wfc_lo if which == "fc" else s_wv_lo
            jj = j if which == "fc" else j - 16
            ev = b.op("vector", mk("scalar_tensor_tensor", cst, stg, 1.0, cst,
                                   Alu.mult, Alu.subtract), inc="vV",
                      waits=[(f"qGN{slot}", st)])
            st = b.dma("gpsimd", f"qGN{slot}",
                       mk("dma_start", out=lodst[jj * 128:(jj + 1) * 128, :], in_=cst),
                       waits=[("vV", ev)])
        e_cast.append(ev)
        e_nat_store[slot].append(st)
    e_cast_last = e_cast[-1]
    e_qgn_all = [b.cnt["qGN0"], b.cnt["qGN1"]]

    p0_scale(0)
    p0c_tiles(16)
    p0_scale(1)
    p0c_tiles(28)
    p0_scale(2)
    p0c_tiles(28)
    p0c_tiles(24)

    e_qas_all = [b.cnt["qAS0"], b.cnt["qAS1"]]
    e_grps = b.cnt["tM"]
    e_vmm = {}
    emv = None
    for ds in range(8):
        lw = []
        if ds == 0:
            lw += [("qGN0", e_qgn_all[0]), ("qGN1", e_qgn_all[1]), ("vV", e_cast_last),
                   ("vV", e_p1_v)]
        if ds >= 1:
            lw.append(("tM", e_vmm[ds - 1]))
        ldh = b.dma("sync", "qFCT0",
                    mk("dma_start", out=cast_a,
                       in_=s_wqkv[2 * D:3 * D, ds * 128:(ds + 1) * 128], transpose=True),
                    waits=lw)
        ldl = b.dma("sync", "qFCT1",
                    mk("dma_start", out=cast_b,
                       in_=s_wv_lo[:, ds * 128:(ds + 1) * 128], transpose=True),
                    waits=lw if ds == 0 else ())
        mw0 = [("qFCT0", ldh), ("qFCT1", ldl)]
        if ds == 0:
            mw0 += bw(6)
        for sub in range(8):
            first = (ds == 0 and sub == 0)
            b.op("tensor", mk("matmul", pb[6][:, sub:sub + 1],
                              cast_a[:, sub * 128:(sub + 1) * 128], lncol_hi[:, ds:ds + 1],
                              start=first, stop=False, skip_group_check=True),
                 waits=mw0 if sub == 0 else ())
            b.op("tensor", mk("matmul", pb[6][:, sub:sub + 1],
                              cast_a[:, sub * 128:(sub + 1) * 128], lncol_lo[:, ds:ds + 1],
                              start=False, stop=False, skip_group_check=True))
            emv = b.op("tensor", mk("matmul", pb[6][:, sub:sub + 1],
                                    cast_b[:, sub * 128:(sub + 1) * 128], lncol_hi[:, ds:ds + 1],
                                    start=False, stop=False, skip_group_check=True),
                       inc="tM" if sub == 7 else None)
        e_vmm[ds] = emv
    ev6 = b.op("vector", mk("tensor_copy", colsum_c[:, :], pb[6][:, 0:8]),
               inc="vV", waits=[("tM", emv)])
    bank_gate[6] = ("vV", ev6)
    b.op("vector", mk("tensor_copy", colsum_cb[:, :], colsum_c[:, :]), inc="vV")
    b.op("vector", mk("scalar_tensor_tensor", colsum_lo[:, :], colsum_c[:, :], 1.0,
                      colsum_cb[:, :], Alu.mult, Alu.subtract), inc="vV")
    e_cc = b.cnt["vV"]
    e_fcmm = {}
    em3 = None
    for dc in range(8):
        lw = []
        if dc == 0:
            lw += [("qGN0", e_qgn_all[0]), ("qGN1", e_qgn_all[1]), ("vV", e_cast_last),
                   ("tM", e_vmm[7])]
        if dc >= 1:
            lw.append(("tM", e_fcmm[dc - 1]))
        ldf = b.dma("sync", "qFCT0",
                    mk("dma_start", out=cast_a,
                       in_=s_wfc[:, dc * 128:(dc + 1) * 128], transpose=True), waits=lw)
        ldf2 = b.dma("sync", "qFCT1",
                     mk("dma_start", out=cast_b,
                        in_=s_wfc_lo[:, dc * 128:(dc + 1) * 128], transpose=True),
                     waits=lw if dc == 0 else ())
        mw = [("qFCT0", ldf), ("qFCT1", ldf2)]
        if dc == 0:
            mw += [("vV", e_cc)] + bw(2) + bw(3)
        b.op("tensor", mk("matmul", pb[2][0:1, :], colsum_cb[:, dc:dc + 1],
                          cast_a[:, 0:512], start=(dc == 0), stop=False),
             waits=mw)
        b.op("tensor", mk("matmul", pb[2][0:1, :], colsum_lo[:, dc:dc + 1],
                          cast_a[:, 0:512], start=False, stop=False))
        b.op("tensor", mk("matmul", pb[2][0:1, :], colsum_cb[:, dc:dc + 1],
                          cast_b[:, 0:512], start=False, stop=(dc == 7)))
        b.op("tensor", mk("matmul", pb[3][0:1, :], colsum_cb[:, dc:dc + 1],
                          cast_a[:, 512:1024], start=(dc == 0), stop=False))
        b.op("tensor", mk("matmul", pb[3][0:1, :], colsum_lo[:, dc:dc + 1],
                          cast_a[:, 512:1024], start=False, stop=False))
        em3 = b.op("tensor", mk("matmul", pb[3][0:1, :], colsum_cb[:, dc:dc + 1],
                                cast_b[:, 512:1024], start=False, stop=(dc == 7)),
                   inc="tM")
        e_fcmm[dc] = em3
    e_colfc = None
    for eh in range(2):
        b.op("vector", mk("tensor_copy", colfc_b[0:1, eh * 512:(eh + 1) * 512],
                          pb[2 + eh][0:1, :]), inc="vV", waits=[("tM", em3)])
        e_colfc = b.op("vector", mk("scalar_tensor_tensor", colfc_lo[0:1, eh * 512:(eh + 1) * 512],
                                    pb[2 + eh][0:1, :], 1.0,
                                    colfc_b[0:1, eh * 512:(eh + 1) * 512],
                                    Alu.mult, Alu.subtract), inc="vV")
        bank_gate[2 + eh] = ("vV", e_colfc)
    e_p2_tensor = b.cnt["tM"]

    rt_slots = [AR3[:, 5120:7168], AR3[:, 7168:9216]]
    rt_items = ([("gu", mtx, ds, half) for mtx in (0, 1) for ds in range(8)
                 for half in range(2)] +
                [("wd", ic, 0, 0) for ic in range(32)])
    rt_state = {"i": 0, "st": {}}

    def rt_emit(n):
        while rt_state["i"] < len(rt_items) and n > 0:
            i = rt_state["i"]
            kind, a1, a2, a3 = rt_items[i]
            sl = i % 2
            slot = rt_slots[sl]
            lw = []
            if i == 0:
                lw += [("qAS0", e_qas_all[0]), ("qAS1", e_qas_all[1]),
                       ("tM", e_p2_tensor)]
            if i >= 2:
                lw.append((f"qTS{sl}", rt_state["st"][i - 2]))
            if kind == "gu":
                src = s_w3[a1]
                dst = s_wg2t if a1 == 0 else s_wu2t
                ldr = None
                for q in range(2):
                    ldr = b.dma("scalar", f"qTR{sl}",
                                mk("dma_start", out=slot[:, q * 1024:(q + 1) * 1024],
                                   in_=src[a3 * 2048 + q * 1024:a3 * 2048 + (q + 1) * 1024,
                                           a2 * 128:(a2 + 1) * 128],
                                   transpose=True), waits=lw if q == 0 else ())
                st = b.dma("scalar", f"qTS{sl}",
                           mk("dma_start",
                              out=dst[a2 * 128:(a2 + 1) * 128, a3 * 2048:(a3 + 1) * 2048],
                              in_=slot), waits=[(f"qTR{sl}", ldr)])
            else:
                sl16 = slot[:, 0:1024].bitcast(F16)
                ldr = b.dma("scalar", f"qTR{sl}",
                            mk("dma_start", out=sl16,
                               in_=s_w3[2][:, a1 * 128:(a1 + 1) * 128],
                               transpose=True), waits=lw)
                st = b.dma("scalar", f"qTS{sl}",
                           mk("dma_start", out=s_wd2t[a1 * 128:(a1 + 1) * 128, :],
                              in_=sl16), waits=[(f"qTR{sl}", ldr)])
            rt_state["st"][i] = st
            rt_state["i"] += 1
            n -= 1

    gbrow = fcB[0:1, 0:1024]
    e_gamld = b.dma("gpsimd", "qMB", mk("dma_start", out=gbrow, in_=gam_hbm[:, :]),
                    waits=[("vV", e_bfc), ("tM", e_grps), ("vV", e_p1_v)])
    ev = None
    em = None
    for eh in range(2):
        em = b.op("tensor", mk("matmul", pb[7][:, :], ones_rf[:, :],
                               gbrow[0:1, eh * 512:(eh + 1) * 512], start=True, stop=True),
                  inc="tM", waits=[("qMB", e_gamld)] + bw(7))
        b.op("vector", mk("tensor_copy", gam_b[:, eh * 512:(eh + 1) * 512], pb[7][:, :]),
             inc="vV", waits=[("tM", em), ("tM", e_grps)])
        ev = b.op("vector", mk("scalar_tensor_tensor", gam_lo[:, eh * 512:(eh + 1) * 512],
                               pb[7][:, :], 1.0, gam_b[:, eh * 512:(eh + 1) * 512],
                               Alu.mult, Alu.subtract), inc="vV")
        bank_gate[7] = ("vV", ev)
    e_betld = b.dma("gpsimd", "qMB", mk("dma_start", out=gbrow, in_=bet_hbm[:, :]),
                    waits=[("tM", em)])
    for eh in range(2):
        em = b.op("tensor", mk("matmul", pb[7][:, :], ones_rf[:, :],
                               gbrow[0:1, eh * 512:(eh + 1) * 512], start=True, stop=True),
                  inc="tM", waits=[("qMB", e_betld)] + bw(7))
        b.op("vector", mk("tensor_copy", bet_b[:, eh * 512:(eh + 1) * 512], pb[7][:, :]),
             inc="vV", waits=[("tM", em)])
        ev = b.op("vector", mk("scalar_tensor_tensor", bet_lo[:, eh * 512:(eh + 1) * 512],
                               pb[7][:, :], 1.0, bet_b[:, eh * 512:(eh + 1) * 512],
                               Alu.mult, Alu.subtract), inc="vV")
        bank_gate[7] = ("vV", ev)
    e_gambet_v = ev

    fc_bc = fcB[:, :]
    ev_fb = None
    for eh in range(2):
        w0 = [("vV", e_colfc), ("vV", e_bfc), ("vV", e_gambet_v)] + bw(4)
        b.op("tensor", mk("matmul", pb[4][:, :], ones_rb[:, :],
                          colfc_b[0:1, eh * 512:(eh + 1) * 512], start=True, stop=False),
             waits=w0)
        b.op("tensor", mk("matmul", pb[4][:, :], ones_rb[:, :],
                          colfc_lo[0:1, eh * 512:(eh + 1) * 512], start=False, stop=False))
        b.op("tensor", mk("matmul", pb[4][:, :], ones_rb[:, :],
                          bfc_hi[0:1, eh * 512:(eh + 1) * 512], start=False, stop=False))
        em = b.op("tensor", mk("matmul", pb[4][:, :], ones_rb[:, :],
                               bfc_lo[0:1, eh * 512:(eh + 1) * 512], start=False, stop=True),
                  inc="tM")
        ev_fb = b.op("vector", mk("tensor_copy", fc_bc[:, eh * 512:(eh + 1) * 512],
                                  pb[4][:, :]), inc="vV", waits=[("tM", em)])
        bank_gate[4] = ("vV", ev_fb)
    e_fcbc = ev_fb
    if KDBG:
        edf = b.dma("gpsimd", "qMB", mk("dma_start", out=s_dbg[:, 0:1024],
                                        in_=fc_bc), waits=[("vV", e_fcbc)])
        b.wait("vector", "qMB", edf)

    t2scr = AR4[:, 2048:3072]
    sqscr = t2scr
    ln1t = AR4[:, 0:1024]
    hbuf = AR4[:, 1024:2048]
    prev_ep = {}

    for qt in range(16):
        qslot = qt % 2
        xt = xstage[:, qslot * 1024:(qslot + 1) * 1024]
        xw = [("vV", e_cast_last)]
        if qt >= 2:
            xw.append(("vV", prev_ep[qt - 2]["x_done"]))
        ld_x = None
        for q2 in range(2):
            ld_x = b.dma("sync", f"qXT{qslot}",
                         mk("dma_start", out=xt[:, q2 * 512:(q2 + 1) * 512],
                            in_=x_hbm[qt * 128:(qt + 1) * 128, q2 * 512:(q2 + 1) * 512]),
                         waits=xw if q2 == 0 else ())
        lw = [(f"qXT{qslot}", ld_x), ("vV", e_fcbc)]
        ev_ln1t = b.op("vector", mk("tensor_scalar", ln1t, xt, mu1[:, qt:qt + 1], r1c[:, qt:qt + 1],
                                    Alu.subtract, Alu.mult), inc="vV", waits=lw)
        b.op("gpsimd", mk("tensor_tensor", xt, xt, bet_b[:, :], Alu.add), inc="gP",
             waits=[("vV", ev_ln1t)])
        eg_x4 = b.op("gpsimd", mk("tensor_tensor", xt, xt, bet_lo[:, :], Alu.add), inc="gP")
        hw = []
        if qt >= 1:
            hw.append(("qGX1", 16 * qt))
        b.op("vector", mk("tensor_add", hbuf, ln1t, fc_bc), inc="vV", waits=hw)
        b.op("vector", mk("reduce_sum", svec[:, 8:9], hbuf, AX), inc="vV")
        b.op("vector", mk("scalar_tensor_tensor", sqscr, hbuf, 1.0, hbuf,
                          Alu.mult, Alu.mult, accum_out=svec[:, 9:10]), inc="vV")
        b.op("vector", mk("tensor_single_scalar", svec[:, 10:11], svec[:, 8:9], 1.0 / D, Alu.mult), inc="vV")
        b.op("vector", mk("tensor_single_scalar", svec[:, 11:12], svec[:, 9:10], 1.0 / D, Alu.mult), inc="vV")
        b.op("vector", mk("scalar_tensor_tensor", svec[:, 12:13], svec[:, 10:11], 1.0,
                          svec[:, 10:11], Alu.mult, Alu.mult), inc="vV")
        b.op("vector", mk("tensor_sub", svec[:, 12:13], svec[:, 11:12], svec[:, 12:13]), inc="vV")
        b.op("vector", mk("tensor_single_scalar", svec[:, 12:13], svec[:, 12:13], LN_EPS, Alu.add), inc="vV")
        ev_r = b.op("vector", mk("reciprocal", svec[:, 13:14], svec[:, 12:13]), inc="vV")
        es_s = b.op("scalar", mk("activation", svec[:, 14:15], svec[:, 13:14], Act.Sqrt),
                    inc="sC", waits=[("vV", ev_r)])
        first = True
        for _ in range(2):
            b.op("vector", mk("scalar_tensor_tensor", svec[:, 15:16], svec[:, 14:15], 1.0,
                              svec[:, 14:15], Alu.mult, Alu.mult), inc="vV",
                 waits=[("sC", es_s)] if first else ())
            first = False
            b.op("vector", mk("tensor_mul", svec[:, 15:16], svec[:, 15:16], svec[:, 12:13]), inc="vV")
            b.op("vector", mk("tensor_scalar", svec[:, 15:16], svec[:, 15:16], -0.5, 1.5,
                              Alu.mult, Alu.add), inc="vV")
            b.op("vector", mk("tensor_mul", svec[:, 14:15], svec[:, 14:15], svec[:, 15:16]), inc="vV")
        ev_t2 = b.op("vector", mk("tensor_scalar", t2scr, hbuf, svec[:, 10:11], svec[:, 14:15],
                                  Alu.subtract, Alu.mult), inc="vV")
        t2b = AR4[:, 3072:4096]
        b.op("gpsimd", mk("tensor_mul", t2b, t2scr, gam_lo[:, :]), inc="gP",
             waits=[("vV", ev_t2)])
        eg_t3 = b.op("gpsimd", mk("tensor_mul", t2scr, t2scr, gam_b[:, :]), inc="gP")
        b.op("vector", mk("tensor_add", hbuf, t2scr, t2b), inc="vV",
             waits=[("gP", eg_t3), ("gP", eg_x4)])
        ev_x1 = b.op("vector", mk("tensor_add", hbuf, hbuf, xt), inc="vV")
        b.dma("gpsimd", "qGX1", mk("dma_start", out=s_x1[qt * 128:(qt + 1) * 128, :], in_=hbuf),
              waits=[("vV", ev_x1)])
        prev_ep[qt] = {"x_done": ev_x1}
        rt_emit(4)

    rt_emit(64)
    e_rt_wgwu = [("qTS0", rt_state["st"][30]), ("qTS1", rt_state["st"][31])]
    e_rt_all = [("qTS0", rt_state["st"][62]), ("qTS1", rt_state["st"][63])]
    e_p3_tensor = b.cnt["tM"]
    e_p3_v = b.cnt["vV"]
    e_p3_g = b.cnt["gP"]
    e_p3_x1st = b.cnt["qGX1"]

    ln2s = wstg[:, 0:1024]
    ybuf1 = wstg[:, 1024:2048]
    ybuf = AR3[:, 0:1024].bitcast(F16)
    r0b = qi[:, :]
    g1b = AR3[:, 1024:2048].bitcast(F16)
    g2b = AR3[:, 2048:3072].bitcast(F16)
    g3b = AR3[:, 3072:4096].bitcast(F16)
    q2sp = AR3[:, 4096:5120].bitcast(F16)
    q2sn = AR3[:, 5120:6144].bitcast(F16)
    xqo = AR3[:, 8192:9216]
    xq1T = [AR2[:, 0:4096], AR2[:, 4096:8192]]
    wgb = [AR2[:, 8192:12288], AR2[:, 12288:16384]]
    wub = [AR2[:, 16384:20480], AR2[:, 20480:24576]]
    wdb = [AR2[:, 24576:25600].bitcast(F16), AR2[:, 25600:26624].bitcast(F16)]
    sgs_s = [AR2[:, 26624:27136].bitcast(F16), AR2[:, 27648:28160].bitcast(F16)]
    uss_s = [AR2[:, 27136:27648].bitcast(F16), AR2[:, 28160:28672].bitcast(F16)]
    x1t = [AR4[:, 0:1024], AR4[:, 1024:2048]]
    x2t = [AR4[:, 2048:3072], AR4[:, 3072:4096]]

    e_q1 = {}
    for tt in range(16):
        slot = tt % 2
        xi = x1t[slot]
        w0 = [("qGX1", 16 * (tt + 1))]
        if tt < 2:
            w0.append(("qGX1", e_p3_x1st))
        if tt == 0:
            w0 += [("vV", e_p3_v), ("gP", e_p3_g), ("tM", e_p2_tensor),
                   ("vV", p0c_state["signB"][95])]
        if tt >= 2:
            w0.append(("vV", e_q1[tt - 2]))
        ld = None
        for q2 in range(2):
            ld = b.dma("sync", f"qX1{slot}",
                       mk("dma_start", out=xi[:, q2 * 512:(q2 + 1) * 512],
                          in_=s_x1[tt * 128:(tt + 1) * 128, q2 * 512:(q2 + 1) * 512]),
                       waits=w0 if q2 == 0 else ())
        sw = [(f"qX1{slot}", ld)]
        b.op("vector", mk("reduce_sum", svec[:, 0:1], xi, AX), inc="vV", waits=sw)
        b.op("vector", mk("scalar_tensor_tensor", ln2s, xi, 1.0, xi, Alu.mult, Alu.mult,
                          accum_out=svec[:, 1:2]), inc="vV")
        b.op("vector", mk("tensor_single_scalar", svec[:, 2:3], svec[:, 0:1], 1.0 / D, Alu.mult), inc="vV")
        b.op("vector", mk("tensor_single_scalar", svec[:, 3:4], svec[:, 1:2], 1.0 / D, Alu.mult), inc="vV")
        b.op("vector", mk("scalar_tensor_tensor", svec[:, 4:5], svec[:, 2:3], 1.0, svec[:, 2:3],
                          Alu.mult, Alu.mult), inc="vV")
        b.op("vector", mk("tensor_sub", svec[:, 4:5], svec[:, 3:4], svec[:, 4:5]), inc="vV")
        b.op("vector", mk("tensor_single_scalar", svec[:, 4:5], svec[:, 4:5], LN_EPS, Alu.add), inc="vV")
        ev_r = b.op("vector", mk("reciprocal", svec[:, 5:6], svec[:, 4:5]), inc="vV")
        es_s = b.op("scalar", mk("activation", svec[:, 6:7], svec[:, 5:6], Act.Sqrt),
                    inc="sC", waits=[("vV", ev_r)])
        first = True
        for _ in range(2):
            b.op("vector", mk("scalar_tensor_tensor", svec[:, 7:8], svec[:, 6:7], 1.0,
                              svec[:, 6:7], Alu.mult, Alu.mult), inc="vV",
                 waits=[("sC", es_s)] if first else ())
            first = False
            b.op("vector", mk("tensor_mul", svec[:, 7:8], svec[:, 7:8], svec[:, 4:5]), inc="vV")
            b.op("vector", mk("tensor_scalar", svec[:, 7:8], svec[:, 7:8], -0.5, 1.5,
                              Alu.mult, Alu.add), inc="vV")
            b.op("vector", mk("tensor_mul", svec[:, 6:7], svec[:, 6:7], svec[:, 7:8]), inc="vV")
        b.op("vector", mk("tensor_scalar", ln2s, xi, svec[:, 2:3], svec[:, 6:7],
                          Alu.subtract, Alu.mult), inc="vV")
        b.op("vector", mk("reduce_max", svec[:, 8:9], ln2s, AX, True), inc="vV")
        b.op("vector", mk("tensor_scalar_max", svec[:, 8:9], svec[:, 8:9], 1e-5), inc="vV")
        b.op("vector", mk("reciprocal", svec[:, 9:10], svec[:, 8:9]), inc="vV")
        b.op("vector", mk("tensor_single_scalar", s1c[:, tt:tt + 1], svec[:, 9:10], 127.0, Alu.mult), inc="vV")
        b.op("vector", mk("tensor_single_scalar", svec[:, 10:11], svec[:, 8:9], 1.0 / 127.0, Alu.mult), inc="vV")
        b.op("vector", mk("tensor_mul", a1g[:, tt:tt + 1], svec[:, 10:11], tcols[:, 3:4]), inc="vV",
             waits=[("vV", scale_ready[0]), ("vV", scale_ready[1])])
        ev_sc = b.op("vector", mk("tensor_mul", a1u[:, tt:tt + 1], svec[:, 10:11], tcols[:, 4:5]), inc="vV")
        es_y = b.op("scalar", mk("activation", ybuf1, ln2s, Act.Copy, scale=s1c[:, tt:tt + 1]),
                    inc="sC", waits=[("vV", ev_sc)])
        if KDBG and tt == 0:
            ed1 = b.dma("gpsimd", "qMB", mk("dma_start", out=s_dbg[:, 0:1024], in_=ybuf1),
                        waits=[("sC", es_y)])
            ed2 = b.dma("gpsimd", "qMB", mk("dma_start", out=s_dbg[:, 1024:2048], in_=ln2s))
            b.wait("vector", "qMB", ed2)
        e_r0 = b.op("vector", mk("tensor_copy", r0b, ybuf1), inc="vV", waits=[("sC", es_y)])
        e_g1a = b.op("vector", mk("scalar_tensor_tensor", g1b, ybuf1, -0.5, r0b, Alu.add, Alu.is_gt), inc="vV")
        e_g2a = b.op("vector", mk("scalar_tensor_tensor", g2b, ybuf1, 0.5, r0b, Alu.add, Alu.is_ge), inc="vV")
        if KDBG and tt == 0:
            ed3 = b.dma("gpsimd", "qMB", mk("dma_start", out=s_dbg[:, 2048:3072].bitcast(I32), in_=r0b),
                        waits=[("vV", e_g2a)])
            ed4 = b.dma("gpsimd", "qMB", mk("dma_start", out=s_dbg[:, 3072:4096], in_=g1b))
            ed5 = b.dma("gpsimd", "qMB", mk("dma_start", out=s_dbg[:, 4096:5120], in_=g2b))
            b.wait("vector", "qMB", ed5)
        b.op("vector", mk("scalar_tensor_tensor", g1b, g1b, -1.0, g2b, Alu.add, Alu.add), inc="vV")
        qw2 = [("qGQ1", 16 * tt)] if tt >= 1 else list(e_rt_all)
        ev_q = b.op("vector", mk("scalar_tensor_tensor", xqo, g1b, 1.0, r0b, Alu.mult, Alu.add),
                    inc="vV", waits=qw2)
        e_q1[tt] = ev_q
        b.dma("gpsimd", "qGQ1", mk("dma_start", out=s_xq1[tt * 128:(tt + 1) * 128, :], in_=xqo),
              waits=[("vV", ev_q)])
    e_q1_last = e_q1[15]

    AR1h = [AR1[:, 0:16384], AR1[:, 16384:32768]]

    def prod_ap(k, tl, dc):
        return AR1h[k % 2][:, tl * 4096 + dc * 512: tl * 4096 + (dc + 1) * 512].bitcast(F16)

    def prod_tt(k, tl):
        return AR1h[k % 2][:, tl * 4096:(tl + 1) * 4096].bitcast(F16)

    def xq2T_ap(k, ic):
        return AR1h[k % 2][:, ic * 512:(ic + 1) * 512].bitcast(F16)

    e_xq1T = {}
    e_gu_done = {}
    e_q2_done = {}
    e_xq2T = {}
    e_down_mm = {}
    e_out_store = {}
    e_prev_wchunk = {}
    e_prev_prod = [None]
    e_prod_hist = {0: None, 1: None}
    e_prev_x2 = {}
    e_wd_prev = {}

    def emit_xq1T(k):
        sl = k % 2
        w = [("qGQ1", 16 * (4 * k + 4))]
        if k >= 2:
            w.append(("tM", e_gu_done[k - 2]))
        if k == 0:
            w.append(("tM", e_p3_tensor))
            w += e_rt_all
        last = None
        for ds in range(8):
            last = b.dma("sync", f"qQ1T{sl}",
                         mk("dma_start", out=xq1T[sl][:, ds * 512:(ds + 1) * 512],
                            in_=s_xq1[k * 512:(k + 1) * 512, ds * 128:(ds + 1) * 128],
                            transpose=True), waits=w if ds == 0 else ())
        e_xq1T[k] = last

    def emit_gate_up(k):
        sl = k % 2
        last_mm = None
        e_am_init = {}
        for tl in range(4):
            aw = []
            if k >= 2:
                aw.append(("vV", e_q2_done[k - 2]))
            e_am_init[tl] = b.op("vector",
                                 mk("memset", am2[:, 4 * (k % 2) + tl: 4 * (k % 2) + tl + 1], 0.0),
                                 inc="vV", waits=aw)
        for dc in range(8):
            wsl = dc % 2
            wload_w = []
            if k == 0 and dc <= 1:
                wload_w += e_rt_wgwu
            if k == 0 and dc == 0:
                wload_w.append(("tM", e_p3_tensor))
            prev_key = (k, dc - 2) if dc >= 2 else ((k - 1, 6 + dc) if k > 0 else None)
            if prev_key is not None:
                wload_w.append(("tM", e_prev_wchunk.get(prev_key)))
            lg = None
            for ds in range(8):
                lg = b.dma("sync", f"qWG{wsl}",
                           mk("dma_start", out=wgb[wsl][:, ds * 512:(ds + 1) * 512],
                              in_=s_wg2t[ds * 128:(ds + 1) * 128, dc * 512:(dc + 1) * 512]),
                           waits=wload_w if ds == 0 else ())
            lu = None
            for ds in range(8):
                lu = b.dma("sync", f"qWU{wsl}",
                           mk("dma_start", out=wub[wsl][:, ds * 512:(ds + 1) * 512],
                              in_=s_wu2t[ds * 128:(ds + 1) * 128, dc * 512:(dc + 1) * 512]),
                           waits=wload_w if ds == 0 else ())
            for tl in range(4):
                tt = 4 * k + tl
                bg = (2 * tl) % 4
                bu = (2 * tl + 1) % 4
                w0 = [(f"qWG{wsl}", lg), (f"qQ1T{sl}", e_xq1T[k])] + bw(bg)
                emg = None
                for ds in range(8):
                    lhs = xq1T[sl][:, ds * 512 + tl * 128: ds * 512 + (tl + 1) * 128]
                    rhs = wgb[wsl][:, ds * 512:(ds + 1) * 512]
                    emg = b.op("tensor", mk("matmul", pb[bg][:, :], lhs, rhs,
                                            start=(ds == 0), stop=(ds == 7)),
                               inc="tM" if ds == 7 else None, waits=w0 if ds == 0 else ())
                p2 = (dc * 4 + tl) % 2
                sgw = [("tM", emg)]
                if e_prod_hist[p2] is not None:
                    sgw.append(("vV", e_prod_hist[p2]))
                esg = b.op("scalar", mk("activation", sgs_s[p2], pb[bg][:, :], Act.Silu,
                                        scale=a1g[:, tt:tt + 1]), inc="sC", waits=sgw)
                bank_gate[bg] = ("sC", esg)
                w1 = [(f"qWU{wsl}", lu)] + bw(bu)
                emu = None
                for ds in range(8):
                    lhs = xq1T[sl][:, ds * 512 + tl * 128: ds * 512 + (tl + 1) * 128]
                    rhs = wub[wsl][:, ds * 512:(ds + 1) * 512]
                    emu = b.op("tensor", mk("matmul", pb[bu][:, :], lhs, rhs,
                                            start=(ds == 0), stop=(ds == 7)),
                               inc="tM" if ds == 7 else None, waits=w1 if ds == 0 else ())
                last_mm = emu
                esu = b.op("scalar", mk("activation", uss_s[p2], pb[bu][:, :], Act.Copy,
                                        scale=a1u[:, tt:tt + 1]), inc="sC", waits=[("tM", emu)])
                bank_gate[bu] = ("sC", esu)
                pw = [("sC", esu)]
                if k >= 2 and dc == 0 and tl == 0:
                    pw.append(("qGX2", 16 * 4 * (k - 1)))
                ev_p = b.op("vector", mk("tensor_mul", prod_ap(k, tl, dc), sgs_s[p2], uss_s[p2]),
                            inc="vV", waits=pw)
                e_prod_hist[p2] = ev_p
                b.op("vector", mk("reduce_max", svec[:, 26:27], prod_ap(k, tl, dc), AX, True),
                     inc="vV", waits=[("vV", e_am_init[tl])])
                b.op("vector", mk("tensor_max", am2[:, 4 * (k % 2) + tl: 4 * (k % 2) + tl + 1],
                                  am2[:, 4 * (k % 2) + tl: 4 * (k % 2) + tl + 1], svec[:, 26:27]),
                     inc="vV")
            e_prev_wchunk[(k, dc)] = last_mm
        e_gu_done[k] = last_mm

    q2_st_prev = [None]

    def emit_quant2(k):
        last_v = None
        for tl in range(4):
            tt = 4 * k + tl
            amc = am2[:, 4 * (k % 2) + tl: 4 * (k % 2) + tl + 1]
            b.op("vector", mk("tensor_scalar_max", svec[:, 16:17], amc, 1e-5), inc="vV")
            b.op("vector", mk("reciprocal", svec[:, 17:18], svec[:, 16:17]), inc="vV")
            b.op("vector", mk("tensor_single_scalar", s2c[:, tt:tt + 1], svec[:, 17:18], 127.0,
                              Alu.mult), inc="vV")
            b.op("vector", mk("tensor_single_scalar", svec[:, 18:19], svec[:, 16:17], 1.0 / 127.0,
                              Alu.mult), inc="vV")
            ev_a2 = b.op("vector", mk("tensor_mul", a2d[:, tt:tt + 1], svec[:, 18:19], tcols[:, 5:6]),
                         inc="vV", waits=[("vV", scale_ready[2])])
            prev_g2 = None
            for sub in range(4):
                srcp = prod_tt(k, tl)[:, sub * 1024:(sub + 1) * 1024]
                yw = [("vV", ev_a2)]
                if prev_g2 is not None:
                    yw.append(("vV", prev_g2))
                es_y = b.op("scalar", mk("activation", ybuf, srcp, Act.Copy,
                                         scale=s2c[:, tt:tt + 1]), inc="sC", waits=yw)
                if KDBG:
                    edy = b.dma("gpsimd", "qMB",
                                mk("dma_start",
                                   out=s_dby[(4 * k + tl) * 128:(4 * k + tl + 1) * 128,
                                             sub * 1024:(sub + 1) * 1024],
                                   in_=ybuf), waits=[("sC", es_y)])
                    b.wait("scalar", "qMB", edy)
                    b.wait("vector", "qMB", edy)
                b.op("vector", mk("tensor_copy", r0b, ybuf), inc="vV", waits=[("sC", es_y)])
                b.op("vector", mk("tensor_copy", g3b, r0b), inc="vV")
                prev_g2 = b.op("vector", mk("tensor_sub", g1b, ybuf, g3b), inc="vV")
                b.op("vector", mk("tensor_single_scalar", g2b, g1b, 0.5, Alu.is_gt), inc="vV")
                b.op("vector", mk("tensor_single_scalar", q2sp, g1b, 1.5, Alu.is_gt), inc="vV")
                b.op("vector", mk("tensor_add", g2b, g2b, q2sp), inc="vV")
                b.op("vector", mk("tensor_single_scalar", q2sp, g1b, -0.5, Alu.is_ge), inc="vV")
                b.op("vector", mk("tensor_single_scalar", q2sn, g1b, -1.5, Alu.is_ge), inc="vV")
                b.op("vector", mk("tensor_add", q2sp, q2sp, q2sn), inc="vV")
                b.op("vector", mk("tensor_add", g2b, g2b, q2sp), inc="vV")
                b.op("vector", mk("tensor_single_scalar", g2b, g2b, -2.0, Alu.add), inc="vV")
                last_v = b.op("vector", mk("tensor_add", srcp, g2b, g3b), inc="vV")
            sw_ = [("vV", last_v)]
            if q2_st_prev[0] is not None:
                sw_.append(("qGX2", q2_st_prev[0]))
            q2_st_prev[0] = b.dma("gpsimd", "qGX2",
                                  mk("dma_start", out=s_xq2[tt * 128:(tt + 1) * 128, :],
                                     in_=prod_tt(k, tl)),
                                  waits=sw_)
        e_q2_done[k] = last_v

    xqTb = [AR3[:, 6144 + s * 512:6656 + s * 512].bitcast(F16) for s in range(4)]
    xqT_sem = ["qQ2T0", "qQ2T1", "qQT0", "qQT1"]

    def emit_down(k):
        e_wd = {}
        for ic in range(32):
            wsl = ic % 2
            ts_ = ic % 4
            tw = []
            if ic == 0:
                tw.append(("qGX2", 16 * 4 * (k + 1)))
            if ic >= 4:
                tw.append(("tM", e_wd[ic - 4]))
            elif k > 0:
                tw.append(("tM", e_wd_prev.get(28 + ic)))
            ld_t = b.dma("sync", xqT_sem[ts_],
                         mk("dma_start", out=xqTb[ts_],
                            in_=s_xq2[k * 512:(k + 1) * 512, ic * 128:(ic + 1) * 128],
                            transpose=True), waits=tw)
            w = []
            if k == 0 and ic <= 1:
                w += e_rt_all
            if ic >= 2:
                w.append(("tM", e_wd[ic - 2]))
            elif k > 0:
                w.append(("tM", e_wd_prev.get(30 + ic)))
            ld = b.dma("sync", f"qWD{wsl}",
                       mk("dma_start", out=wdb[wsl],
                          in_=s_wd2t[ic * 128:(ic + 1) * 128, :]),
                       waits=w)
            em = None
            for tl in range(4):
                for eh in range(2):
                    bank = tl * 2 + eh
                    w0 = [(f"qWD{wsl}", ld), (xqT_sem[ts_], ld_t)] + (bw(bank) if ic == 0 else [])
                    em = b.op("tensor",
                              mk("matmul", pb[bank][:, :],
                                 xqTb[ts_][:, tl * 128:(tl + 1) * 128],
                                 wdb[wsl][:, eh * 512:(eh + 1) * 512],
                                 start=(ic == 0), stop=(ic == 31)),
                              inc="tM" if (tl == 3 and eh == 1) else None,
                              waits=w0)
            e_wd[ic] = em
        e_down_mm[k] = e_wd[31]
        e_wd_prev.clear()
        e_wd_prev.update(e_wd)
        for tl in range(4):
            tt = 4 * k + tl
            slot = tt % 2
            xi = x1t[slot]
            lw = [("vV", e_q1_last)]
            if tt >= 2:
                lw.append(("vV", e_prev_x2.get(tt - 2)))
            ld = None
            for q2 in range(2):
                ld = b.dma("sync", f"qX1{slot}",
                           mk("dma_start", out=xi[:, q2 * 512:(q2 + 1) * 512],
                              in_=s_x1[tt * 128:(tt + 1) * 128, q2 * 512:(q2 + 1) * 512]),
                           waits=lw if q2 == 0 else ())
            ev = None
            for eh in range(2):
                bank = tl * 2 + eh
                xw = [("tM", e_down_mm[k]), (f"qX1{slot}", ld)]
                if tt < 2:
                    xw += [("gP", e_p3_g), ("vV", e_p3_v)]
                if tt >= 2:
                    xw.append((f"qGO{slot}", e_out_store.get(tt - 2)))
                ev = b.op("vector", mk("scalar_tensor_tensor", x2t[slot][:, eh * 512:(eh + 1) * 512],
                                       pb[bank][:, :], a2d[:, tt:tt + 1],
                                       xi[:, eh * 512:(eh + 1) * 512], Alu.mult, Alu.add),
                          inc="vV", waits=xw)
                bank_gate[bank] = ("vV", ev)
            e_prev_x2[tt] = ev
            st = b.dma("gpsimd", f"qGO{slot}",
                       mk("dma_start", out=out_hbm[tt * 128:(tt + 1) * 128, :], in_=x2t[slot]),
                       waits=[("vV", ev)])
            e_out_store[tt] = st

    emit_xq1T(0)
    emit_gate_up(0)
    emit_quant2(0)
    emit_xq1T(1)
    emit_gate_up(1)
    emit_quant2(1)
    emit_down(0)
    emit_xq1T(2)
    emit_gate_up(2)
    emit_quant2(2)
    emit_down(1)
    emit_xq1T(3)
    emit_gate_up(3)
    emit_quant2(3)
    emit_down(2)
    emit_down(3)

    b.wait("scalar", "qGO0", b.cnt["qGO0"])
    b.wait("scalar", "qGO1", b.cnt["qGO1"])
    b.wait("sync", "qGO0", b.cnt["qGO0"])
    b.wait("sync", "qGO1", b.cnt["qGO1"])

    for name in SEMS:
        b.sems[name] = stack.enter_context(nc.semaphore(name))
    with nc.Block() as block:
        @block.sync
        def _(E):
            for th in b.ops["sync"]:
                th(E)

        @block.tensor
        def _(E):
            for th in b.ops["tensor"]:
                th(E)

        @block.vector
        def _(E):
            for th in b.ops["vector"]:
                th(E)

        @block.scalar
        def _(E):
            for th in b.ops["scalar"]:
                th(E)

        @block.gpsimd
        def _(E):
            for th in b.ops["gpsimd"]:
                th(E)

    stack.close()
    return nc


_NC_CACHE = None


def kernel(L0_attn_x, L0_attn_w_qkv, L0_attn_w_fc, L0_attn_b_fc, L0_attn_gamma,
           L0_attn_beta, L0_ffn_w_gate, L0_ffn_w_up, L0_ffn_w_down):
    global _NC_CACHE
    x = np.ascontiguousarray(np.asarray(L0_attn_x, dtype=np.float32))
    B_, S_, D_ = x.shape
    weights = {
        "w_qkv": np.ascontiguousarray(np.asarray(L0_attn_w_qkv, dtype=np.float32)),
        "w_fc": np.ascontiguousarray(np.asarray(L0_attn_w_fc, dtype=np.float32)),
        "b_fc": np.ascontiguousarray(np.asarray(L0_attn_b_fc, dtype=np.float32)).reshape(1, D),
        "gamma": np.ascontiguousarray(np.asarray(L0_attn_gamma, dtype=np.float32)).reshape(1, D),
        "beta": np.ascontiguousarray(np.asarray(L0_attn_beta, dtype=np.float32)).reshape(1, D),
        "w_gate": np.ascontiguousarray(np.asarray(L0_ffn_w_gate, dtype=np.float32)),
        "w_up": np.ascontiguousarray(np.asarray(L0_ffn_w_up, dtype=np.float32)),
        "w_down": np.ascontiguousarray(np.asarray(L0_ffn_w_down, dtype=np.float32)),
    }
    in_maps = []
    for c in range(8):
        bi, h = divmod(c, 2)
        own = x[bi, h * T:(h + 1) * T, :]
        oth = x[bi, (1 - h) * T:(2 - h) * T, :]
        xc = np.ascontiguousarray(np.concatenate([own, oth], axis=0))
        m = {"x_full": xc}
        m.update(weights)
        in_maps.append(m)

    if _NC_CACHE is None:
        nc_ = bass.Bass("TRN2", target_bir_lowering=False)
        build_program(nc_)
        _NC_CACHE = nc_
    res = run_bass_kernel_spmd(_NC_CACHE, in_maps, core_ids=list(range(8)))
    out = np.empty_like(x)
    for c in range(8):
        bi, h = divmod(c, 2)
        out[bi, h * T:(h + 1) * T, :] = res.results[c]["out"]
    return out


# version 24 (final)
# speedup vs baseline: 1.1937x; 1.0144x over previous
import numpy as np
import ml_dtypes
import concourse.bass as bass
import concourse.mybir as mybir
from concourse.bass_utils import run_bass_kernel_spmd
from contextlib import ExitStack

BFD = ml_dtypes.bfloat16
F32 = mybir.dt.float32
BF16 = mybir.dt.bfloat16
F16 = mybir.dt.float16
I32 = mybir.dt.int32
Alu = mybir.AluOpType
Act = mybir.ActivationFunctionType
AX = mybir.AxisListType.X

S, T, D, NE, NI, P = 4096, 2048, 1024, 3072, 4096, 128
KDBG = False
Q2ENG = "vector"
LN_EPS = 1e-5
INV32 = 1.0 / 32.0

ENGINES = ("sync", "tensor", "vector", "scalar", "gpsimd")
SERIAL = ("vector", "scalar", "gpsimd")

SEMS = (["tM", "vV", "sC", "gP", "sP0", "gC",
         "qM", "qMB", "qP1X", "qLN1T", "qGX1", "qGQ1", "qGX2", "qTR0", "qTR1"] +
        [f"{p}{i}" for p in ("qFCT", "qWN", "qWT", "qQT", "qXT", "qX1", "qQ1T", "qWG",
                             "qWU", "qQ2T", "qWD", "qA", "qAS", "qGLN", "qGQs",
                             "qGN", "qGO", "qTS") for i in (0, 1)])


def mk(meth, *args, **kw):
    def f(E):
        return getattr(E, meth)(*args, **kw)
    return f


class Builder:
    def __init__(self, sem_names):
        self.ops = {e: [] for e in ENGINES}
        self.cnt = {s: 0 for s in sem_names}
        self.sems = {}
        self.serial = {e: None for e in SERIAL}

    def op(self, eng, fn, inc=None, waits=(), n=1, is_dma=False):
        if inc is not None:
            self.cnt[inc] += n
        val = self.cnt[inc] if inc is not None else None
        waits = list(waits)
        if eng in SERIAL and not is_dma:
            m = self.serial.get(eng)
            if m:
                waits.append(m)
            if inc is not None:
                self.serial[eng] = (inc, val)
        waits = [(s, v) for (s, v) in waits if v is not None and v > 0]
        sems = self.sems
        inc_pair = (inc, n) if inc is not None else None

        def thunk(E):
            for s, v in waits:
                E.wait_ge(sems[s], v)
            ins = fn(E)
            if inc_pair is not None:
                ins.then_inc(sems[inc_pair[0]], inc_pair[1])
        self.ops[eng].append(thunk)
        return val

    def splice(self, eng, thunk, sem, val, pre_cell):
        if eng in SERIAL:
            pre_cell[0] = self.serial.get(eng)
            self.serial[eng] = (sem, val)
        self.ops[eng].append(thunk)

    def dma(self, eng, q, fn, waits=()):
        return self.op(eng, fn, inc=q, waits=waits, n=16, is_dma=True)

    def wait(self, eng, sem, val):
        if val is None or val <= 0:
            return
        sems = self.sems

        def thunk(E):
            E.wait_ge(sems[sem], val)
        self.ops[eng].append(thunk)


def build_program(nc):
    b = Builder(SEMS)

    x_hbm = nc.dram_tensor("x_full", [S, D], F32, kind="ExternalInput").ap()
    wqkv_hbm = nc.dram_tensor("w_qkv", [NE, D], F32, kind="ExternalInput").ap()
    wfc_hbm = nc.dram_tensor("w_fc", [D, D], F32, kind="ExternalInput").ap()
    bfc_hbm = nc.dram_tensor("b_fc", [1, D], F32, kind="ExternalInput").ap()
    gam_hbm = nc.dram_tensor("gamma", [1, D], F32, kind="ExternalInput").ap()
    bet_hbm = nc.dram_tensor("beta", [1, D], F32, kind="ExternalInput").ap()
    w3_hbm = [nc.dram_tensor("w_gate", [NI, D], F32, kind="ExternalInput").ap(),
              nc.dram_tensor("w_up", [NI, D], F32, kind="ExternalInput").ap(),
              nc.dram_tensor("w_down", [D, NI], F32, kind="ExternalInput").ap()]
    out_hbm = nc.dram_tensor("out", [T, D], F32, kind="ExternalOutput").ap()

    s_ln1 = nc.dram_tensor("s_ln1", [S, D], F16, kind="Internal").ap()
    s_q = nc.dram_tensor("s_q", [D, T], F16, kind="Internal").ap()
    s_wqkv = nc.dram_tensor("s_wqkv", [NE, D], F16, kind="Internal").ap()
    s_wv_lo = nc.dram_tensor("s_wv_lo", [D, D], F16, kind="Internal").ap()
    s_wfc = nc.dram_tensor("s_wfc", [D, D], F16, kind="Internal").ap()
    s_wfc_lo = nc.dram_tensor("s_wfc_lo", [D, D], F16, kind="Internal").ap()
    s_w3 = [nc.dram_tensor("s_wg2", [NI, D], BF16, kind="Internal").ap(),
            nc.dram_tensor("s_wu2", [NI, D], BF16, kind="Internal").ap(),
            nc.dram_tensor("s_wd2", [D, NI], F16, kind="Internal").ap()]
    s_wg2t = nc.dram_tensor("s_wg2t", [D, NI], BF16, kind="Internal").ap()
    s_wu2t = nc.dram_tensor("s_wu2t", [D, NI], BF16, kind="Internal").ap()
    s_wd2t = nc.dram_tensor("s_wd2t", [NI, D], F16, kind="Internal").ap()
    s_x1 = nc.dram_tensor("s_x1", [T, D], F32, kind="Internal").ap()
    s_xq1 = nc.dram_tensor("s_xq1", [T, D], BF16, kind="Internal").ap()
    s_xq2 = nc.dram_tensor("s_xq2", [T, NI], F16, kind="Internal").ap()
    s_dbg = nc.dram_tensor("s_dbg", [P, 5120], F32, kind="Internal").ap() if KDBG else None
    s_dby = nc.dram_tensor("s_dby", [T, NI], F16, kind="Internal").ap() if KDBG else None

    stack = ExitStack()
    sb = lambda name, shape, dt: stack.enter_context(nc.sbuf_tensor(name, shape, dt))
    psa = lambda name: stack.enter_context(nc.psum_tensor(name, [P, 512], F32))

    AR1 = sb("AR1", [P, 32768], BF16)
    AR2 = sb("AR2", [P, 32768], BF16)
    AR3 = sb("AR3", [P, 9216], BF16)
    AR4 = sb("AR4", [P, 4096], F32)

    ident = sb("ident", [P, P], F16)
    iota_a = sb("iota_a", [P, P], F32)
    iota_b = sb("iota_b", [P, P], F32)
    ones_rb = sb("ones_rb", [1, P], F16)
    ones_rf = sb("ones_rf", [1, P], F32)
    ones_cf = sb("ones_cf", [P, 1], F32)
    bfc_hi = sb("bfc_hi", [1, D], F16)
    bfc_lo = sb("bfc_lo", [1, D], F16)
    colfc_b = sb("colfc_b", [1, D], F16)
    colfc_lo = sb("colfc_lo", [1, D], F16)
    colsum_c = sb("colsum_c", [P, 8], F32)
    lnsum_c = sb("lnsum_c", [P, 8], F32)
    lncol_hi = sb("lncol_hi", [P, 8], F16)
    lncol_lo = sb("lncol_lo", [P, 8], F16)
    colsum_cb = sb("colsum_cb", [P, 8], F16)
    colsum_lo = sb("colsum_lo", [P, 8], F16)
    mu1 = sb("mu1", [P, 16], F32)
    r1c = sb("r1c", [P, 16], F32)
    mur2 = sb("mur2", [P, 32], F32)
    musq = sb("musq", [P, 16], F32)
    sqs = sb("sqs", [P, 16], F32)
    a1g = sb("a1g", [P, 16], F32)
    a1u = sb("a1u", [P, 16], F32)
    a2d = sb("a2d", [P, 16], F32)
    s1c = sb("s1c", [P, 16], F32)
    s2c = sb("s2c", [P, 16], F32)
    am2 = sb("am2", [P, 8], F32)
    svec = sb("svec", [P, 32], F32)
    wacc = sb("wacc", [P, 4], F32)
    tcols = sb("tcols", [P, 12], F32)
    rowt = sb("rowt", [1, 16], F32)
    rs_row = sb("rs_row", [1, P], F16)
    rs_col = sb("rs_col", [P, 1], F16)
    rs_row2 = sb("rs_row2", [1, P], F16)
    rs_col2 = sb("rs_col2", [P, 1], F16)
    rs_f = sb("rs_f", [P, 1], F32)
    xstage = sb("xstage", [P, 2048], F32)
    wstg = sb("wstg", [P, 2048], F32)
    p0scr = sb("p0scr", [P, 4096], BF16)
    p0f = sb("p0f", [P, 512], F32)
    qi = sb("qi", [P, 1024], I32)
    fcB = sb("fcB", [P, 1024], F32)

    pb = [psa(f"pb{i}") for i in range(8)]
    bank_gate = [None] * 8

    def bw(i):
        return [bank_gate[i]] if bank_gate[i] else []

    gam_b = AR1[:, 30720:31744].bitcast(F16)
    bet_b = AR1[:, 31744:32768].bitcast(F16)
    gam_lo = AR1[:, 28672:29696].bitcast(F16)
    bet_lo = AR1[:, 29696:30720].bitcast(F16)
    brow = xstage[0:1, 0:1024]

    b.op("gpsimd", mk("iota", iota_a[:, :], [[1, P]], channel_multiplier=0,
                      allow_small_or_imprecise_dtypes=True), inc="gP")
    e = b.op("gpsimd", mk("iota", iota_b[:, :], [[1, P]], channel_multiplier=1,
                          allow_small_or_imprecise_dtypes=True), inc="gP")
    b.op("vector", mk("tensor_sub", iota_b[:, :], iota_b[:, :], iota_a[:, :]),
         inc="vV", waits=[("gP", e)])
    b.op("vector", mk("tensor_tensor", ident[:, :], iota_b[:, :], iota_a[:, :], Alu.is_equal), inc="vV")
    b.op("vector", mk("memset", ones_rb[:, :], 1.0), inc="vV")
    b.op("vector", mk("memset", ones_rf[:, :], 1.0), inc="vV")
    b.op("vector", mk("memset", ones_cf[:, :], 1.0), inc="vV")
    b.op("vector", mk("memset", colsum_c[:, :], 0.0), inc="vV")
    b.op("vector", mk("memset", wacc[:, :], 0.0), inc="vV")
    e_init_v = b.cnt["vV"]

    e_bv = b.dma("sync", "qM", mk("dma_start", out=brow[:, :], in_=bfc_hbm[:, :]))
    b.op("vector", mk("tensor_copy", bfc_hi[:, :], brow[:, :]), inc="vV", waits=[("qM", e_bv)])
    e_bfc = b.op("vector", mk("scalar_tensor_tensor", bfc_lo[:, :], brow[:, :], 1.0, bfc_hi[:, :],
                              Alu.mult, Alu.subtract), inc="vV")

    p0_scrap = p0f[:, :]
    sgA = p0scr[:, 0:1024]
    sgB = p0scr[:, 1024:2048]
    qout = [p0scr[:, 2048:3072], p0scr[:, 3072:4096]]

    def w3_tile(m, j):
        if m < 2:
            return w3_hbm[m][j * 128:(j + 1) * 128, :]
        r, qc = divmod(j, 4)
        return w3_hbm[2][r * 128:(r + 1) * 128, qc * 1024:(qc + 1) * 1024]

    def s3_tile(m, j):
        if m < 2:
            return s_w3[m][j * 128:(j + 1) * 128, :]
        r, qc = divmod(j, 4)
        return s_w3[2][r * 128:(r + 1) * 128, qc * 1024:(qc + 1) * 1024]

    p0b_abs = []
    p0b_tiles_done = [0]

    def p0b_tiles_chunk(n):
        while p0b_tiles_done[0] < 96 and n > 0:
            idx = p0b_tiles_done[0]
            m, j = divmod(idx, 32)
            slot = idx % 2
            stg = wstg[:, slot * 1024:(slot + 1) * 1024]
            waits = []
            if idx >= 2:
                waits.append(("vV", p0b_abs[idx - 2]))
            src = w3_tile(m, j)
            ld = None
            for q4 in range(4):
                ld = b.dma("scalar", f"qA{slot}",
                           mk("dma_start", out=stg[:, q4 * 256:(q4 + 1) * 256],
                              in_=src[:, q4 * 256:(q4 + 1) * 256]),
                           waits=waits if q4 == 0 else ())
            aw = [(f"qA{slot}", ld)]
            if idx >= 2:
                aw.append(("gC", 2 * (idx - 2) + 2))
            b.op("vector", mk("scalar_tensor_tensor", p0_scrap, stg[:, 0:512], -1.0,
                              stg[:, 0:512], Alu.mult, Alu.max,
                              accum_out=svec[:, 20 + slot:21 + slot]),
                 inc="vV", waits=aw)
            ab = b.op("vector", mk("scalar_tensor_tensor", p0_scrap, stg[:, 512:1024], -1.0,
                                   stg[:, 512:1024], Alu.mult, Alu.max,
                                   accum_out=svec[:, 22 + slot:23 + slot]), inc="vV")
            p0b_abs.append(ab)
            p0b_tiles_done[0] += 1
            n -= 1

    p0b_emitted = [0]

    def p0b_add_chunk(n):
        while p0b_emitted[0] < len(p0b_abs) and n > 0:
            idx = p0b_emitted[0]
            m = idx // 32
            slot = idx % 2
            aw0 = [("vV", p0b_abs[idx])]
            if idx == 0:
                aw0.append(("vV", e_init_v))
            b.op("gpsimd", mk("tensor_add", wacc[:, m:m + 1], wacc[:, m:m + 1],
                              svec[:, 20 + slot:21 + slot]),
                 inc="gC", waits=aw0)
            b.op("gpsimd", mk("tensor_add", wacc[:, m:m + 1], wacc[:, m:m + 1],
                              svec[:, 22 + slot:23 + slot]), inc="gC")
            p0b_emitted[0] += 1
            n -= 1

    scale_ready = [None, None, None]

    def p0_scale(m):
        em = b.op("tensor", mk("matmul", pb[7][0:1, 0:1], wacc[:, m:m + 1], ones_cf[:, :],
                               start=True, stop=True),
                  inc="tM", waits=[("gC", (m + 1) * 64)] + bw(7))
        ev = b.op("vector", mk("tensor_copy", rowt[0:1, m:m + 1], pb[7][0:1, 0:1]),
                  inc="vV", waits=[("tM", em)])
        bank_gate[7] = ("vV", ev)
        b.op("vector", mk("tensor_scalar", rowt[0:1, 4 + m:5 + m], rowt[0:1, m:m + 1],
                          1.0 / (NI * D), 1e-5, Alu.mult, Alu.max), inc="vV")
        vv_now = b.cnt["vV"]
        em2 = b.op("tensor", mk("matmul", pb[7][:, 0:1], ones_rf[:, :], rowt[0:1, 4 + m:5 + m],
                                start=True, stop=True),
                   inc="tM", waits=[("vV", vv_now)] + bw(7))
        ev2 = b.op("vector", mk("tensor_copy", tcols[:, m:m + 1], pb[7][:, 0:1]),
                   inc="vV", waits=[("tM", em2)])
        bank_gate[7] = ("vV", ev2)
        b.op("vector", mk("tensor_single_scalar", tcols[:, 3 + m:4 + m], tcols[:, m:m + 1],
                          0.5, Alu.mult), inc="vV")
        ev3 = b.op("vector", mk("tensor_single_scalar", tcols[:, 6 + m:7 + m], tcols[:, 3 + m:4 + m],
                                -1.0, Alu.mult), inc="vV")
        scale_ready[m] = ev3

    p0c_state = {"i": 0, "signB": {}, "comb": {}, "st": {0: [], 1: []}}

    def p0c_tiles(n):
        while p0c_state["i"] < 96 and n > 0:
            idx = p0c_state["i"]
            m, j = divmod(idx, 32)
            slot = idx % 2
            stg = wstg[:, slot * 1024:(slot + 1) * 1024]
            lw = [("vV", p0b_abs[95])]
            if idx >= 2:
                lw.append(("vV", p0c_state["signB"][idx - 2]))
            src = w3_tile(m, j)
            ld = None
            for q4 in range(4):
                ld = b.dma("scalar", f"qA{slot}",
                           mk("dma_start", out=stg[:, q4 * 256:(q4 + 1) * 256],
                              in_=src[:, q4 * 256:(q4 + 1) * 256]),
                           waits=lw if q4 == 0 else ())
            wA = [(f"qA{slot}", ld), ("vV", scale_ready[m])]
            qv = qout[slot] if m < 2 else qout[slot].bitcast(F16)
            b.op("vector", mk("tensor_scalar", sgA[:, :], stg, tcols[:, 3 + m:4 + m], 2.0,
                              Alu.is_gt, Alu.mult), inc="vV", waits=wA)
            vB = b.op("vector", mk("tensor_scalar", sgB[:, :], stg, tcols[:, 6 + m:7 + m], 2.0,
                                   Alu.is_ge, Alu.mult), inc="vV")
            p0c_state["signB"][idx] = vB
            wC = []
            if len(p0c_state["st"][slot]) >= 1:
                wC.append((f"qAS{slot}", p0c_state["st"][slot][-1]))
            vC = b.op("vector", mk("scalar_tensor_tensor", qv, sgA[:, :], -2.0, sgB[:, :],
                                   Alu.add, Alu.add), inc="vV", waits=wC)
            p0c_state["comb"][idx] = vC
            st = b.dma("scalar", f"qAS{slot}", mk("dma_start", out=s3_tile(m, j), in_=qv),
                       waits=[("vV", vC)])
            p0c_state["st"][slot].append(st)
            p0c_state["i"] += 1
            n -= 1

    p0b_tiles_chunk(2)
    p0b_add_chunk(2)
    scr1 = AR3[:, 0:1024]
    e_norm_done = {}
    e_lnsum_mm = [None]
    e_lnsum_slot = {0: None, 1: None}
    for g in range(8):
        if g < 4:
            mu_dst = mu1[:, g * 4:(g + 1) * 4]
            r_dst = r1c[:, g * 4:(g + 1) * 4]
        else:
            mu_dst = mur2[:, (g - 4) * 4:(g - 3) * 4]
            r_dst = mur2[:, 16 + (g - 4) * 4:16 + (g - 3) * 4]
        for i in range(4):
            t_idx = g * 4 + i
            xt = AR4[:, i * 1024:(i + 1) * 1024]
            w = [("vV", e_norm_done[t_idx - 4])] if t_idx >= 4 else []
            for q2 in range(2):
                b.dma("gpsimd", "qP1X",
                      mk("dma_start", out=xt[:, q2 * 512:(q2 + 1) * 512],
                         in_=x_hbm[t_idx * 128:(t_idx + 1) * 128, q2 * 512:(q2 + 1) * 512]),
                      waits=w if q2 == 0 else ())
        grp_total = b.cnt["qP1X"]
        for i in range(4):
            xt = AR4[:, i * 1024:(i + 1) * 1024]
            b.op("vector", mk("reduce_sum", musq[:, i:i + 1], xt, AX),
                 inc="vV", waits=[("qP1X", grp_total)] if i == 0 else ())
            b.op("vector", mk("scalar_tensor_tensor", scr1, xt, 1.0, xt,
                              Alu.mult, Alu.mult, accum_out=sqs[:, i:i + 1]), inc="vV")
        b.op("vector", mk("tensor_single_scalar", mu_dst, musq[:, 0:4], 1.0 / D, Alu.mult), inc="vV")
        b.op("vector", mk("tensor_single_scalar", sqs[:, 0:4], sqs[:, 0:4], 1.0 / D, Alu.mult), inc="vV")
        b.op("vector", mk("scalar_tensor_tensor", svec[:, 0:4], mu_dst, 1.0,
                          mu_dst, Alu.mult, Alu.mult), inc="vV")
        b.op("vector", mk("tensor_sub", svec[:, 0:4], sqs[:, 0:4], svec[:, 0:4]), inc="vV")
        b.op("vector", mk("tensor_single_scalar", svec[:, 0:4], svec[:, 0:4], LN_EPS, Alu.add), inc="vV")
        ev_ve = b.op("vector", mk("reciprocal", musq[:, 0:4], svec[:, 0:4]), inc="vV")
        es = b.op("scalar", mk("activation", r_dst, musq[:, 0:4], Act.Sqrt),
                  inc="sC", waits=[("vV", ev_ve)])
        first = True
        for _ in range(2):
            b.op("vector", mk("scalar_tensor_tensor", sqs[:, 0:4], r_dst, 1.0,
                              r_dst, Alu.mult, Alu.mult), inc="vV",
                 waits=[("sC", es)] if first else ())
            first = False
            b.op("vector", mk("tensor_mul", sqs[:, 0:4], sqs[:, 0:4], svec[:, 0:4]), inc="vV")
            b.op("vector", mk("tensor_scalar", sqs[:, 0:4], sqs[:, 0:4], -0.5, 1.5,
                              Alu.mult, Alu.add), inc="vV")
            b.op("vector", mk("tensor_mul", r_dst, r_dst, sqs[:, 0:4]), inc="vV")
        for i in range(4):
            t_idx = g * 4 + i
            xt = AR4[:, i * 1024:(i + 1) * 1024]
            slot = t_idx % 2
            lnf = xstage[:, slot * 1024:(slot + 1) * 1024]
            waits = []
            if e_lnsum_slot[slot] is not None:
                waits.append(("tM", e_lnsum_slot[slot]))
            ev_f = b.op("vector", mk("tensor_scalar", lnf, xt,
                                     mu_dst[:, i:i + 1], r_dst[:, i:i + 1],
                                     Alu.subtract, Alu.mult), inc="vV", waits=waits)
            e_norm_done[t_idx] = ev_f
            em5 = None
            for ds in range(8):
                em5 = b.op("tensor", mk("matmul", pb[5][:, ds:ds + 1],
                                        lnf[:, ds * 128:(ds + 1) * 128], ones_cf[:, :],
                                        start=(t_idx == 0 and ds == 0), stop=False,
                                        skip_group_check=True),
                           inc="tM" if ds == 7 else None,
                           waits=[("vV", ev_f)] + (bw(5) if t_idx == 0 and ds == 0 else []))
            e_lnsum_slot[slot] = em5
            e_lnsum_mm[0] = em5
            p0b_tiles_chunk(3)
            p0b_add_chunk(3)
    p0b_tiles_chunk(96)
    p0b_add_chunk(96)
    ev5 = b.op("vector", mk("tensor_copy", lnsum_c[:, :], pb[5][:, 0:8]),
               inc="vV", waits=[("tM", e_lnsum_mm[0])])
    bank_gate[5] = ("vV", ev5)
    b.op("vector", mk("tensor_copy", lncol_hi[:, :], lnsum_c[:, :]), inc="vV")
    b.op("vector", mk("scalar_tensor_tensor", lncol_lo[:, :], lnsum_c[:, :], 1.0,
                      lncol_hi[:, :], Alu.mult, Alu.subtract), inc="vV")
    e_p1_v = b.cnt["vV"]

    cast_a = AR3[:, 6144:7168].bitcast(F16)
    cast_b = AR3[:, 7168:8192].bitcast(F16)
    e_nat_store = {0: [], 1: []}
    e_cast = []
    tiles = [("qkv", j, wqkv_hbm, s_wqkv) for j in range(16, 24)] + \
            [("fc", j, wfc_hbm, s_wfc) for j in range(8)]
    for idx, (which, j, src, dst) in enumerate(tiles):
        slot = idx % 2
        stg = xstage[:, slot * 1024:(slot + 1) * 1024]
        waits = []
        if idx < 2:
            waits.append(("vV", e_bfc))
            waits.append(("tM", e_lnsum_mm[0]))
        if idx >= 2:
            waits.append(("vV", e_cast[idx - 2]))
        ld = None
        for q4 in range(4):
            ld = b.dma("sync", f"qWN{slot}",
                       mk("dma_start", out=stg[:, q4 * 256:(q4 + 1) * 256],
                          in_=src[j * 128:(j + 1) * 128, q4 * 256:(q4 + 1) * 256]),
                       waits=waits if q4 == 0 else ())
        cst = cast_a if slot == 0 else cast_b
        cw = [(f"qWN{slot}", ld)]
        if idx < 2:
            cw.append(("vV", e_p1_v))
        if idx >= 2:
            cw.append((f"qGN{slot}", e_nat_store[slot][-1]))
        ev = b.op("vector", mk("tensor_copy", cst, stg), inc="vV", waits=cw)
        st = b.dma("gpsimd", f"qGN{slot}", mk("dma_start", out=dst[j * 128:(j + 1) * 128, :], in_=cst),
                   waits=[("vV", ev)])
        if which == "fc" or (which == "qkv" and j >= 16):
            lodst = s_wfc_lo if which == "fc" else s_wv_lo
            jj = j if which == "fc" else j - 16
            ev = b.op("vector", mk("scalar_tensor_tensor", cst, stg, 1.0, cst,
                                   Alu.mult, Alu.subtract), inc="vV",
                      waits=[(f"qGN{slot}", st)])
            st = b.dma("gpsimd", f"qGN{slot}",
                       mk("dma_start", out=lodst[jj * 128:(jj + 1) * 128, :], in_=cst),
                       waits=[("vV", ev)])
        e_cast.append(ev)
        e_nat_store[slot].append(st)
    e_cast_last = e_cast[-1]
    e_qgn_all = [b.cnt["qGN0"], b.cnt["qGN1"]]

    p0_scale(0)
    p0c_tiles(16)
    p0_scale(1)
    p0c_tiles(28)
    p0_scale(2)
    p0c_tiles(28)
    p0c_tiles(24)

    e_qas_all = [b.cnt["qAS0"], b.cnt["qAS1"]]
    e_grps = b.cnt["tM"]
    e_vmm = {}
    emv = None
    for ds in range(8):
        lw = []
        if ds == 0:
            lw += [("qGN0", e_qgn_all[0]), ("qGN1", e_qgn_all[1]), ("vV", e_cast_last),
                   ("vV", e_p1_v)]
        if ds >= 1:
            lw.append(("tM", e_vmm[ds - 1]))
        ldh = b.dma("sync", "qFCT0",
                    mk("dma_start", out=cast_a,
                       in_=s_wqkv[2 * D:3 * D, ds * 128:(ds + 1) * 128], transpose=True),
                    waits=lw)
        ldl = b.dma("sync", "qFCT1",
                    mk("dma_start", out=cast_b,
                       in_=s_wv_lo[:, ds * 128:(ds + 1) * 128], transpose=True),
                    waits=lw if ds == 0 else ())
        mw0 = [("qFCT0", ldh), ("qFCT1", ldl)]
        if ds == 0:
            mw0 += bw(6)
        for sub in range(8):
            first = (ds == 0 and sub == 0)
            b.op("tensor", mk("matmul", pb[6][:, sub:sub + 1],
                              cast_a[:, sub * 128:(sub + 1) * 128], lncol_hi[:, ds:ds + 1],
                              start=first, stop=False, skip_group_check=True),
                 waits=mw0 if sub == 0 else ())
            b.op("tensor", mk("matmul", pb[6][:, sub:sub + 1],
                              cast_a[:, sub * 128:(sub + 1) * 128], lncol_lo[:, ds:ds + 1],
                              start=False, stop=False, skip_group_check=True))
            emv = b.op("tensor", mk("matmul", pb[6][:, sub:sub + 1],
                                    cast_b[:, sub * 128:(sub + 1) * 128], lncol_hi[:, ds:ds + 1],
                                    start=False, stop=False, skip_group_check=True),
                       inc="tM" if sub == 7 else None)
        e_vmm[ds] = emv
    ev6 = b.op("vector", mk("tensor_copy", colsum_c[:, :], pb[6][:, 0:8]),
               inc="vV", waits=[("tM", emv)])
    bank_gate[6] = ("vV", ev6)
    b.op("vector", mk("tensor_copy", colsum_cb[:, :], colsum_c[:, :]), inc="vV")
    b.op("vector", mk("scalar_tensor_tensor", colsum_lo[:, :], colsum_c[:, :], 1.0,
                      colsum_cb[:, :], Alu.mult, Alu.subtract), inc="vV")
    e_cc = b.cnt["vV"]
    e_fcmm = {}
    em3 = None
    for dc in range(8):
        lw = []
        if dc == 0:
            lw += [("qGN0", e_qgn_all[0]), ("qGN1", e_qgn_all[1]), ("vV", e_cast_last),
                   ("tM", e_vmm[7])]
        if dc >= 1:
            lw.append(("tM", e_fcmm[dc - 1]))
        ldf = b.dma("sync", "qFCT0",
                    mk("dma_start", out=cast_a,
                       in_=s_wfc[:, dc * 128:(dc + 1) * 128], transpose=True), waits=lw)
        ldf2 = b.dma("sync", "qFCT1",
                     mk("dma_start", out=cast_b,
                        in_=s_wfc_lo[:, dc * 128:(dc + 1) * 128], transpose=True),
                     waits=lw if dc == 0 else ())
        mw = [("qFCT0", ldf), ("qFCT1", ldf2)]
        if dc == 0:
            mw += [("vV", e_cc)] + bw(2) + bw(3)
        b.op("tensor", mk("matmul", pb[2][0:1, :], colsum_cb[:, dc:dc + 1],
                          cast_a[:, 0:512], start=(dc == 0), stop=False),
             waits=mw)
        b.op("tensor", mk("matmul", pb[2][0:1, :], colsum_lo[:, dc:dc + 1],
                          cast_a[:, 0:512], start=False, stop=False))
        b.op("tensor", mk("matmul", pb[2][0:1, :], colsum_cb[:, dc:dc + 1],
                          cast_b[:, 0:512], start=False, stop=(dc == 7)))
        b.op("tensor", mk("matmul", pb[3][0:1, :], colsum_cb[:, dc:dc + 1],
                          cast_a[:, 512:1024], start=(dc == 0), stop=False))
        b.op("tensor", mk("matmul", pb[3][0:1, :], colsum_lo[:, dc:dc + 1],
                          cast_a[:, 512:1024], start=False, stop=False))
        em3 = b.op("tensor", mk("matmul", pb[3][0:1, :], colsum_cb[:, dc:dc + 1],
                                cast_b[:, 512:1024], start=False, stop=(dc == 7)),
                   inc="tM")
        e_fcmm[dc] = em3
    e_colfc = None
    for eh in range(2):
        b.op("vector", mk("tensor_copy", colfc_b[0:1, eh * 512:(eh + 1) * 512],
                          pb[2 + eh][0:1, :]), inc="vV", waits=[("tM", em3)])
        e_colfc = b.op("vector", mk("scalar_tensor_tensor", colfc_lo[0:1, eh * 512:(eh + 1) * 512],
                                    pb[2 + eh][0:1, :], 1.0,
                                    colfc_b[0:1, eh * 512:(eh + 1) * 512],
                                    Alu.mult, Alu.subtract), inc="vV")
        bank_gate[2 + eh] = ("vV", e_colfc)
    e_p2_tensor = b.cnt["tM"]

    rt_slots = [AR3[:, 5120:7168], AR3[:, 7168:9216]]
    rt_items = ([("gu", mtx, ds, half) for mtx in (0, 1) for ds in range(8)
                 for half in range(2)] +
                [("wd", ic, 0, 0) for ic in range(32)])
    rt_state = {"i": 0, "st": {}}

    def rt_emit(n):
        while rt_state["i"] < len(rt_items) and n > 0:
            i = rt_state["i"]
            kind, a1, a2, a3 = rt_items[i]
            sl = i % 2
            slot = rt_slots[sl]
            lw = []
            if i == 0:
                lw += [("qAS0", e_qas_all[0]), ("qAS1", e_qas_all[1]),
                       ("tM", e_p2_tensor)]
            if i >= 2:
                lw.append((f"qTS{sl}", rt_state["st"][i - 2]))
            if kind == "gu":
                src = s_w3[a1]
                dst = s_wg2t if a1 == 0 else s_wu2t
                ldr = None
                for q in range(2):
                    ldr = b.dma("scalar", f"qTR{sl}",
                                mk("dma_start", out=slot[:, q * 1024:(q + 1) * 1024],
                                   in_=src[a3 * 2048 + q * 1024:a3 * 2048 + (q + 1) * 1024,
                                           a2 * 128:(a2 + 1) * 128],
                                   transpose=True), waits=lw if q == 0 else ())
                st = b.dma("scalar", f"qTS{sl}",
                           mk("dma_start",
                              out=dst[a2 * 128:(a2 + 1) * 128, a3 * 2048:(a3 + 1) * 2048],
                              in_=slot), waits=[(f"qTR{sl}", ldr)])
            else:
                sl16 = slot[:, 0:1024].bitcast(F16)
                ldr = b.dma("scalar", f"qTR{sl}",
                            mk("dma_start", out=sl16,
                               in_=s_w3[2][:, a1 * 128:(a1 + 1) * 128],
                               transpose=True), waits=lw)
                st = b.dma("scalar", f"qTS{sl}",
                           mk("dma_start", out=s_wd2t[a1 * 128:(a1 + 1) * 128, :],
                              in_=sl16), waits=[(f"qTR{sl}", ldr)])
            rt_state["st"][i] = st
            rt_state["i"] += 1
            n -= 1

    gbrow = fcB[0:1, 0:1024]
    e_gamld = b.dma("gpsimd", "qMB", mk("dma_start", out=gbrow, in_=gam_hbm[:, :]),
                    waits=[("vV", e_bfc), ("tM", e_grps), ("vV", e_p1_v)])
    ev = None
    em = None
    for eh in range(2):
        em = b.op("tensor", mk("matmul", pb[7][:, :], ones_rf[:, :],
                               gbrow[0:1, eh * 512:(eh + 1) * 512], start=True, stop=True),
                  inc="tM", waits=[("qMB", e_gamld)] + bw(7))
        b.op("vector", mk("tensor_copy", gam_b[:, eh * 512:(eh + 1) * 512], pb[7][:, :]),
             inc="vV", waits=[("tM", em), ("tM", e_grps)])
        ev = b.op("vector", mk("scalar_tensor_tensor", gam_lo[:, eh * 512:(eh + 1) * 512],
                               pb[7][:, :], 1.0, gam_b[:, eh * 512:(eh + 1) * 512],
                               Alu.mult, Alu.subtract), inc="vV")
        bank_gate[7] = ("vV", ev)
    e_betld = b.dma("gpsimd", "qMB", mk("dma_start", out=gbrow, in_=bet_hbm[:, :]),
                    waits=[("tM", em)])
    for eh in range(2):
        em = b.op("tensor", mk("matmul", pb[7][:, :], ones_rf[:, :],
                               gbrow[0:1, eh * 512:(eh + 1) * 512], start=True, stop=True),
                  inc="tM", waits=[("qMB", e_betld)] + bw(7))
        b.op("vector", mk("tensor_copy", bet_b[:, eh * 512:(eh + 1) * 512], pb[7][:, :]),
             inc="vV", waits=[("tM", em)])
        ev = b.op("vector", mk("scalar_tensor_tensor", bet_lo[:, eh * 512:(eh + 1) * 512],
                               pb[7][:, :], 1.0, bet_b[:, eh * 512:(eh + 1) * 512],
                               Alu.mult, Alu.subtract), inc="vV")
        bank_gate[7] = ("vV", ev)
    e_gambet_v = ev

    fc_bc = fcB[:, :]
    ev_fb = None
    for eh in range(2):
        w0 = [("vV", e_colfc), ("vV", e_bfc), ("vV", e_gambet_v)] + bw(4)
        b.op("tensor", mk("matmul", pb[4][:, :], ones_rb[:, :],
                          colfc_b[0:1, eh * 512:(eh + 1) * 512], start=True, stop=False),
             waits=w0)
        b.op("tensor", mk("matmul", pb[4][:, :], ones_rb[:, :],
                          colfc_lo[0:1, eh * 512:(eh + 1) * 512], start=False, stop=False))
        b.op("tensor", mk("matmul", pb[4][:, :], ones_rb[:, :],
                          bfc_hi[0:1, eh * 512:(eh + 1) * 512], start=False, stop=False))
        em = b.op("tensor", mk("matmul", pb[4][:, :], ones_rb[:, :],
                               bfc_lo[0:1, eh * 512:(eh + 1) * 512], start=False, stop=True),
                  inc="tM")
        ev_fb = b.op("vector", mk("tensor_copy", fc_bc[:, eh * 512:(eh + 1) * 512],
                                  pb[4][:, :]), inc="vV", waits=[("tM", em)])
        bank_gate[4] = ("vV", ev_fb)
    e_fcbc = ev_fb
    if KDBG:
        edf = b.dma("gpsimd", "qMB", mk("dma_start", out=s_dbg[:, 0:1024],
                                        in_=fc_bc), waits=[("vV", e_fcbc)])
        b.wait("vector", "qMB", edf)

    t2scr = AR4[:, 2048:3072]
    sqscr = t2scr
    ln1t = AR4[:, 0:1024]
    hbuf = AR4[:, 1024:2048]
    prev_ep = {}

    for qt in range(16):
        qslot = qt % 2
        xt = xstage[:, qslot * 1024:(qslot + 1) * 1024]
        xw = [("vV", e_cast_last)]
        if qt >= 2:
            xw.append(("vV", prev_ep[qt - 2]["x_done"]))
        ld_x = None
        for q2 in range(2):
            ld_x = b.dma("sync", f"qXT{qslot}",
                         mk("dma_start", out=xt[:, q2 * 512:(q2 + 1) * 512],
                            in_=x_hbm[qt * 128:(qt + 1) * 128, q2 * 512:(q2 + 1) * 512]),
                         waits=xw if q2 == 0 else ())
        lw = [(f"qXT{qslot}", ld_x), ("vV", e_fcbc)]
        ev_ln1t = b.op("vector", mk("tensor_scalar", ln1t, xt, mu1[:, qt:qt + 1], r1c[:, qt:qt + 1],
                                    Alu.subtract, Alu.mult), inc="vV", waits=lw)
        b.op("gpsimd", mk("tensor_tensor", xt, xt, bet_b[:, :], Alu.add), inc="gP",
             waits=[("vV", ev_ln1t)])
        eg_x4 = b.op("gpsimd", mk("tensor_tensor", xt, xt, bet_lo[:, :], Alu.add), inc="gP")
        hw = []
        if qt >= 1:
            hw.append(("qGX1", 16 * qt))
        b.op("vector", mk("tensor_add", hbuf, ln1t, fc_bc), inc="vV", waits=hw)
        b.op("vector", mk("reduce_sum", svec[:, 8:9], hbuf, AX), inc="vV")
        b.op("vector", mk("scalar_tensor_tensor", sqscr, hbuf, 1.0, hbuf,
                          Alu.mult, Alu.mult, accum_out=svec[:, 9:10]), inc="vV")
        b.op("vector", mk("tensor_single_scalar", svec[:, 10:11], svec[:, 8:9], 1.0 / D, Alu.mult), inc="vV")
        b.op("vector", mk("tensor_single_scalar", svec[:, 11:12], svec[:, 9:10], 1.0 / D, Alu.mult), inc="vV")
        b.op("vector", mk("scalar_tensor_tensor", svec[:, 12:13], svec[:, 10:11], 1.0,
                          svec[:, 10:11], Alu.mult, Alu.mult), inc="vV")
        b.op("vector", mk("tensor_sub", svec[:, 12:13], svec[:, 11:12], svec[:, 12:13]), inc="vV")
        b.op("vector", mk("tensor_single_scalar", svec[:, 12:13], svec[:, 12:13], LN_EPS, Alu.add), inc="vV")
        ev_r = b.op("vector", mk("reciprocal", svec[:, 13:14], svec[:, 12:13]), inc="vV")
        es_s = b.op("scalar", mk("activation", svec[:, 14:15], svec[:, 13:14], Act.Sqrt),
                    inc="sC", waits=[("vV", ev_r)])
        first = True
        for _ in range(2):
            b.op("vector", mk("scalar_tensor_tensor", svec[:, 15:16], svec[:, 14:15], 1.0,
                              svec[:, 14:15], Alu.mult, Alu.mult), inc="vV",
                 waits=[("sC", es_s)] if first else ())
            first = False
            b.op("vector", mk("tensor_mul", svec[:, 15:16], svec[:, 15:16], svec[:, 12:13]), inc="vV")
            b.op("vector", mk("tensor_scalar", svec[:, 15:16], svec[:, 15:16], -0.5, 1.5,
                              Alu.mult, Alu.add), inc="vV")
            b.op("vector", mk("tensor_mul", svec[:, 14:15], svec[:, 14:15], svec[:, 15:16]), inc="vV")
        ev_t2 = b.op("vector", mk("tensor_scalar", t2scr, hbuf, svec[:, 10:11], svec[:, 14:15],
                                  Alu.subtract, Alu.mult), inc="vV")
        t2b = AR4[:, 3072:4096]
        b.op("gpsimd", mk("tensor_mul", t2b, t2scr, gam_lo[:, :]), inc="gP",
             waits=[("vV", ev_t2)])
        eg_t3 = b.op("gpsimd", mk("tensor_mul", t2scr, t2scr, gam_b[:, :]), inc="gP")
        b.op("vector", mk("tensor_add", hbuf, t2scr, t2b), inc="vV",
             waits=[("gP", eg_t3), ("gP", eg_x4)])
        ev_x1 = b.op("vector", mk("tensor_add", hbuf, hbuf, xt), inc="vV")
        b.dma("gpsimd", "qGX1", mk("dma_start", out=s_x1[qt * 128:(qt + 1) * 128, :], in_=hbuf),
              waits=[("vV", ev_x1)])
        prev_ep[qt] = {"x_done": ev_x1}
        rt_emit(4)

    rt_emit(64)
    e_rt_wgwu = [("qTS0", rt_state["st"][30]), ("qTS1", rt_state["st"][31])]
    e_rt_all = [("qTS0", rt_state["st"][62]), ("qTS1", rt_state["st"][63])]
    e_p3_tensor = b.cnt["tM"]
    e_p3_v = b.cnt["vV"]
    e_p3_g = b.cnt["gP"]
    e_p3_x1st = b.cnt["qGX1"]

    ln2s = wstg[:, 0:1024]
    ybuf1 = wstg[:, 1024:2048]
    ybuf = AR3[:, 0:1024].bitcast(F16)
    r0b = qi[:, :]
    g1b = AR3[:, 1024:2048].bitcast(F16)
    g2b = AR3[:, 2048:3072].bitcast(F16)
    g3b = AR3[:, 3072:4096].bitcast(F16)
    q2sp = AR3[:, 4096:5120].bitcast(F16)
    q2sn = AR3[:, 5120:6144].bitcast(F16)
    xqo = AR3[:, 8192:9216]
    xq1T = [AR2[:, 0:4096], AR2[:, 4096:8192]]
    wgb = [AR2[:, 8192:12288], AR2[:, 12288:16384]]
    wub = [AR2[:, 16384:20480], AR2[:, 20480:24576]]
    wdb = [AR2[:, 24576:25600].bitcast(F16), AR2[:, 25600:26624].bitcast(F16),
           AR2[:, 28672:29696].bitcast(F16), AR2[:, 29696:30720].bitcast(F16)]
    wd_sem = ["qWD0", "qWD1", "qXT0", "qXT1"]
    sgs_s = [AR2[:, 26624:27136].bitcast(F16), AR2[:, 27648:28160].bitcast(F16)]
    uss_s = [AR2[:, 27136:27648].bitcast(F16), AR2[:, 28160:28672].bitcast(F16)]
    x1t = [AR4[:, 0:1024], AR4[:, 1024:2048]]
    x2t = [AR4[:, 2048:3072], AR4[:, 3072:4096]]

    e_q1 = {}
    for tt in range(16):
        slot = tt % 2
        xi = x1t[slot]
        w0 = [("qGX1", 16 * (tt + 1))]
        if tt < 2:
            w0.append(("qGX1", e_p3_x1st))
        if tt == 0:
            w0 += [("vV", e_p3_v), ("gP", e_p3_g), ("tM", e_p2_tensor),
                   ("vV", p0c_state["signB"][95])]
        if tt >= 2:
            w0.append(("vV", e_q1[tt - 2]))
        ld = None
        for q2 in range(2):
            ld = b.dma("sync", f"qX1{slot}",
                       mk("dma_start", out=xi[:, q2 * 512:(q2 + 1) * 512],
                          in_=s_x1[tt * 128:(tt + 1) * 128, q2 * 512:(q2 + 1) * 512]),
                       waits=w0 if q2 == 0 else ())
        sw = [(f"qX1{slot}", ld)]
        b.op("vector", mk("reduce_sum", svec[:, 0:1], xi, AX), inc="vV", waits=sw)
        b.op("vector", mk("scalar_tensor_tensor", ln2s, xi, 1.0, xi, Alu.mult, Alu.mult,
                          accum_out=svec[:, 1:2]), inc="vV")
        b.op("vector", mk("tensor_single_scalar", svec[:, 2:3], svec[:, 0:1], 1.0 / D, Alu.mult), inc="vV")
        b.op("vector", mk("tensor_single_scalar", svec[:, 3:4], svec[:, 1:2], 1.0 / D, Alu.mult), inc="vV")
        b.op("vector", mk("scalar_tensor_tensor", svec[:, 4:5], svec[:, 2:3], 1.0, svec[:, 2:3],
                          Alu.mult, Alu.mult), inc="vV")
        b.op("vector", mk("tensor_sub", svec[:, 4:5], svec[:, 3:4], svec[:, 4:5]), inc="vV")
        b.op("vector", mk("tensor_single_scalar", svec[:, 4:5], svec[:, 4:5], LN_EPS, Alu.add), inc="vV")
        ev_r = b.op("vector", mk("reciprocal", svec[:, 5:6], svec[:, 4:5]), inc="vV")
        es_s = b.op("scalar", mk("activation", svec[:, 6:7], svec[:, 5:6], Act.Sqrt),
                    inc="sC", waits=[("vV", ev_r)])
        first = True
        for _ in range(2):
            b.op("vector", mk("scalar_tensor_tensor", svec[:, 7:8], svec[:, 6:7], 1.0,
                              svec[:, 6:7], Alu.mult, Alu.mult), inc="vV",
                 waits=[("sC", es_s)] if first else ())
            first = False
            b.op("vector", mk("tensor_mul", svec[:, 7:8], svec[:, 7:8], svec[:, 4:5]), inc="vV")
            b.op("vector", mk("tensor_scalar", svec[:, 7:8], svec[:, 7:8], -0.5, 1.5,
                              Alu.mult, Alu.add), inc="vV")
            b.op("vector", mk("tensor_mul", svec[:, 6:7], svec[:, 6:7], svec[:, 7:8]), inc="vV")
        b.op("vector", mk("tensor_scalar", ln2s, xi, svec[:, 2:3], svec[:, 6:7],
                          Alu.subtract, Alu.mult), inc="vV")
        b.op("vector", mk("reduce_max", svec[:, 8:9], ln2s, AX, True), inc="vV")
        b.op("vector", mk("tensor_scalar_max", svec[:, 8:9], svec[:, 8:9], 1e-5), inc="vV")
        b.op("vector", mk("reciprocal", svec[:, 9:10], svec[:, 8:9]), inc="vV")
        b.op("vector", mk("tensor_single_scalar", s1c[:, tt:tt + 1], svec[:, 9:10], 127.0, Alu.mult), inc="vV")
        b.op("vector", mk("tensor_single_scalar", svec[:, 10:11], svec[:, 8:9], 1.0 / 127.0, Alu.mult), inc="vV")
        b.op("vector", mk("tensor_mul", a1g[:, tt:tt + 1], svec[:, 10:11], tcols[:, 3:4]), inc="vV",
             waits=[("vV", scale_ready[0]), ("vV", scale_ready[1])])
        ev_sc = b.op("vector", mk("tensor_mul", a1u[:, tt:tt + 1], svec[:, 10:11], tcols[:, 4:5]), inc="vV")
        es_y = b.op("scalar", mk("activation", ybuf1, ln2s, Act.Copy, scale=s1c[:, tt:tt + 1]),
                    inc="sC", waits=[("vV", ev_sc)])
        if KDBG and tt == 0:
            ed1 = b.dma("gpsimd", "qMB", mk("dma_start", out=s_dbg[:, 0:1024], in_=ybuf1),
                        waits=[("sC", es_y)])
            ed2 = b.dma("gpsimd", "qMB", mk("dma_start", out=s_dbg[:, 1024:2048], in_=ln2s))
            b.wait("vector", "qMB", ed2)
        e_r0 = b.op("vector", mk("tensor_copy", r0b, ybuf1), inc="vV", waits=[("sC", es_y)])
        e_g1a = b.op("vector", mk("scalar_tensor_tensor", g1b, ybuf1, -0.5, r0b, Alu.add, Alu.is_gt), inc="vV")
        e_g2a = b.op("vector", mk("scalar_tensor_tensor", g2b, ybuf1, 0.5, r0b, Alu.add, Alu.is_ge), inc="vV")
        if KDBG and tt == 0:
            ed3 = b.dma("gpsimd", "qMB", mk("dma_start", out=s_dbg[:, 2048:3072].bitcast(I32), in_=r0b),
                        waits=[("vV", e_g2a)])
            ed4 = b.dma("gpsimd", "qMB", mk("dma_start", out=s_dbg[:, 3072:4096], in_=g1b))
            ed5 = b.dma("gpsimd", "qMB", mk("dma_start", out=s_dbg[:, 4096:5120], in_=g2b))
            b.wait("vector", "qMB", ed5)
        b.op("vector", mk("scalar_tensor_tensor", g1b, g1b, -1.0, g2b, Alu.add, Alu.add), inc="vV")
        qw2 = [("qGQ1", 16 * tt)] if tt >= 1 else list(e_rt_all)
        ev_q = b.op("vector", mk("scalar_tensor_tensor", xqo, g1b, 1.0, r0b, Alu.mult, Alu.add),
                    inc="vV", waits=qw2)
        e_q1[tt] = ev_q
        b.dma("gpsimd", "qGQ1", mk("dma_start", out=s_xq1[tt * 128:(tt + 1) * 128, :], in_=xqo),
              waits=[("vV", ev_q)])
    e_q1_last = e_q1[15]

    AR1h = [AR1[:, 0:16384], AR1[:, 16384:32768]]

    def prod_ap(k, tl, dc):
        return AR1h[k % 2][:, tl * 4096 + dc * 512: tl * 4096 + (dc + 1) * 512].bitcast(F16)

    def prod_tt(k, tl):
        return AR1h[k % 2][:, tl * 4096:(tl + 1) * 4096].bitcast(F16)

    def xq2T_ap(k, ic):
        return AR1h[k % 2][:, ic * 512:(ic + 1) * 512].bitcast(F16)

    e_xq1T = {}
    e_gu_done = {}
    e_q2_done = {}
    e_xq2T = {}
    e_down_mm = {}
    e_out_store = {}
    e_prev_wchunk = {}
    e_prev_prod = [None]
    e_prod_hist = {0: None, 1: None}
    e_prev_x2 = {}
    e_wd_prev = {}

    def emit_xq1T(k):
        sl = k % 2
        w = [("qGQ1", 16 * (4 * k + 4))]
        if k >= 2:
            w.append(("tM", e_gu_done[k - 2]))
        if k == 0:
            w.append(("tM", e_p3_tensor))
            w += e_rt_all
        last = None
        for ds in range(8):
            last = b.dma("sync", f"qQ1T{sl}",
                         mk("dma_start", out=xq1T[sl][:, ds * 512:(ds + 1) * 512],
                            in_=s_xq1[k * 512:(k + 1) * 512, ds * 128:(ds + 1) * 128],
                            transpose=True), waits=w if ds == 0 else ())
        e_xq1T[k] = last

    def emit_gate_up(k):
        sl = k % 2
        last_mm = None
        e_am_init = {}
        for tl in range(4):
            aw = []
            if k >= 2:
                aw.append(("vV", e_q2_done[k - 2]))
            e_am_init[tl] = b.op("vector",
                                 mk("memset", am2[:, 4 * (k % 2) + tl: 4 * (k % 2) + tl + 1], 0.0),
                                 inc="vV", waits=aw)
        for dc in range(8):
            wsl = dc % 2
            wload_w = []
            if k == 0 and dc <= 1:
                wload_w += e_rt_wgwu
            if k == 0 and dc == 0:
                wload_w.append(("tM", e_p3_tensor))
            prev_key = (k, dc - 2) if dc >= 2 else ((k - 1, 6 + dc) if k > 0 else None)
            if prev_key is not None:
                wload_w.append(("tM", e_prev_wchunk.get(prev_key)))
            lg = None
            for ds in range(8):
                lg = b.dma("sync", f"qWG{wsl}",
                           mk("dma_start", out=wgb[wsl][:, ds * 512:(ds + 1) * 512],
                              in_=s_wg2t[ds * 128:(ds + 1) * 128, dc * 512:(dc + 1) * 512]),
                           waits=wload_w if ds == 0 else ())
            lu = None
            for ds in range(8):
                lu = b.dma("sync", f"qWU{wsl}",
                           mk("dma_start", out=wub[wsl][:, ds * 512:(ds + 1) * 512],
                              in_=s_wu2t[ds * 128:(ds + 1) * 128, dc * 512:(dc + 1) * 512]),
                           waits=wload_w if ds == 0 else ())
            for tl in range(4):
                tt = 4 * k + tl
                bg = (2 * tl) % 4
                bu = (2 * tl + 1) % 4
                w0 = [(f"qWG{wsl}", lg), (f"qQ1T{sl}", e_xq1T[k])] + bw(bg)
                emg = None
                for ds in range(8):
                    lhs = xq1T[sl][:, ds * 512 + tl * 128: ds * 512 + (tl + 1) * 128]
                    rhs = wgb[wsl][:, ds * 512:(ds + 1) * 512]
                    emg = b.op("tensor", mk("matmul", pb[bg][:, :], lhs, rhs,
                                            start=(ds == 0), stop=(ds == 7)),
                               inc="tM" if ds == 7 else None, waits=w0 if ds == 0 else ())
                p2 = (dc * 4 + tl) % 2
                sgw = [("tM", emg)]
                if e_prod_hist[p2] is not None:
                    sgw.append(("vV", e_prod_hist[p2]))
                esg = b.op("scalar", mk("activation", sgs_s[p2], pb[bg][:, :], Act.Silu,
                                        scale=a1g[:, tt:tt + 1]), inc="sC", waits=sgw)
                bank_gate[bg] = ("sC", esg)
                w1 = [(f"qWU{wsl}", lu)] + bw(bu)
                emu = None
                for ds in range(8):
                    lhs = xq1T[sl][:, ds * 512 + tl * 128: ds * 512 + (tl + 1) * 128]
                    rhs = wub[wsl][:, ds * 512:(ds + 1) * 512]
                    emu = b.op("tensor", mk("matmul", pb[bu][:, :], lhs, rhs,
                                            start=(ds == 0), stop=(ds == 7)),
                               inc="tM" if ds == 7 else None, waits=w1 if ds == 0 else ())
                last_mm = emu
                esu = b.op("scalar", mk("activation", uss_s[p2], pb[bu][:, :], Act.Copy,
                                        scale=a1u[:, tt:tt + 1]), inc="sC", waits=[("tM", emu)])
                bank_gate[bu] = ("sC", esu)
                pw = [("sC", esu)]
                if k >= 2 and dc == 0 and tl == 0:
                    pw.append(("qGX2", 16 * 4 * (k - 1)))
                ev_p = b.op("vector", mk("tensor_mul", prod_ap(k, tl, dc), sgs_s[p2], uss_s[p2]),
                            inc="vV", waits=pw)
                e_prod_hist[p2] = ev_p
                b.op("vector", mk("reduce_max", svec[:, 26:27], prod_ap(k, tl, dc), AX, True),
                     inc="vV", waits=[("vV", e_am_init[tl])])
                b.op("vector", mk("tensor_max", am2[:, 4 * (k % 2) + tl: 4 * (k % 2) + tl + 1],
                                  am2[:, 4 * (k % 2) + tl: 4 * (k % 2) + tl + 1], svec[:, 26:27]),
                     inc="vV")
            e_prev_wchunk[(k, dc)] = last_mm
        e_gu_done[k] = last_mm

    q2_st_prev = [None]

    def emit_quant2(k):
        last_v = None
        for tl in range(4):
            tt = 4 * k + tl
            amc = am2[:, 4 * (k % 2) + tl: 4 * (k % 2) + tl + 1]
            b.op("vector", mk("tensor_scalar_max", svec[:, 16:17], amc, 1e-5), inc="vV")
            b.op("vector", mk("reciprocal", svec[:, 17:18], svec[:, 16:17]), inc="vV")
            b.op("vector", mk("tensor_single_scalar", s2c[:, tt:tt + 1], svec[:, 17:18], 127.0,
                              Alu.mult), inc="vV")
            b.op("vector", mk("tensor_single_scalar", svec[:, 18:19], svec[:, 16:17], 1.0 / 127.0,
                              Alu.mult), inc="vV")
            ev_a2 = b.op("vector", mk("tensor_mul", a2d[:, tt:tt + 1], svec[:, 18:19], tcols[:, 5:6]),
                         inc="vV", waits=[("vV", scale_ready[2])])
            prev_g2 = None
            for sub in range(4):
                srcp = prod_tt(k, tl)[:, sub * 1024:(sub + 1) * 1024]
                yw = [("vV", ev_a2)]
                if prev_g2 is not None:
                    yw.append(("vV", prev_g2))
                es_y = b.op("scalar", mk("activation", ybuf, srcp, Act.Copy,
                                         scale=s2c[:, tt:tt + 1]), inc="sC", waits=yw)
                if KDBG:
                    edy = b.dma("gpsimd", "qMB",
                                mk("dma_start",
                                   out=s_dby[(4 * k + tl) * 128:(4 * k + tl + 1) * 128,
                                             sub * 1024:(sub + 1) * 1024],
                                   in_=ybuf), waits=[("sC", es_y)])
                    b.wait("scalar", "qMB", edy)
                    b.wait("vector", "qMB", edy)
                b.op("vector", mk("tensor_copy", r0b, ybuf), inc="vV", waits=[("sC", es_y)])
                b.op("vector", mk("tensor_copy", g3b, r0b), inc="vV")
                prev_g2 = b.op("vector", mk("tensor_sub", g1b, ybuf, g3b), inc="vV")
                b.op("vector", mk("tensor_single_scalar", g2b, g1b, 0.5, Alu.is_gt), inc="vV")
                b.op("vector", mk("tensor_single_scalar", q2sp, g1b, 1.5, Alu.is_gt), inc="vV")
                b.op("vector", mk("tensor_add", g2b, g2b, q2sp), inc="vV")
                b.op("vector", mk("tensor_single_scalar", q2sp, g1b, -0.5, Alu.is_ge), inc="vV")
                b.op("vector", mk("tensor_single_scalar", q2sn, g1b, -1.5, Alu.is_ge), inc="vV")
                b.op("vector", mk("tensor_add", q2sp, q2sp, q2sn), inc="vV")
                b.op("vector", mk("tensor_add", g2b, g2b, q2sp), inc="vV")
                b.op("vector", mk("tensor_single_scalar", g2b, g2b, -2.0, Alu.add), inc="vV")
                last_v = b.op("vector", mk("tensor_add", srcp, g2b, g3b), inc="vV")
            sw_ = [("vV", last_v)]
            if q2_st_prev[0] is not None:
                sw_.append(("qGX2", q2_st_prev[0]))
            q2_st_prev[0] = b.dma("gpsimd", "qGX2",
                                  mk("dma_start", out=s_xq2[tt * 128:(tt + 1) * 128, :],
                                     in_=prod_tt(k, tl)),
                                  waits=sw_)
        e_q2_done[k] = last_v

    xqTb = [AR3[:, 6144 + s * 512:6656 + s * 512].bitcast(F16) for s in range(4)]
    xqT_sem = ["qQ2T0", "qQ2T1", "qQT0", "qQT1"]

    def emit_down(k):
        e_wd = {}
        for ic in range(32):
            wsl = ic % 4
            ts_ = ic % 4
            tw = []
            if ic == 0:
                tw.append(("qGX2", 16 * 4 * (k + 1)))
            if ic >= 4:
                tw.append(("tM", e_wd[ic - 4]))
            elif k > 0:
                tw.append(("tM", e_wd_prev.get(28 + ic)))
            ld_t = b.dma("sync", xqT_sem[ts_],
                         mk("dma_start", out=xqTb[ts_],
                            in_=s_xq2[k * 512:(k + 1) * 512, ic * 128:(ic + 1) * 128],
                            transpose=True), waits=tw)
            w = []
            if k == 0 and ic <= 1:
                w += e_rt_all
            if ic >= 4:
                w.append(("tM", e_wd[ic - 4]))
            elif k > 0:
                w.append(("tM", e_wd_prev.get(28 + ic)))
            ld = b.dma("sync", wd_sem[wsl],
                       mk("dma_start", out=wdb[wsl],
                          in_=s_wd2t[ic * 128:(ic + 1) * 128, :]),
                       waits=w)
            em = None
            for tl in range(4):
                for eh in range(2):
                    bank = tl * 2 + eh
                    w0 = [(wd_sem[wsl], ld), (xqT_sem[ts_], ld_t)] + (bw(bank) if ic == 0 else [])
                    em = b.op("tensor",
                              mk("matmul", pb[bank][:, :],
                                 xqTb[ts_][:, tl * 128:(tl + 1) * 128],
                                 wdb[wsl][:, eh * 512:(eh + 1) * 512],
                                 start=(ic == 0), stop=(ic == 31)),
                              inc="tM" if (tl == 3 and eh == 1) else None,
                              waits=w0)
            e_wd[ic] = em
        e_down_mm[k] = e_wd[31]
        e_wd_prev.clear()
        e_wd_prev.update(e_wd)
        for tl in range(4):
            tt = 4 * k + tl
            slot = tt % 2
            xi = x1t[slot]
            lw = [("vV", e_q1_last)]
            if tt >= 2:
                lw.append(("vV", e_prev_x2.get(tt - 2)))
            ld = None
            for q2 in range(2):
                ld = b.dma("sync", f"qX1{slot}",
                           mk("dma_start", out=xi[:, q2 * 512:(q2 + 1) * 512],
                              in_=s_x1[tt * 128:(tt + 1) * 128, q2 * 512:(q2 + 1) * 512]),
                           waits=lw if q2 == 0 else ())
            ev = None
            for eh in range(2):
                bank = tl * 2 + eh
                xw = [("tM", e_down_mm[k]), (f"qX1{slot}", ld)]
                if tt < 2:
                    xw += [("gP", e_p3_g), ("vV", e_p3_v)]
                if tt >= 2:
                    xw.append((f"qGO{slot}", e_out_store.get(tt - 2)))
                ev = b.op("vector", mk("scalar_tensor_tensor", x2t[slot][:, eh * 512:(eh + 1) * 512],
                                       pb[bank][:, :], a2d[:, tt:tt + 1],
                                       xi[:, eh * 512:(eh + 1) * 512], Alu.mult, Alu.add),
                          inc="vV", waits=xw)
                bank_gate[bank] = ("vV", ev)
            e_prev_x2[tt] = ev
            st = b.dma("gpsimd", f"qGO{slot}",
                       mk("dma_start", out=out_hbm[tt * 128:(tt + 1) * 128, :], in_=x2t[slot]),
                       waits=[("vV", ev)])
            e_out_store[tt] = st

    emit_xq1T(0)
    emit_gate_up(0)
    emit_quant2(0)
    emit_xq1T(1)
    emit_gate_up(1)
    emit_quant2(1)
    emit_down(0)
    emit_xq1T(2)
    emit_gate_up(2)
    emit_quant2(2)
    emit_down(1)
    emit_xq1T(3)
    emit_gate_up(3)
    emit_quant2(3)
    emit_down(2)
    emit_down(3)

    b.wait("scalar", "qGO0", b.cnt["qGO0"])
    b.wait("scalar", "qGO1", b.cnt["qGO1"])
    b.wait("sync", "qGO0", b.cnt["qGO0"])
    b.wait("sync", "qGO1", b.cnt["qGO1"])

    for name in SEMS:
        b.sems[name] = stack.enter_context(nc.semaphore(name))
    with nc.Block() as block:
        @block.sync
        def _(E):
            for th in b.ops["sync"]:
                th(E)

        @block.tensor
        def _(E):
            for th in b.ops["tensor"]:
                th(E)

        @block.vector
        def _(E):
            for th in b.ops["vector"]:
                th(E)

        @block.scalar
        def _(E):
            for th in b.ops["scalar"]:
                th(E)

        @block.gpsimd
        def _(E):
            for th in b.ops["gpsimd"]:
                th(E)

    stack.close()
    return nc


_NC_CACHE = None


def kernel(L0_attn_x, L0_attn_w_qkv, L0_attn_w_fc, L0_attn_b_fc, L0_attn_gamma,
           L0_attn_beta, L0_ffn_w_gate, L0_ffn_w_up, L0_ffn_w_down):
    global _NC_CACHE
    x = np.ascontiguousarray(np.asarray(L0_attn_x, dtype=np.float32))
    B_, S_, D_ = x.shape
    weights = {
        "w_qkv": np.ascontiguousarray(np.asarray(L0_attn_w_qkv, dtype=np.float32)),
        "w_fc": np.ascontiguousarray(np.asarray(L0_attn_w_fc, dtype=np.float32)),
        "b_fc": np.ascontiguousarray(np.asarray(L0_attn_b_fc, dtype=np.float32)).reshape(1, D),
        "gamma": np.ascontiguousarray(np.asarray(L0_attn_gamma, dtype=np.float32)).reshape(1, D),
        "beta": np.ascontiguousarray(np.asarray(L0_attn_beta, dtype=np.float32)).reshape(1, D),
        "w_gate": np.ascontiguousarray(np.asarray(L0_ffn_w_gate, dtype=np.float32)),
        "w_up": np.ascontiguousarray(np.asarray(L0_ffn_w_up, dtype=np.float32)),
        "w_down": np.ascontiguousarray(np.asarray(L0_ffn_w_down, dtype=np.float32)),
    }
    in_maps = []
    for c in range(8):
        bi, h = divmod(c, 2)
        own = x[bi, h * T:(h + 1) * T, :]
        oth = x[bi, (1 - h) * T:(2 - h) * T, :]
        xc = np.ascontiguousarray(np.concatenate([own, oth], axis=0))
        m = {"x_full": xc}
        m.update(weights)
        in_maps.append(m)

    if _NC_CACHE is None:
        nc_ = bass.Bass("TRN2", target_bir_lowering=False)
        build_program(nc_)
        _NC_CACHE = nc_
    res = run_bass_kernel_spmd(_NC_CACHE, in_maps, core_ids=list(range(8)))
    out = np.empty_like(x)
    for c in range(8):
        bi, h = divmod(c, 2)
        out[bi, h * T:(h + 1) * T, :] = res.results[c]["out"]
    return out
